# version 24
# baseline (speedup 1.0000x reference)
from collections import defaultdict
from contextlib import ExitStack

import numpy as np

import concourse.bass as bass
import concourse.mybir as mybir
from concourse.alu_op_type import AluOpType
from concourse.bass_utils import run_bass_kernel_spmd

F32 = mybir.dt.float32
BF16 = mybir.dt.bfloat16
AF = mybir.ActivationFunctionType
AX = mybir.AxisListType
OP = AluOpType

D = 1024
H = 16
DH = 64
DC = 256
HID = 2048
NEXP = 8
EPS_RMS = 1e-6
EPS_LN = 1e-5
NEGBIG = -1e30

ENGINES = ("sync", "tensor", "vector", "scalar", "gpsimd")


class Prog:

    def __init__(self, nc):
        self.nc = nc
        self.ops = {e: [] for e in ENGINES}
        self.sem = {}
        self.cnt = defaultdict(int)
        self.waited = {e: defaultdict(int) for e in ENGINES}

    def make_sems(self, es):
        self._es = es
        for name in ("bar_sync", "bar_tensor", "bar_vector", "bar_scalar",
                     "bar_gpsimd"):
            self.sem[name] = es.enter_context(self.nc.semaphore(name))
        self.dma_sems = {}

    def dma_sem(self, name, q):
        if name not in self.sem:
            self.sem[name] = self._es.enter_context(self.nc.semaphore(name))
            self.dma_sems[name] = q
        assert self.dma_sems[name] == q, f"dma sem {name} used from two queues"
        return name

    def emit(self, eng, fn, inc=None, inc_amt=1):
        if inc is None:
            self.ops[eng].append(fn)
            return None
        self.cnt[inc] += inc_amt
        val = self.cnt[inc]

        def run(e, fn=fn, s=inc, a=inc_amt):
            fn(e).then_inc(self.sem[s], a)

        self.ops[eng].append(run)
        return (inc, val)

    def wait_max(self, eng, tokens):
        best = {}
        for t in tokens:
            if t is None:
                continue
            s, v = t
            best[s] = max(best.get(s, 0), v)
        for s, v in best.items():
            self.wait(eng, (s, v))

    def wait(self, eng, token):
        if token is None:
            return
        s, v = token
        if self.waited[eng][s] >= v:
            return
        self.waited[eng][s] = v
        self.ops[eng].append(lambda e, s=s, v=v: e.wait_ge(self.sem[s], v))

    def drain(self, eng):
        self.ops[eng].append(lambda e: e.drain())

    def dma(self, q, out, in_, stream=None):
        name = self.dma_sem("dq_" + (stream or q), q)
        return self.emit(q, lambda e, o=out, i=in_: e.dma_start(out=o, in_=i),
                         inc=name, inc_amt=16)

    def tick(self, eng, fn):
        return self.emit(eng, fn, inc="bar_" + eng)

    def mark(self, eng):
        self.drain(eng)
        return self.tick(eng, lambda e: e.nop())

    BAR_SKIP = ("dq_wsa", "dq_wsb", "dq_w2s", "dq_wso", "dq_wq", "dq_fmld",
                "dq_simr", "dq_wsl", "dq_wc_")

    def barrier(self, note="", light=True):
        for name, q in self.dma_sems.items():
            if light and name.startswith(self.BAR_SKIP):
                continue
            self.wait(q, (name, self.cnt[name]))
        toks = {}
        for e in ENGINES:
            self.drain(e)
            toks[e] = self.emit(e, lambda x: x.nop(), inc="bar_" + e)
        for e in ENGINES:
            for o in ENGINES:
                if o != e:
                    self.wait(e, toks[o])

    def finalize(self, block):
        for e in ENGINES:
            lst = self.ops[e]
            body = getattr(block, e)

            def make(lst):
                def _(eng):
                    for fn in lst:
                        fn(eng)
                return _
            body(make(lst))


def ts(i, w):
    return slice(i * w, (i + 1) * w)


def build_program(T=2048, stages=3):
    NT = T // 128
    QW = min(512, T)
    NQT = T // QW
    NKT = T // 128
    NQ = max(1, T // 512)
    QTOK = T // NQ
    TPQ = QTOK // 128
    NHP = H // 2

    nc = bass.Bass("TRN2", target_bir_lowering=False)

    x_in = nc.dram_tensor("x_in", [T, D], F32, kind="ExternalInput").ap()
    ema_in = nc.dram_tensor("ema_in", [1, D], F32, kind="ExternalInput").ap()
    scal_in = nc.dram_tensor("scal_in", [1, 4], F32, kind="ExternalInput").ap()
    wq_in = nc.dram_tensor("wq_in", [D, D], F32, kind="ExternalInput").ap()
    wkc_in = nc.dram_tensor("wkc_in", [D, DC], F32, kind="ExternalInput").ap()
    wvc_in = nc.dram_tensor("wvc_in", [D, DC], F32, kind="ExternalInput").ap()
    wk_in = nc.dram_tensor("wk_in", [DC, D], F32, kind="ExternalInput").ap()
    wv_in = nc.dram_tensor("wv_in", [DC, D], F32, kind="ExternalInput").ap()
    wo_in = nc.dram_tensor("wo_in", [D, D], F32, kind="ExternalInput").ap()
    wsi_in = nc.dram_tensor("wsi_in", [D, 2 * HID], F32, kind="ExternalInput").ap()
    wso_in = nc.dram_tensor("wso_in", [HID, D], F32, kind="ExternalInput").ap()
    w1_in = nc.dram_tensor("w1_in", [D, 2 * HID], F32, kind="ExternalInput").ap()
    w2_in = nc.dram_tensor("w2_in", [NEXP, HID, D], F32, kind="ExternalInput").ap()
    wgate_in = nc.dram_tensor("wgate_in", [D, 16], F32, kind="ExternalInput").ap()
    gbias_in = nc.dram_tensor("gbias_in", [1, 16], F32, kind="ExternalInput").ap()
    n1w_in = nc.dram_tensor("n1w_in", [1, D], F32, kind="ExternalInput").ap()
    n2w_in = nc.dram_tensor("n2w_in", [1, D], F32, kind="ExternalInput").ap()
    iden_in = nc.dram_tensor("iden_in", [128, 128], F32, kind="ExternalInput").ap()
    trig_in = nc.dram_tensor("trig_in", [128, 2 * NT * 32], F32,
                             kind="ExternalInput").ap()
    mask_in = nc.dram_tensor("mask_in", [128, 4 * QW], F32, kind="ExternalInput").ap()
    out_t = nc.dram_tensor("out_t", [T, D], F32, kind="ExternalOutput").ap()

    sp_x = nc.dram_tensor("sp_x", [T, D], F32).ap()
    sp_y = nc.dram_tensor("sp_y", [T, D], F32).ap()
    sp_flat = nc.dram_tensor("sp_flat", [T, D], F32).ap()
    sp_g = nc.dram_tensor("sp_g", [T, D], F32).ap()
    sp_q = {s: nc.dram_tensor(f"sp_q_{s}", [NHP, 128, T], BF16).ap() for s in "hl"}
    sp_k = {s: nc.dram_tensor(f"sp_k_{s}", [NHP, 128, T], BF16).ap() for s in "hl"}
    sp_sa = {s: nc.dram_tensor(f"sp_sa_{s}", [NHP, 128, T], BF16).ap() for s in "hl"}
    sp_v = nc.dram_tensor("sp_v", [NHP, T, 130], F32).ap()
    sp_fm = {s: nc.dram_tensor(f"sp_fm_{s}", [8, 128, T], BF16).ap() for s in "hl"}
    bounce = nc.dram_tensor("bounce", [1, 4096], F32).ap()
    wc = {}
    for nm, w in (("wq", 8 * D), ("wkc", 2 * D), ("wvc", 2 * D), ("wk", 2 * D),
                  ("wv", 2 * D), ("wo", 8 * D), ("wgate", 128)):
        wc[nm] = {s: nc.dram_tensor(f"wc_{nm}_{s}", [128, w], BF16).ap() for s in "hl"}
    wcm = {"wq": nc.dram_tensor("wcm_wq", [8, 128, 2048], BF16).ap(),
           "wsi": nc.dram_tensor("wcm_wsi", [32, 128, 2048], BF16).ap(),
           "w1": nc.dram_tensor("wcm_w1", [32, 128, 2048], BF16).ap(),
           "wso": nc.dram_tensor("wcm_wso", [16, 128, 2048], BF16).ap(),
           "w2": nc.dram_tensor("wcm_w2", [NEXP * 16, 128, 2048], BF16).ap()}

    es = ExitStack()
    p = Prog(nc)
    p.make_sems(es)

    ACCQ = 0
    TA = ACCQ + 4096
    TB = TA + 1024
    TC = TB + 1024
    SS = TC + 1024
    TRIG = SS + 8448
    MASK = TRIG + 2 * NT * 32
    BCN = MASK + 4 * QW
    MISC = BCN + 3072
    AF32W = MISC + 1792
    af = es.enter_context(nc.sbuf_tensor("af", [128, AF32W], F32))

    BFM = 0
    BSH1 = 16384
    BSH2 = 32768
    BST = 49152
    ABFW = BST + 8192
    ab = es.enter_context(nc.sbuf_tensor("ab", [128, ABFW], BF16))

    ps = [es.enter_context(nc.psum_tensor(f"ps{i}", [128, 512], F32))
          for i in range(8)]

    def fap(off, pattern):
        return bass.AP(af, off, [[AF32W, 128]] + pattern)

    def bap(off, pattern, npart=128):
        return bass.AP(ab, off, [[ABFW, npart]] + pattern)

    def tA(n=1024):
        return fap(TA, [[1, n]])

    def tB(n=1024):
        return fap(TB, [[1, n]])

    def tC(n=1024):
        return fap(TC, [[1, n]])

    def accq(ti, n=1024, off=0):
        return fap(ACCQ + ti * 1024 + off, [[1, n]])

    trig_ap = fap(TRIG, [[1, 2 * NT * 32]])
    mask_full = fap(MASK, [[1, 4 * QW]])

    def mask_ap(mi):
        return fap(MASK + mi * QW, [[1, QW]])

    n1w_bc = fap(BCN, [[1, D]])
    n2w_bc = fap(BCN + 1024, [[1, D]])
    ema_bc = fap(BCN + 2048, [[1, D]])
    COLV = MISC
    WGT = MISC + 32
    SEQS = MISC + 160
    EMAS = MISC + 176
    INVN = MISC + 192
    SCBC = MISC + 208
    GBBC = MISC + 216
    ROWS = MISC + 232

    def colv(a, b):
        return fap(COLV + a, [[1, b - a]])

    def row(off, n, parts=1):
        return bass.AP(af, ROWS + off, [[AF32W, parts], [1, n]])

    CPOOL = ROWS + 1200
    _cvals = [0.0, float(EPS_LN), float(EPS_RMS)]
    for _i, _v in enumerate(_cvals):
        _ap = fap(CPOOL + _i, [[1, 1]])
        p.emit("vector", lambda e, a=_ap, v=_v: e.memset(a, v))
        nc.const_aps.register(F32, _v, _ap)
    p.drain("vector")
    _tcp = p.mark("vector")
    for _e in ("scalar", "tensor", "gpsimd", "sync"):
        p.wait(_e, _tcp)

    def split_pair(eng, dst_h, dst_l, src_ap, hi_scalar=False):
        if hi_scalar:
            th = p.tick("scalar", lambda e, a=dst_h, b=src_ap:
                        e.activation(out=a, in_=b, func=AF.Copy))
            p.wait(eng, th)
            return p.tick(eng, lambda e, a=dst_l, b=src_ap, c=dst_h:
                          e.tensor_tensor(out=a, in0=b, in1=c, op=OP.subtract))
        p.tick(eng, lambda e, a=dst_h, b=src_ap: e.tensor_copy(out=a, in_=b))
        p.drain(eng)
        return p.tick(eng, lambda e, a=dst_l, b=src_ap, c=dst_h:
                      e.tensor_tensor(out=a, in0=b, in1=c, op=OP.subtract))

    def mm3(psum_ap, lh, ll, rh, rl, start, stop, tile_position=None, inc=False):
        p.emit("tensor", lambda e, o=psum_ap, a=lh, b=rh, tp=tile_position:
               e.matmul(out=o, lhsT=a, rhs=b, start=start, stop=False,
                        tile_position=tp))
        p.emit("tensor", lambda e, o=psum_ap, a=lh, b=rl, tp=tile_position:
               e.matmul(out=o, lhsT=a, rhs=b, start=False, stop=False,
                        tile_position=tp))
        return p.emit("tensor", lambda e, o=psum_ap, a=ll, b=rh, tp=tile_position:
                      e.matmul(out=o, lhsT=a, rhs=b, start=False, stop=stop,
                               tile_position=tp),
                      inc="bar_tensor" if inc else None)

    def ln_tile(src_ap, dst_ap, sq=None, cv=None):
        sq = tB(D) if sq is None else sq
        cb = COLV if cv is None else cv

        def c(a, b):
            return fap(cb + a, [[1, b - a]])

        p.drain("vector")
        p.drain("scalar")
        t1 = p.tick("vector", lambda e: e.reduce_sum(out=c(0, 1), in_=src_ap,
                                                     axis=AX.X))
        t2 = p.tick("scalar", lambda e: e.activation(out=sq, in_=src_ap,
                                                     func=AF.Square,
                                                     accum_out=c(1, 2)))
        p.wait("vector", t2)
        p.drain("vector")
        p.tick("vector", lambda e: e.tensor_scalar(out=c(2, 3), in0=c(0, 1),
                                                   scalar1=1.0 / D, scalar2=None,
                                                   op0=OP.mult))
        p.drain("vector")
        p.tick("vector", lambda e: e.tensor_tensor(out=c(3, 4), in0=c(2, 3),
                                                   in1=c(2, 3), op=OP.mult))
        p.drain("vector")
        t3 = p.tick("vector", lambda e: e.tensor_scalar(out=c(4, 5), in0=c(1, 2),
                                                        scalar1=1.0 / D,
                                                        scalar2=c(3, 4),
                                                        op0=OP.mult, op1=OP.subtract))
        p.wait("scalar", t3)
        t4 = p.tick("scalar", lambda e: e.activation(out=c(5, 6), in_=c(4, 5),
                                                     func=AF.Sqrt, bias=float(EPS_LN)))
        p.wait("vector", t4)
        p.tick("vector", lambda e: e.reciprocal(out=c(6, 7), in_=c(5, 6)))
        p.drain("vector")
        t5 = p.tick("vector", lambda e, s=src_ap, d=dst_ap:
                    e.tensor_scalar(out=d, in0=s, scalar1=c(2, 3),
                                    scalar2=c(6, 7),
                                    op0=OP.subtract, op1=OP.mult))
        p.drain("vector")
        return t5

    def rms_tile(src_ap, dst_ap, w_bc, sq=None, cv=None):
        sq = tB(D) if sq is None else sq
        cb = COLV if cv is None else cv

        def c(a, b):
            return fap(cb + a, [[1, b - a]])

        p.drain("scalar")
        t2 = p.tick("scalar", lambda e, s=src_ap: e.activation(
            out=sq, in_=s, func=AF.Square, accum_out=c(1, 2)))
        p.wait("scalar", t2)
        t4 = p.tick("scalar", lambda e: e.activation(
            out=c(5, 6), in_=c(1, 2), func=AF.Sqrt,
            scale=1.0 / D, bias=float(EPS_RMS)))
        p.wait("vector", t4)
        p.tick("vector", lambda e: e.reciprocal(out=c(6, 7), in_=c(5, 6)))
        p.drain("vector")
        p.tick("vector", lambda e, s=src_ap: e.tensor_scalar(
            out=sq, in0=s, scalar1=c(6, 7), scalar2=None, op0=OP.mult))
        p.drain("vector")
        t = p.tick("vector", lambda e, d=dst_ap, w=w_bc, s2=sq: e.tensor_tensor(
            out=d, in0=s2, in1=w, op=OP.mult))
        p.drain("vector")
        return t

    iden_sb = es.enter_context(nc.sbuf_tensor("iden_sb", [128, 128], F32))

    def transpose_split(src_f32, nblk, dst_fn, banks=(4, 5, 6, 7), guard=None):
        nb = len(banks)
        tvs = {}
        last = None
        if guard is not None:
            p.wait("tensor", guard)
        for c in range(nblk):
            bank = ps[banks[c % nb]]
            if c - nb >= 0:
                p.wait("tensor", tvs[c - nb])
            ttok = p.tick("tensor", lambda e, s=src_f32, c=c, b=bank:
                          e.transpose(out=b[:, 0:128],
                                      in_=bass.AP(s.tensor, s.offset + c * 128,
                                                  [[AF32W, 128], [1, 128]]),
                                      identity=iden_sb[:]))
            p.wait("vector", ttok)
            p.wait("scalar", ttok)
            dh, dl = dst_fn(c)
            tv = split_pair("vector", dh, dl, bank[:, 0:128], hi_scalar=True)
            tvs[c] = tv
            last = tv
        p.wait("tensor", last)
        return last

    stage_guard = [None]

    def load_warena(nm, in_ap, base, kchunks, N, first):
        W = kchunks * N
        if first:
            step = max(1, 2048 // N) if N <= 2048 else 1
            prev = stage_guard[0]
            for c0 in range(0, kchunks, step):
                cn = min(step, kchunks - c0)
                stg_ap = fap(SS, [[1, cn * N]])
                stg3 = fap(SS, [[N, cn], [1, N]])
                if prev is not None:
                    p.wait("sync", prev)
                tok = p.dma("sync", stg3,
                            in_ap.rearrange("(c p) n -> p c n", p=128)[:, c0:c0 + cn, :],
                            stream="wst")
                p.wait("vector", tok)
                prev = split_pair("vector",
                                  bap(base + c0 * N, [[1, cn * N]]),
                                  bap(base + W + c0 * N, [[1, cn * N]]),
                                  stg_ap)
            stage_guard[0] = prev
            p.drain("vector")
            tv = p.mark("vector")
            p.wait("gpsimd", tv)
            t1 = p.dma("gpsimd", wc[nm]["h"][:, :], bap(base, [[1, W]]))
            t2 = p.dma("gpsimd", wc[nm]["l"][:, :], bap(base + W, [[1, W]]))
            return (tv, t2)
        th = p.dma("sync", bap(base, [[1, W]]), wc[nm]["h"][:, :])
        tl = p.dma("sync", bap(base + W, [[1, W]]), wc[nm]["l"][:, :])
        return (tl, tl)

    cache_guard = {}

    def load_slice(nm, in_ap, j, slot_base, first, kind, stream="wsl"):
        if kind == "w2e":
            e_i, jj = j
            rec = wcm["w2"][e_i * 16 + jj, :, :]
            srcw = w2_in[e_i, ts(jj, 128), :]
            colmode = False
        elif kind == "row":
            rec = wcm[nm][j, :, :]
            srcw = in_ap[ts(j, 128), :]
            colmode = False
        else:
            rec = wcm[nm][j, :, :]
            srcw = in_ap.rearrange("(c p) n -> p c n", p=128)[:, :, ts(j, 128)]
            colmode = True
        if first:
            stg_ap = fap(SS, [[1, 1024]])
            if stage_guard[0] is not None:
                p.wait("sync", stage_guard[0])
            tok = p.dma("sync", fap(SS, [[128, 8], [1, 128]]) if colmode else stg_ap,
                        srcw, stream="wst")
            p.wait("vector", tok)
            if cache_guard.get(slot_base) is not None:
                p.wait("vector", cache_guard[slot_base])
            p.drain("vector")
            p.tick("vector", lambda e: e.tensor_copy(
                out=bap(slot_base, [[1, 1024]]), in_=stg_ap))
            p.drain("vector")
            tg = p.tick("vector", lambda e: e.tensor_tensor(
                out=bap(slot_base + 1024, [[1, 1024]]), in0=stg_ap,
                in1=bap(slot_base, [[1, 1024]]), op=OP.subtract))
            p.wait("gpsimd", tg)
            tl = p.dma("gpsimd", rec, bap(slot_base, [[1, 2048]]),
                       stream="wc_" + stream)
            stage_guard[0] = tg
            cache_guard[slot_base] = tl
            return tg
        if cache_guard.get(slot_base) is not None:
            p.wait("sync", cache_guard[slot_base])
            cache_guard.pop(slot_base, None)
        return p.dma("sync", bap(slot_base, [[1, 2048]]), rec, stream=stream)

    p.dma("sync", iden_sb[:], iden_in[:, :])
    p.dma("sync", trig_ap, trig_in[:, :])
    p.dma("sync", mask_full, mask_in[:, :])
    tok_e = p.dma("sync", row(0, D), ema_in[:, :], stream="lde")
    tok_s = p.dma("sync", row(1024, 4), scal_in[:, :], stream="lds")
    tok_g = p.dma("sync", row(1056, 16), gbias_in[:, :], stream="ldg")
    tok_n1 = p.dma("sync", bass.AP(af, SS, [[AF32W, 1], [1, D]]), n1w_in[:, :], stream="ldn1")
    tok_n2 = p.dma("sync", bass.AP(af, SS + 1024, [[AF32W, 1], [1, D]]),
                   n2w_in[:, :], stream="ldn2")

    def srow_in(i):
        return bass.AP(af, ROWS + 1024 + i, [[AF32W, 1], [1, 1]])

    def sout(i, n=1):
        return bass.AP(af, ROWS + 1104 + i, [[AF32W, 1], [1, n]])

    p.wait("scalar", tok_s)
    p.tick("scalar", lambda e: e.activation(out=sout(5), in_=srow_in(0), func=AF.Exp))
    p.drain("scalar")
    p.tick("scalar", lambda e: e.activation(out=sout(1), in_=srow_in(1),
                                            func=AF.Sigmoid))
    p.drain("scalar")
    t_sc = p.tick("scalar", lambda e: e.activation(out=sout(3), in_=srow_in(2),
                                                   func=AF.Sigmoid))
    p.wait("vector", t_sc)
    p.tick("vector", lambda e: e.tensor_scalar(out=sout(0), in0=sout(5),
                                               scalar1=-1.0, scalar2=None,
                                               op0=OP.mult))
    p.tick("vector", lambda e: e.tensor_scalar(out=sout(2), in0=sout(1),
                                               scalar1=-1.0, scalar2=1.0,
                                               op0=OP.mult, op1=OP.add))
    t_sc2 = p.tick("vector", lambda e: e.tensor_scalar(out=sout(4), in0=sout(3),
                                                       scalar1=-1.0, scalar2=1.0,
                                                       op0=OP.mult, op1=OP.add))
    p.wait("scalar", tok_e)
    t_e1 = p.tick("scalar", lambda e: e.activation(out=bass.AP(af, SS + 2048,
                                                               [[AF32W, 1], [1, D]]),
                                                   in_=row(0, D), func=AF.Square,
                                                   accum_out=sout(8)))
    p.wait("scalar", t_e1)
    t_e2 = p.tick("scalar", lambda e: e.activation(out=sout(9), in_=sout(8),
                                                   func=AF.Sqrt))
    p.wait("vector", t_e2)
    p.tick("vector", lambda e: e.tensor_scalar(out=sout(10), in0=sout(9),
                                               scalar1=1e-12, scalar2=None,
                                               op0=OP.max))
    p.drain("vector")
    p.tick("vector", lambda e: e.reciprocal(out=sout(11), in_=sout(10)))
    p.drain("vector")
    t_e3 = p.tick("vector", lambda e: e.tensor_scalar(out=row(0, D), in0=row(0, D),
                                                      scalar1=sout(11),
                                                      scalar2=None, op0=OP.mult))
    p.wait("gpsimd", t_e3)
    tb1 = p.dma("gpsimd", bounce[0:1, 0:D], row(0, D))
    p.wait("gpsimd", tok_n1)
    tb2 = p.dma("gpsimd", bounce[0:1, D:2 * D],
                bass.AP(af, SS, [[AF32W, 1], [1, D]]))
    p.wait("gpsimd", tok_n2)
    tb3 = p.dma("gpsimd", bounce[0:1, 2 * D:3 * D],
                bass.AP(af, SS + 1024, [[AF32W, 1], [1, D]]))
    p.wait("gpsimd", t_sc2)
    p.wait("gpsimd", tok_g)
    tb4 = p.dma("gpsimd", bounce[0:1, 3 * D:3 * D + 5], sout(0, 5))
    tb5 = p.dma("gpsimd", bounce[0:1, 3 * D + 16:3 * D + 32], row(1056, 16))
    p.wait("gpsimd", tb5)
    p.dma("gpsimd", ema_bc, bass.AP(bounce.tensor, 0, [[0, 128], [1, D]]))
    p.dma("gpsimd", n1w_bc, bass.AP(bounce.tensor, D, [[0, 128], [1, D]]))
    p.dma("gpsimd", n2w_bc, bass.AP(bounce.tensor, 2 * D, [[0, 128], [1, D]]))
    p.dma("gpsimd", fap(SCBC, [[1, 5]]),
          bass.AP(bounce.tensor, 3 * D, [[0, 128], [1, 5]]))
    p.dma("gpsimd", fap(GBBC, [[1, 16]]),
          bass.AP(bounce.tensor, 3 * D + 16, [[0, 128], [1, 16]]))
    p.barrier("consts", light=False)

    scbc = fap(SCBC, [[1, 5]])

    def scb(i):
        return fap(SCBC + i, [[1, 1]])

    gbbc = fap(GBBC, [[1, 16]])

    def ffn_call(call, src_x, dst_x, gated=False):
        first = (call == 0)

        a1_pre = []
        spv_prev = {}
        qspl_prev = None
        load_warena("wkc", wkc_in, BSH2, 8, DC, first)
        load_warena("wvc", wvc_in, BSH2 + 4096, 8, DC, first)
        load_warena("wk", wk_in, BSH2 + 8192, 2, D, first)
        load_warena("wv", wv_in, BSH2 + 12288, 2, D, first)
        p.barrier(f"aw_{call}")
        for q in range(NQ):
            firstq = first and q == 0
            prev_ts = [None]
            tsv = {}
            XBS = (TA, TC, SS, SS + 1024)
            SQS = (TB, SS + 2080, SS + 3104, SS + 4224)
            CVS = (COLV, COLV + 8, COLV + 16, COLV + 24)
            GT = MISC + 1768

            def c_(i, a, b):
                return fap(CVS[i] + a, [[1, b - a]])

            def xv(i, n=D):
                return fap(XBS[i], [[1, n]])

            def sv(i, n=D):
                return fap(SQS[i], [[1, n]])

            def gt(a, b):
                return fap(GT + a, [[1, b - a]])

            ltok = {}
            gtok = {}
            if a1_pre:
                p.wait_max("sync", a1_pre)
            for i in range(TPQ):
                tt = q * TPQ + i
                if gated:
                    gtok[i] = p.dma("sync", sv(i), sp_g[ts(tt, 128), :],
                                    stream=f"ldg{i}")
                ltok[i] = p.dma("sync", xv(i), src_x[ts(tt, 128), :],
                                stream=f"ldx{i}")
            if gated:
                p.drain("vector")
                p.tick("vector", lambda e, q=q: e.tensor_scalar(
                    out=gt(0, TPQ), in0=fap(EMAS + q * TPQ, [[1, TPQ]]),
                    scalar1=scb(4), scalar2=None, op0=OP.mult))
                p.drain("vector")
                tg = p.tick("vector", lambda e, q=q: e.scalar_tensor_tensor(
                    out=gt(4, 4 + TPQ), in0=fap(SEQS + q * TPQ, [[1, TPQ]]),
                    scalar=scb(3), in1=gt(0, TPQ), op0=OP.mult, op1=OP.add))
                p.wait("scalar", tg)
                p.drain("scalar")
                tg = p.tick("scalar", lambda e: e.activation(
                    out=gt(8, 8 + TPQ), in_=gt(4, 4 + TPQ), func=AF.Exp,
                    scale=scb(0)))
                p.wait("vector", tg)
                p.tick("vector", lambda e: e.scalar_tensor_tensor(
                    out=gt(12, 12 + TPQ), in0=gt(8, 8 + TPQ), scalar=scb(1),
                    in1=bass.AP(af, SCBC + 2, [[AF32W, 128], [0, TPQ]]),
                    op0=OP.mult, op1=OP.add))
                p.drain("vector")
                for i in range(TPQ):
                    p.wait("vector", gtok[i])
                    p.tick("vector", lambda e, i=i: e.tensor_scalar(
                        out=sv(i), in0=sv(i), scalar1=fap(GT + 12 + i, [[1, 1]]),
                        scalar2=None, op0=OP.mult))
                p.drain("vector")
                tres = None
                sp2 = {}
                for i in range(TPQ):
                    tt = q * TPQ + i
                    p.wait("vector", ltok[i])
                    tres = p.tick("vector", lambda e, i=i: e.tensor_tensor(
                        out=xv(i), in0=xv(i), in1=sv(i), op=OP.add))
                    p.drain("vector")
                    p.wait("gpsimd", tres)
                    sp2[i] = p.dma("gpsimd", sp_x[ts(tt, 128), :], xv(i),
                                   stream=f"spx2{i}")
                for i in range(TPQ):
                    p.wait("vector", sp2[i])
                p.wait("scalar", tres)
            p.drain("vector")
            p.drain("scalar")
            for i in range(TPQ):
                p.wait("vector", ltok[i])
                p.wait("scalar", ltok[i])
                p.tick("vector", lambda e, i=i: e.reduce_sum(
                    out=c_(i, 0, 1), in_=xv(i), axis=AX.X))
                p.tick("scalar", lambda e, i=i: e.activation(
                    out=sv(i), in_=xv(i), func=AF.Square, accum_out=c_(i, 1, 2)))
            tS = p.mark("scalar")
            p.wait("vector", tS)
            p.drain("vector")
            for i in range(TPQ):
                p.tick("vector", lambda e, i=i: e.tensor_scalar(
                    out=c_(i, 2, 3), in0=c_(i, 0, 1), scalar1=1.0 / D,
                    scalar2=None, op0=OP.mult))
            p.drain("vector")
            for i in range(TPQ):
                p.tick("vector", lambda e, i=i: e.tensor_tensor(
                    out=c_(i, 3, 4), in0=c_(i, 2, 3), in1=c_(i, 2, 3),
                    op=OP.mult))
            p.drain("vector")
            tV = None
            for i in range(TPQ):
                tV = p.tick("vector", lambda e, i=i: e.tensor_scalar(
                    out=c_(i, 4, 5), in0=c_(i, 1, 2), scalar1=1.0 / D,
                    scalar2=c_(i, 3, 4), op0=OP.mult, op1=OP.subtract))
            p.wait("scalar", tV)
            p.drain("scalar")
            tS = None
            for i in range(TPQ):
                tS = p.tick("scalar", lambda e, i=i: e.activation(
                    out=c_(i, 5, 6), in_=c_(i, 4, 5), func=AF.Sqrt,
                    bias=float(EPS_LN)))
            p.wait("vector", tS)
            p.drain("vector")
            for i in range(TPQ):
                p.tick("vector", lambda e, i=i: e.reciprocal(
                    out=c_(i, 6, 7), in_=c_(i, 5, 6)))
            p.drain("vector")
            sy_tok = {}
            for i in range(TPQ):
                tt = q * TPQ + i
                t5 = p.tick("vector", lambda e, i=i: e.tensor_scalar(
                    out=xv(i), in0=xv(i), scalar1=c_(i, 2, 3), scalar2=c_(i, 6, 7),
                    op0=OP.subtract, op1=OP.mult))
                p.drain("vector")
                p.wait("gpsimd", t5)
                sy_tok[i] = p.dma("gpsimd", sp_y[ts(tt, 128), :], xv(i),
                                  stream=f"spy{i % 2}")
                p.wait("tensor", t5)
                prev_ts[0] = transpose_split(
                    fap(XBS[i], [[1, 1024]]), 8,
                    lambda c, ti=i: (
                        bap(BFM + c * 2 * QTOK + ti * 128, [[1, 128]]),
                        bap(BFM + c * 2 * QTOK + QTOK + ti * 128, [[1, 128]])),
                    guard=prev_ts[0])
                tsv[i] = prev_ts[0]
            p.barrier(f"a1_{call}_{q}")
            def fmh(c, w0, wn):
                return bap(BFM + c * 2 * QTOK + w0, [[1, wn]])

            def fml(c, w0, wn):
                return bap(BFM + c * 2 * QTOK + QTOK + w0, [[1, wn]])

            NW = QTOK // 512 if QTOK >= 512 else 1
            W5 = min(512, QTOK)
            bank_guard = {}
            for (wb, vb) in ((BSH2, BSH1), (BSH2 + 4096, BSH1 + 4 * QTOK)):
                for dc in range(2):
                    for w in range(NW):
                        bi = dc * NW + w
                        bank = ps[bi]
                        if bi in bank_guard:
                            p.wait("tensor", bank_guard[bi])
                        tokmm = None
                        for kc in range(8):
                            tokmm = mm3(bank[:, 0:W5],
                                        bap(wb + kc * DC + ts(dc, 128).start, [[1, 128]]),
                                        bap(wb + 8 * DC + kc * DC + dc * 128, [[1, 128]]),
                                        fmh(kc, w * W5, W5), fml(kc, w * W5, W5),
                                        start=(kc == 0), stop=(kc == 7),
                                        inc=(kc == 7))
                        tmm = tokmm
                        p.wait("vector", tmm)
                        p.wait("scalar", tmm)
                        tv = split_pair("vector",
                                        bap(vb + dc * QTOK + w * W5, [[1, W5]]),
                                        bap(vb + 2 * QTOK + dc * QTOK + w * W5,
                                            [[1, W5]]),
                                        bank[:, 0:W5], hi_scalar=True)
                        bank_guard[bi] = tv

            def kch(dc, w0, wn):
                return bap(BSH1 + dc * QTOK + w0, [[1, wn]])

            def kcl(dc, w0, wn):
                return bap(BSH1 + 2 * QTOK + dc * QTOK + w0, [[1, wn]])

            def vch(dc, w0, wn):
                return bap(BSH1 + 4 * QTOK + dc * QTOK + w0, [[1, wn]])

            def vcl(dc, w0, wn):
                return bap(BSH1 + 6 * QTOK + dc * QTOK + w0, [[1, wn]])

            WVB = BSH2 + 12288
            for _t in spv_prev.values():
                p.wait("vector", _t)
            for bi in range(2):
                p.tick("vector", lambda e, bi=bi: e.memset(
                    bass.AP(af, SS + bi * 1040 + 64, [[AF32W, 128], [65, 16]]),
                    1.0))
            p.drain("vector")
            tvz = p.mark("vector")
            p.wait("tensor", tvz)
            spv_last = {}
            for ti in range(TPQ):
                tt = q * TPQ + ti
                vboff = SS + (ti % 2) * 1040
                if ti - 2 >= 0:
                    p.wait("vector", spv_last[ti - 2])
                for half in range(2):
                    bank = ps[half]
                    if half in bank_guard:
                        p.wait("tensor", bank_guard[half])
                    tokmm = None
                    for dc in range(2):
                        tokmm = mm3(bank[:, 0:512],
                                    vch(dc, ti * 128, 128), vcl(dc, ti * 128, 128),
                                    bap(WVB + dc * D + half * 512, [[1, 512]]),
                                    bap(WVB + 2 * D + dc * D + half * 512, [[1, 512]]),
                                    start=(dc == 0), stop=(dc == 1),
                                    inc=(dc == 1))
                    tmm = tokmm
                    p.wait("vector", tmm)
                    tv = p.tick("vector", lambda e, b=bank, o=vboff, hf=half:
                                e.tensor_copy(
                                    out=bass.AP(af, o + hf * 8 * 65,
                                                [[AF32W, 128], [65, 8], [1, 64]]),
                                    in_=b[:, 0:512].rearrange("p (h d) -> p h d", h=8)))
                    bank_guard[half] = tv
                p.drain("vector")
                tv2 = p.mark("vector")
                p.wait("gpsimd", tv2)
                for hp in range(NHP):
                    spv_last[ti] = p.dma(
                        "gpsimd", sp_v[hp, ts(tt, 128), :],
                        bass.AP(af, vboff + hp * 130, [[AF32W, 128], [1, 130]]),
                        stream=f"spv{ti % 2}")

            WKB = BSH2 + 8192

            def qk_post(sp_dst, tag, tmm, prev_spl=None, bar=True):
                p.wait("vector", tmm)
                raw = SS + 2080
                for ti in range(TPQ):
                    for half in range(2):
                        bank = ps[ti * 2 + half]
                        p.tick("vector", lambda e, b=bank, ti=ti, hf=half:
                               e.tensor_copy(out=bass.AP(af,
                                                         raw + ti * 1024 + hf * 512,
                                                         [[AF32W, 128], [1, 512]]),
                                             in_=b[:, 0:512]))
                p.drain("vector")
                tv = p.mark("vector")
                p.wait("tensor", tv)
                spl_tok = {}
                prev_ts = [None]
                for ti in range(TPQ):
                    tt = q * TPQ + ti
                    par = ti % 2
                    robase = TB if par == 0 else TC
                    stg_b = BSH1 + 8 * QTOK + par * 2048
                    if ti - 2 >= 0 and (ti - 2) in spl_tok:
                        p.wait("vector", spl_tok[ti - 2])
                        p.wait("scalar", spl_tok[ti - 2])
                    elif prev_spl is not None and par in prev_spl:
                        p.wait("vector", prev_spl[par])
                        p.wait("scalar", prev_spl[par])
                    trp = rope_tile(raw + ti * 1024, tt, out_off=robase)
                    p.wait("tensor", trp)
                    prev_ts[0] = transpose_split(
                        fap(robase, [[1, 1024]]), 8,
                        lambda c, sb=stg_b: (bap(sb + c * 128, [[1, 128]]),
                                             bap(sb + 1024 + c * 128, [[1, 128]])),
                        guard=prev_ts[0])
                    p.drain("vector")
                    tvx = p.mark("vector")
                    p.wait("gpsimd", tvx)
                    tsp = None
                    for c in range(8):
                        p.dma("gpsimd", sp_dst["h"][c, :, ts(tt, 128)],
                              bap(stg_b + c * 128, [[1, 128]]), stream=f"spqk{par}")
                        tsp = p.dma("gpsimd", sp_dst["l"][c, :, ts(tt, 128)],
                                    bap(stg_b + 1024 + c * 128, [[1, 128]]),
                                    stream=f"spqk{par}")
                    spl_tok[ti] = tsp
                if bar:
                    p.barrier(f"qk_{tag}_{call}_{q}")
                return {ti % 2: t for ti, t in spl_tok.items()}

            for _bi, _t in bank_guard.items():
                p.wait("tensor", _t)
            bank_guard = {}
            qk_last = [None]
            for kc in range(2):
                for ti in range(TPQ):
                    for half in range(2):
                        bank = ps[ti * 2 + half]
                        last = (kc == 1 and ti == TPQ - 1 and half == 1)
                        qk_last[0] = mm3(bank[:, 0:512],
                            kch(kc, ti * 128, 128), kcl(kc, ti * 128, 128),
                            bap(WKB + kc * D + half * 512, [[1, 512]]),
                            bap(WKB + 2 * D + kc * D + half * 512, [[1, 512]]),
                            start=(kc == 0), stop=(kc == 1), inc=last)
            spk_tok = qk_post(sp_k, "k", qk_last[0], prev_spl=qspl_prev,
                              bar=False)

            wq_tok = {}

            def load_wq(kc):
                wq_tok[kc] = load_slice("wq", wq_in, kc, BST + (kc % 4) * 2048,
                                        firstq, "row", stream=f"wq{kc % 4}")

            if firstq:
                p.wait_max("sync", [spv_last[ti] for ti in spv_last])
            load_wq(0)
            load_wq(1)
            load_wq(2)
            load_wq(3)
            for kc in range(8):
                p.wait("tensor", wq_tok[kc])
                for ti in range(TPQ):
                    for half in range(2):
                        bank = ps[ti * 2 + half]
                        last = (ti == TPQ - 1 and half == 1)
                        qk_last[0] = mm3(bank[:, 0:512],
                            fmh(kc, ti * 128, 128), fml(kc, ti * 128, 128),
                            bap(BST + (kc % 4) * 2048 + half * 512, [[1, 512]]),
                            bap(BST + (kc % 4) * 2048 + 1024 + half * 512,
                                [[1, 512]]),
                            start=(kc == 0), stop=(kc == 7), inc=last)
                if kc < 7:
                    tmk = qk_last[0]
                    nxt = kc + 4
                    if nxt < 8:
                        p.wait("sync", tmk)
                        if firstq:
                            p.wait("vector", tmk)
                        load_wq(nxt)
            qspl_prev = qk_post(sp_q, "q", qk_last[0], prev_spl=spk_tok,
                                bar=False)
            spv_prev = {ti: spv_last[ti] for ti in spv_last if ti >= TPQ - 2}
            a1_pre = list(qspl_prev.values()) + list(spv_prev.values())
            a1_pre += [sy_tok[i] for i in sy_tok if i >= TPQ - 2]
            a1_pre += list(sp2.values()) if gated else []
            if stage_guard[0] is not None:
                a1_pre.append(stage_guard[0])
        p.barrier(f"a4f_{call}")

        attn_phase()
        p.barrier(f"attn_{call}", light=False)

        t_wo = load_warena("wo", wo_in, BSH2, 8, D, first)
        t_wg = load_warena("wgate", wgate_in, BST, 8, 16, first)
        p.barrier(f"wo_w_{call}")
        for q in range(NQ):
            prev_ts = [None]
            sp_tok = {}
            tsv = {}
            add_tok = {}
            for ti in range(TPQ):
                tt = q * TPQ + ti
                par = ti % 2
                sab = BSH1 + 8 * QTOK + par * 2048
                x1b = TA if par == 0 else SS
                yb = TC if par == 0 else SS + 1024
                sqb = TB if par == 0 else SS + 2080
                cvb = COLV if par == 0 else COLV + 8
                if ti - 2 >= 0:
                    p.wait("sync", tsv[ti - 2])
                    p.wait("sync", sp_tok[ti - 2])
                tok1 = p.dma("sync", bap(sab, [[128, 8], [1, 128]]),
                             sp_sa["h"][:, :, ts(tt, 128)].rearrange("c p j -> p c j"),
                             stream=f"ldsa{par}")
                tok2 = p.dma("sync", bap(sab + 1024, [[128, 8], [1, 128]]),
                             sp_sa["l"][:, :, ts(tt, 128)].rearrange("c p j -> p c j"),
                             stream=f"ldsa{par}")
                toky = p.dma("sync", fap(yb, [[1, D]]), sp_y[ts(tt, 128), :],
                             stream=f"ldy{par}")
                p.wait_max("tensor", [tok1, tok2])
                for half in range(2):
                    bank = ps[half + 2 * par]
                    if ti - 2 >= 0:
                        p.wait("tensor", add_tok[(ti - 2, half)])
                    for kc in range(8):
                        tokmm = mm3(bank[:, 0:512],
                                    bap(sab + kc * 128, [[1, 128]]),
                                    bap(sab + 1024 + kc * 128, [[1, 128]]),
                                    bap(BSH2 + kc * D + half * 512, [[1, 512]]),
                                    bap(BSH2 + 8 * D + kc * D + half * 512, [[1, 512]]),
                                    start=(kc == 0), stop=(kc == 7),
                                    inc=(kc == 7))
                    tmm = tokmm
                    p.wait("vector", tmm)
                    p.wait("vector", toky)
                    add_tok[(ti, half)] = p.tick(
                        "vector", lambda e, b=bank, hf=half, xb=x1b, yb2=yb:
                        e.tensor_tensor(out=fap(xb + hf * 512, [[1, 512]]),
                                        in0=fap(yb2 + hf * 512, [[1, 512]]),
                                        in1=b[:, 0:512], op=OP.add))
                p.drain("vector")
                tvv = p.mark("vector")
                p.wait("scalar", tvv)
                trms = rms_tile(fap(x1b, [[1, D]]), fap(x1b, [[1, D]]), n1w_bc,
                                sq=fap(sqb, [[1, D]]), cv=cvb)
                p.wait("gpsimd", trms)
                sp_tok[ti] = p.dma("gpsimd", sp_flat[ts(tt, 128), :],
                                   fap(x1b, [[1, D]]), stream=f"spfl{par}")
                p.wait("tensor", trms)
                prev_ts[0] = transpose_split(
                    fap(x1b, [[1, 1024]]), 8,
                    lambda c, ti=ti: (
                        bap(BFM + c * 2 * QTOK + ti * 128, [[1, 128]]),
                        bap(BFM + c * 2 * QTOK + QTOK + ti * 128, [[1, 128]])),
                    guard=prev_ts[0])
                tsv[ti] = prev_ts[0]

            def fmh(c, w0, wn):
                return bap(BFM + c * 2 * QTOK + w0, [[1, wn]])

            def fml(c, w0, wn):
                return bap(BFM + c * 2 * QTOK + QTOK + w0, [[1, wn]])

            p.wait("gpsimd", tsv[TPQ - 1])
            for c in range(8):
                p.dma("gpsimd", sp_fm["h"][c, :, ts(q, QTOK)], fmh(c, 0, QTOK))
                p.dma("gpsimd", sp_fm["l"][c, :, ts(q, QTOK)], fml(c, 0, QTOK))
            gating_quarter(q, fmh, fml)
            p.barrier(f"cg_{call}_{q}")

        fm_war = None
        ps_war = None

        def reload_fm(q):
            toks = []
            for c in range(8):
                toks.append(p.dma("sync", bap(BFM + c * 2 * QTOK, [[1, QTOK]]),
                                  sp_fm["h"][c, :, ts(q, QTOK)], stream="fmld"))
                toks.append(p.dma("sync",
                                  bap(BFM + c * 2 * QTOK + QTOK, [[1, QTOK]]),
                                  sp_fm["l"][c, :, ts(q, QTOK)], stream="fmld"))
            return toks

        fm_pre = {}
        for q in range(NQ):
            firstq = first and q == 0
            if q in fm_pre:
                fm_toks = fm_pre.pop(q)
            else:
                if fm_war is not None:
                    p.wait("sync", fm_war)
                fm_toks = reload_fm(q)

            def fmh(c, w0, wn):
                return bap(BFM + c * 2 * QTOK + w0, [[1, wn]])

            def fml(c, w0, wn):
                return bap(BFM + c * 2 * QTOK + QTOK + w0, [[1, wn]])

            wr = wso_ring(firstq)
            if not firstq:
                for _j in range(4):
                    wr[1](_j)
            sw_mm, sw_spl = produce_hidden("wsi", wsi_in, q, fmh, fml, firstq,
                                           fm_guard=fm_toks, psum_guard=ps_war)
            if firstq:
                for _j in range(4):
                    wr[1](_j)
            for _t in sw_spl.values():
                p.wait("tensor", _t)
            consume_hidden_shared(q, firstq, wr)
            fm_war, h_spl = produce_hidden("w1", w1_in, q, fmh, fml, firstq,
                                           ring_guard=sw_mm)
            if q + 1 < NQ:
                p.wait("sync", fm_war)
                fm_pre[q + 1] = reload_fm(q + 1)
            ps_war = experts_quarter(q, firstq, pre_splits=h_spl,
                                     ring_guard=fm_war)
            sp_tok = {}
            res_tok = {}
            p.drain("vector")
            for ti in range(TPQ):
                tt = q * TPQ + ti
                par = ti % 2
                fb = TA if par == 0 else SS + 4224
                xb = TC if par == 0 else SS + 5248
                sqb = SS + 3104 if par == 0 else SS + 6272
                cvb = COLV + 16 if par == 0 else COLV + 24
                if ti - 2 >= 0:
                    p.wait("sync", sp_tok[ti - 2])
                tokf = p.dma("sync", fap(fb, [[1, D]]), sp_flat[ts(tt, 128), :],
                             stream=f"ldy{par}")
                tokx = p.dma("sync", fap(xb, [[1, D]]), src_x[ts(tt, 128), :],
                             stream=f"ldx{par}")
                p.wait("vector", tokf)
                p.tick("vector", lambda e, ti=ti, fb=fb: e.tensor_tensor(
                    out=fap(fb, [[1, D]]), in0=fap(fb, [[1, D]]), in1=accq(ti),
                    op=OP.add))
                p.drain("vector")
                tvv = p.mark("vector")
                p.wait("scalar", tvv)
                trms = rms_tile(fap(fb, [[1, D]]), fap(fb, [[1, D]]), n2w_bc,
                                sq=fap(sqb, [[1, D]]), cv=cvb)
                p.wait("vector", tokx)
                tres = p.tick("vector", lambda e, fb=fb, xb=xb:
                              e.scalar_tensor_tensor(
                    out=fap(fb, [[1, D]]), in0=fap(fb, [[1, D]]), scalar=0.5,
                    in1=fap(xb, [[1, D]]), op0=OP.mult, op1=OP.add))
                p.drain("vector")
                res_tok[ti] = tres
                p.wait("gpsimd", tres)
                sp_tok[ti] = p.dma("gpsimd", dst_x[ts(tt, 128), :],
                                   fap(fb, [[1, D]]), stream=f"spx{par}")
        p.barrier(f"ffn_{call}", light=False)

    def rope_tile(raw_off, tt, out_off=None):
        out_off = TB if out_off is None else out_off
        sc = SS + 2080 + TPQ * 1024
        cosap = bass.AP(af, TRIG + tt * 64, [[AF32W, 128], [0, 16], [1, 32]])
        sinap = bass.AP(af, TRIG + tt * 64 + 32,
                        [[AF32W, 128], [0, 16], [1, 32]])
        ev = bass.AP(af, raw_off, [[AF32W, 128], [64, 16], [2, 32]])
        od = bass.AP(af, raw_off + 1, [[AF32W, 128], [64, 16], [2, 32]])
        top = bass.AP(af, out_off, [[AF32W, 128], [64, 16], [1, 32]])
        bot = bass.AP(af, out_off + 32, [[AF32W, 128], [64, 16], [1, 32]])
        sc1 = bass.AP(af, sc, [[AF32W, 128], [64, 16], [1, 32]])
        sc2 = bass.AP(af, sc + 32, [[AF32W, 128], [64, 16], [1, 32]])
        p.tick("vector", lambda e: e.tensor_tensor(out=sc1, in0=ev, in1=cosap, op=OP.mult))
        p.tick("vector", lambda e: e.tensor_tensor(out=sc2, in0=od, in1=sinap, op=OP.mult))
        p.drain("vector")
        p.tick("vector", lambda e: e.tensor_tensor(out=top, in0=sc1, in1=sc2,
                                                   op=OP.subtract))
        p.drain("vector")
        p.tick("vector", lambda e: e.tensor_tensor(out=sc1, in0=ev, in1=sinap, op=OP.mult))
        p.tick("vector", lambda e: e.tensor_tensor(out=sc2, in0=od, in1=cosap, op=OP.mult))
        p.drain("vector")
        t = p.tick("vector", lambda e: e.tensor_tensor(out=bot, in0=sc1, in1=sc2,
                                                       op=OP.add))
        p.drain("vector")
        return t

    def attn_phase():
        KB = BSH2
        QB = BSH2 + 4 * T
        SAB = QB + 4 * QW
        VS = NKT * 130
        EXPS = SS + 2 * VS
        BCD = EXPS + 6 * QW
        RR = BCD + QW
        tpv = {}
        tfin = {}
        kv_tok = {}
        q_tok = {}

        def load_kv(hp):
            db = hp % 2
            kst = f"k{db}"
            t1 = p.dma("sync", bap(KB + db * T, [[1, T]]), sp_k["h"][hp, :, :],
                       stream=kst)
            t2 = p.dma("sync", bap(KB + 2 * T + db * T, [[1, T]]),
                       sp_k["l"][hp, :, :], stream=kst)
            t3 = p.dma("sync",
                       bass.AP(af, SS + db * VS,
                               [[AF32W, 128], [130, NKT], [1, 130]]),
                       sp_v[hp, :, :].rearrange("(k p) n -> p k n", p=128),
                       stream=f"v{db}")
            kv_tok[hp] = [t1, t2, t3]

        def load_q(hp, qt, qdb):
            qst = f"q{qdb}"
            t1 = p.dma("sync", bap(QB + qdb * QW, [[1, QW]]),
                       sp_q["h"][hp, :, ts(qt, QW)], stream=qst)
            t2 = p.dma("sync", bap(QB + 2 * QW + qdb * QW, [[1, QW]]),
                       sp_q["l"][hp, :, ts(qt, QW)], stream=qst)
            q_tok[(hp, qt)] = [t1, t2]

        load_kv(0)
        load_q(0, 0, 0)
        last_pv_user = {}
        seq = [(hp, qt) for hp in range(NHP) for qt in range(NQT)]
        for idx, (hp, qt) in enumerate(seq):
            db = hp % 2
            qdb = idx % 2
            pvpar = idx % 2
            pvb = (ps[4], ps[5]) if pvpar == 0 else (ps[6], ps[7])
            if idx + 1 < len(seq):
                nhp, nqt = seq[idx + 1]
                if idx >= 1:
                    p.wait("sync", tpv[seq[idx - 1]])
                if nqt == 0:
                    load_kv(nhp)
                load_q(nhp, nqt, (idx + 1) % 2)
            p.wait_max("tensor", kv_tok[hp] + q_tok[(hp, qt)])
            texp = {}

            def scores(kt):
                pa = ps[kt % 2]
                pb = ps[2 + kt % 2]
                if kt >= 2 and (kt - 2) in texp:
                    for t in texp[kt - 2]:
                        p.wait("tensor", t)
                tmm = None
                for (base, bank, tp) in ((0, pa, (0, 0)), (64, pb, (64, 0))):
                    tmm = mm3(bank[:, 0:QW],
                        bass.AP(ab, base * ABFW + KB + db * T + kt * 128,
                                [[ABFW, 64], [1, 128]]),
                        bass.AP(ab, base * ABFW + KB + 2 * T + db * T
                                + kt * 128, [[ABFW, 64], [1, 128]]),
                        bass.AP(ab, base * ABFW + QB + qdb * QW,
                                [[ABFW, 64], [1, QW]]),
                        bass.AP(ab, base * ABFW + QB + 2 * QW + qdb * QW,
                                [[ABFW, 64], [1, QW]]),
                        start=True, stop=True, tile_position=tp,
                        inc=(base == 64))
                p.wait("scalar", tmm)
                if qt >= 1 or (hp, qt) != (0, 0):
                    prev = seq[idx - 1] if kt == 0 else None
                    if prev is not None and prev in tpv:
                        p.wait("scalar", tpv[prev])
                es = []
                for (i, bank) in ((0, pa), (1, pb)):
                    es.append(p.tick("scalar", lambda e, i=i, b=bank, kt=kt:
                                     e.activation(
                                         out=fap(EXPS + (3 * i + kt % 3) * QW,
                                                 [[1, QW]]),
                                         in_=b[:, 0:QW], func=AF.Exp,
                                         scale=0.125)))
                texp[kt] = es

            def pv(kt):
                for t in texp[kt]:
                    p.wait("tensor", t)
                if kt == 0 and last_pv_user.get(pvpar) is not None:
                    p.wait("tensor", tfin[last_pv_user[pvpar]])
                for (i, bank) in ((0, pvb[0]), (1, pvb[1])):
                    voff = SS + db * VS + kt * 130 + i * 65
                    pv_last[0] = p.emit(
                        "tensor", lambda e, b=bank, i=i, kt=kt, voff=voff:
                        e.matmul(out=b[0:65, 0:QW],
                                 lhsT=bass.AP(af, voff,
                                              [[AF32W, 128], [1, 65]]),
                                 rhs=fap(EXPS + (3 * i + kt % 3) * QW,
                                         [[1, QW]]),
                                 start=(kt == 0), stop=(kt == NKT - 1)),
                        inc="bar_tensor" if (kt == NKT - 1 and i == 1) else None)

            pv_last = [None]
            scores(0)
            if NKT > 1:
                scores(1)
            for kt in range(2, NKT):
                scores(kt)
                pv(kt - 2)
            if NKT > 1:
                pv(NKT - 2)
            pv(NKT - 1)
            tpv[(hp, qt)] = pv_last[0]
            p.wait("vector", tpv[(hp, qt)])
            for (i, bank) in ((0, pvb[0]), (1, pvb[1])):
                p.tick("vector", lambda e, b=bank: e.reciprocal(
                    out=bass.AP(af, 64 * AF32W + RR, [[AF32W, 1], [1, QW]]),
                    in_=b[64:65, 0:QW]))
                p.drain("vector")
                tv = p.mark("vector")
                p.wait("gpsimd", tv)
                t1 = p.dma("gpsimd", bounce[0:1, 0:QW],
                           bass.AP(af, 64 * AF32W + RR,
                                   [[AF32W, 1], [1, QW]]), stream="bnc")
                p.wait("gpsimd", t1)
                t2 = p.dma("gpsimd",
                           bass.AP(af, BCD, [[AF32W, 64], [1, QW]]),
                           bass.AP(bounce.tensor, 0, [[0, 64], [1, QW]]),
                           stream="bnc")
                p.wait("vector", t2)
                p.tick("vector", lambda e, b=bank: e.tensor_tensor(
                    out=bass.AP(af, RR, [[AF32W, 64], [1, QW]]),
                    in0=b[0:64, 0:QW],
                    in1=bass.AP(af, BCD, [[AF32W, 64], [1, QW]]),
                    op=OP.mult))
                p.drain("vector")
                tv2 = split_pair("vector",
                                 bass.AP(ab, SAB, [[ABFW, 64], [1, QW]]),
                                 bass.AP(ab, SAB + QW, [[ABFW, 64], [1, QW]]),
                                 bass.AP(af, RR, [[AF32W, 64], [1, QW]]))
                p.wait("gpsimd", tv2)
                rb = 0 if i == 0 else 64
                td1 = p.dma("gpsimd", sp_sa["h"][hp, rb:rb + 64, ts(qt, QW)],
                            bass.AP(ab, SAB, [[ABFW, 64], [1, QW]]), stream="spsa")
                td2 = p.dma("gpsimd", sp_sa["l"][hp, rb:rb + 64, ts(qt, QW)],
                            bass.AP(ab, SAB + QW, [[ABFW, 64], [1, QW]]),
                            stream="spsa")
                p.wait("vector", td2)
            tfin[(hp, qt)] = p.mark("vector")
            last_pv_user[pvpar] = (hp, qt)

    def gating_quarter(q, fmh, fml):
        NW = QTOK // 512 if QTOK >= 512 else 1
        W5 = min(512, QTOK)
        LOG = ACCQ
        for w in range(NW):
            bank = ps[w % 2]
            tokmm = None
            for kc in range(8):
                tokmm = mm3(bank[0:10, 0:W5],
                            bap(BST + kc * 16, [[1, 10]]),
                            bap(BST + 8 * 16 + kc * 16, [[1, 10]]),
                            fmh(kc, w * W5, W5), fml(kc, w * W5, W5),
                            start=(kc == 0), stop=(kc == 7), inc=(kc == 7))
            tmm = tokmm
            p.wait("vector", tmm)
            tv = p.tick("vector", lambda e, b=bank, w=w: e.tensor_copy(
                out=bass.AP(af, LOG + w * W5, [[AF32W, 10], [1, W5]]),
                in_=b[0:10, 0:W5]))
            p.wait("tensor", tv)
        p.drain("vector")
        tvx = p.mark("vector")
        p.wait("tensor", tvx)
        tcop = None
        for ti in range(TPQ):
            bank = ps[2 + ti % 2]
            if ti - 2 >= 0 and tcop is not None:
                p.wait("tensor", tcop)
            ttok = p.tick("tensor", lambda e, ti=ti, b=bank: e.transpose(
                out=b[:, 0:10], in_=bass.AP(af, LOG + ti * 128,
                                            [[AF32W, 10], [1, 128]]),
                identity=iden_sb[0:10, 0:10]))
            p.wait("vector", ttok)
            tcop = p.tick("vector", lambda e, ti=ti, b=bank: e.tensor_copy(
                out=fap(TB + ti * 24, [[1, 10]]), in_=b[:, 0:10]))
        p.drain("vector")
        gate_batch(q)
        p.drain("vector")

    def gate_batch(q):
        NTI = TPQ

        def L(a, w=1):
            return bass.AP(af, TB + a, [[AF32W, 128], [24, NTI], [1, w]])

        def Lb(a, w):
            return bass.AP(af, TB + a, [[AF32W, 128], [24, NTI], [0, w]])

        def vt(fn):
            p.tick("vector", fn)
            p.drain("vector")

        tS0 = p.mark("scalar")
        p.wait("vector", tS0)
        vt(lambda e: e.tensor_tensor(
            out=L(0, 10), in0=L(0, 10),
            in1=bass.AP(af, GBBC, [[AF32W, 128], [0, NTI], [1, 10]]), op=OP.add))
        vt(lambda e: e.tensor_tensor(out=L(10), in0=L(0), in1=L(1), op=OP.max))
        tsub = p.tick("vector", lambda e: e.tensor_tensor(
            out=L(11, 2), in0=L(0, 2), in1=Lb(10, 2), op=OP.subtract))
        p.drain("vector")
        p.wait("scalar", tsub)
        p.drain("scalar")
        t = p.tick("scalar", lambda e: e.activation(out=L(11, 2), in_=L(11, 2),
                                                    func=AF.Exp))
        p.wait("vector", t)
        vt(lambda e: e.tensor_tensor(out=L(13), in0=L(11), in1=L(12), op=OP.add))
        vt(lambda e: e.reciprocal(out=L(14), in_=L(13)))
        vt(lambda e: e.tensor_tensor(out=L(11, 2), in0=L(11, 2), in1=Lb(14, 2),
                                     op=OP.mult))
        vt(lambda e: e.tensor_tensor(out=L(15), in0=L(11), in1=L(12), op=OP.max))
        vt(lambda e: e.tensor_tensor(out=L(10), in0=L(12), in1=L(11), op=OP.is_gt))
        vt(lambda e: e.tensor_scalar(out=L(16), in0=L(10), scalar1=float(NEGBIG),
                                     scalar2=None, op0=OP.mult))
        vt(lambda e: e.tensor_scalar(out=L(17), in0=L(10), scalar1=-1.0,
                                     scalar2=1.0, op0=OP.mult, op1=OP.add))
        vt(lambda e: e.tensor_scalar(out=L(17), in0=L(17), scalar1=float(NEGBIG),
                                     scalar2=None, op0=OP.mult))
        vt(lambda e: e.tensor_tensor(out=L(2, 4), in0=L(2, 4), in1=Lb(16, 4),
                                     op=OP.add))
        vt(lambda e: e.tensor_tensor(out=L(6, 4), in0=L(6, 4), in1=Lb(17, 4),
                                     op=OP.add))
        vt(lambda e: e.reduce_max(out=L(10), in_=L(2, 8), axis=AX.X))
        tv = p.tick("vector", lambda e: e.tensor_tensor(
            out=L(2, 8), in0=L(2, 8), in1=Lb(10, 8), op=OP.subtract))
        p.wait("scalar", tv)
        p.drain("scalar")
        t = p.tick("scalar", lambda e: e.activation(out=L(2, 8), in_=L(2, 8),
                                                    func=AF.Exp))
        p.wait("vector", t)
        vt(lambda e: e.reduce_sum(out=L(10), in_=L(2, 8), axis=AX.X))
        vt(lambda e: e.reciprocal(out=L(11), in_=L(10)))
        vt(lambda e: e.tensor_tensor(out=L(2, 8), in0=L(2, 8), in1=Lb(11, 8),
                                     op=OP.mult))
        vt(lambda e: e.tensor_tensor(out=L(2, 8), in0=L(2, 8), in1=Lb(15, 8),
                                     op=OP.mult))
        vt(lambda e: e.reduce_max(out=L(10), in_=L(2, 8), axis=AX.X))
        vt(lambda e: e.tensor_tensor(out=L(16, 8), in0=L(2, 8), in1=Lb(10, 8),
                                     op=OP.is_equal))
        vt(lambda e: e.tensor_scalar(out=L(16, 8), in0=L(16, 8), scalar1=-1.0,
                                     scalar2=1.0, op0=OP.mult, op1=OP.add))
        vt(lambda e: e.tensor_tensor(out=L(16, 8), in0=L(16, 8), in1=L(2, 8),
                                     op=OP.mult))
        vt(lambda e: e.reduce_max(out=L(12), in_=L(16, 8), axis=AX.X))
        vt(lambda e: e.tensor_tensor(out=L(16, 8), in0=L(2, 8), in1=Lb(12, 8),
                                     op=OP.is_ge))
        p.tick("vector", lambda e, q=q: e.tensor_tensor(
            out=bass.AP(af, WGT + q * TPQ * 8, [[AF32W, 128], [8, NTI], [1, 8]]),
            in0=L(16, 8), in1=L(2, 8), op=OP.mult))
        p.drain("vector")

    def produce_hidden(nm, in_ap, q, fmh, fml, firstq, fm_guard=None,
                       psum_guard=None, ring_guard=None):
        NW = QTOK // 512 if QTOK >= 512 else 1
        W5 = min(512, QTOK)

        def slot_a(j):
            return BST + (j % 4) * 2048

        def slot_b(j):
            return BSH2 + (j % 4) * 2048

        tok_load = {}

        def load_j(j):
            ta_ = load_slice(nm, in_ap, j, slot_a(j), firstq, "col",
                             stream=f"wsa{j % 4}")
            tb_ = load_slice(nm, in_ap, 16 + j, slot_b(j), firstq, "col",
                             stream=f"wsb{j % 4}")
            tok_load[j] = (ta_, tb_)

        mm_tok = {}
        prev_split = {}
        if ring_guard is not None:
            p.wait("sync", ring_guard)
        load_j(0)
        load_j(1)
        load_j(2)
        load_j(3)
        if psum_guard is not None:
            p.wait("tensor", psum_guard)
        if fm_guard:
            p.wait_max("tensor", fm_guard)
        for j in range(16):
            p.wait_max("tensor", list(tok_load[j]))
            for w in range(NW):
                if (j + w) % 2 in prev_split:
                    p.wait("tensor", prev_split[(j + w) % 2])
                banka = ps[(j + w) % 2]
                bankb = ps[2 + (j + w) % 2]
                for kc in range(8):
                    mm3(banka[:, 0:W5],
                        bap(slot_a(j) + kc * 128, [[1, 128]]),
                        bap(slot_a(j) + 1024 + kc * 128, [[1, 128]]),
                        fmh(kc, w * W5, W5), fml(kc, w * W5, W5),
                        start=(kc == 0), stop=(kc == 7))
                tmm = None
                for kc in range(8):
                    tmm = mm3(bankb[:, 0:W5],
                        bap(slot_b(j) + kc * 128, [[1, 128]]),
                        bap(slot_b(j) + 1024 + kc * 128, [[1, 128]]),
                        fmh(kc, w * W5, W5), fml(kc, w * W5, W5),
                        start=(kc == 0), stop=(kc == 7), inc=(kc == 7))
                if w == NW - 1:
                    mm_tok[j] = tmm
                    nxt = j + 4
                    if nxt < 16:
                        p.wait("sync", tmm)
                        if firstq:
                            p.wait("vector", tmm)
                        load_j(nxt)
                par = (j + w) % 2
                tbo = TB + par * W5 if W5 <= 512 else TB
                if par in prev_split:
                    p.wait("scalar", prev_split[par])
                p.wait("scalar", tmm)
                tsi = p.tick("scalar", lambda e, ba=banka, o=tbo: e.activation(
                    out=fap(o, [[1, W5]]), in_=ba[:, 0:W5], func=AF.Silu))
                p.wait("vector", tsi)
                p.tick("vector", lambda e, bb=bankb, o=tbo: e.tensor_tensor(
                    out=fap(o, [[1, W5]]), in0=fap(o, [[1, W5]]),
                    in1=bb[:, 0:W5], op=OP.mult))
                p.drain("vector")
                tv = split_pair("vector",
                                bap(BSH1 + j * QTOK + w * W5, [[1, W5]]),
                                bap(BSH1 + 8192 + j * QTOK + w * W5, [[1, W5]]),
                                fap(tbo, [[1, W5]]))
                prev_split[par] = tv
        return mm_tok[15], dict(prev_split)
    def wso_ring(firstq):
        tok_load = {}

        def load_j(j):
            tok_load[j] = load_slice("wso", wso_in, j,
                                     BSH2 + 8192 + (j % 4) * 2048,
                                     firstq, "row", stream=f"wso{j % 4}")

        return tok_load, load_j

    def consume_hidden_shared(q, firstq, ring):
        tok_load, load_j = ring
        tmm = None
        for j in range(16):
            p.wait("tensor", tok_load[j])
            for ti in range(TPQ):
                for half in range(2):
                    bank = ps[ti * 2 + half]
                    last = (ti == TPQ - 1 and half == 1)
                    tmm = mm3(bank[:, 0:512],
                        bap(BSH1 + j * QTOK + ti * 128, [[1, 128]]),
                        bap(BSH1 + 8192 + j * QTOK + ti * 128, [[1, 128]]),
                        bap(BSH2 + 8192 + (j % 4) * 2048 + half * 512,
                            [[1, 512]]),
                        bap(BSH2 + 8192 + (j % 4) * 2048 + 1024 + half * 512,
                            [[1, 512]]),
                        start=(j == 0), stop=(j == 15), inc=last)
            nxt = j + 4
            if nxt < 16:
                p.wait("sync", tmm)
                if firstq:
                    p.wait("vector", tmm)
                load_j(nxt)
        p.wait("vector", tmm)
        p.drain("vector")
        for ti in range(TPQ):
            for half in range(2):
                bank = ps[ti * 2 + half]
                tv = p.tick("vector", lambda e, b=bank, ti=ti, hf=half:
                            e.tensor_copy(out=accq(ti, 512, hf * 512),
                                          in_=b[:, 0:512]))
                p.wait("tensor", tv)

    def experts_quarter(q, firstq, pre_splits=None, ring_guard=None):
        tok_load = {}

        NSLOT = 4

        def load_L(L):
            e_i, jj = divmod(L, 16)
            tok_load[L] = load_slice("w2", None, (e_i, jj),
                                     BST + (L % NSLOT) * 2048, firstq, "w2e",
                                     stream=f"w2s{L % NSLOT}")

        total = NEXP * 16
        if ring_guard is not None:
            p.wait("sync", ring_guard)
        for _l in range(NSLOT):
            load_L(_l)
        if pre_splits is not None:
            for _t in pre_splits.values():
                p.wait("tensor", _t)
        prev_evac = {}
        last_evac = [None]
        for e_i in range(NEXP):
            for j in range(16):
                L = e_i * 16 + j
                slot = BST + (L % NSLOT) * 2048
                p.wait("tensor", tok_load[L])
                for ti in range(TPQ):
                    for half in range(2):
                        bank = ps[ti * 2 + half]
                        if j == 0 and (ti * 2 + half) in prev_evac:
                            p.wait("tensor", prev_evac[ti * 2 + half])
                        last = (ti == TPQ - 1 and half == 1)
                        tmm = mm3(bank[:, 0:512],
                            bap(BSH1 + j * QTOK + ti * 128, [[1, 128]]),
                            bap(BSH1 + 8192 + j * QTOK + ti * 128, [[1, 128]]),
                            bap(slot + half * 512, [[1, 512]]),
                            bap(slot + 1024 + half * 512, [[1, 512]]),
                            start=(j == 0), stop=(j == 15), inc=last)
                nxt = L + NSLOT
                if nxt < total:
                    p.wait("sync", tmm)
                    if firstq:
                        p.wait("vector", tmm)
                    load_L(nxt)
            p.wait("vector", tmm)
            for ti in range(TPQ):
                tt = q * TPQ + ti
                for half in range(2):
                    bank = ps[ti * 2 + half]
                    tv = p.tick("vector", lambda e, b=bank, ti=ti, hf=half, ei=e_i,
                                tt=tt: e.scalar_tensor_tensor(
                        out=accq(ti, 512, hf * 512), in0=b[:, 0:512],
                        scalar=fap(WGT + tt * 8 + ei, [[1, 1]]),
                        in1=accq(ti, 512, hf * 512), op0=OP.mult, op1=OP.add))
                    p.drain("vector")
                    prev_evac[ti * 2 + half] = tv
                    last_evac[0] = tv
        return last_evac[0]

    def attn_slot():
        qguard = {}
        for q in range(NQ):
            prev_ts = [None]
            tsv = {}
            XBS = (TA, TC, SS, SS + 1024)
            SQS = (TB, SS + 2080, SS + 3104, SS + 4224)
            CVS = (COLV, COLV + 8, COLV + 16, COLV + 24)

            def c_(i, a, b):
                return fap(CVS[i] + a, [[1, b - a]])

            def xv(i, n=D):
                return fap(XBS[i], [[1, n]])

            def sv(i, n=D):
                return fap(SQS[i], [[1, n]])

            ltok = {}
            for i in range(TPQ):
                tt = q * TPQ + i
                if qguard.get(i) is not None:
                    p.wait("sync", qguard[i])
                ltok[i] = p.dma("sync", xv(i), sp_x[ts(tt, 128), :],
                                stream=f"ldx{i}")
            p.drain("vector")
            p.drain("scalar")
            for i in range(TPQ):
                p.wait("vector", ltok[i])
                p.wait("scalar", ltok[i])
                p.tick("vector", lambda e, i=i: e.reduce_sum(
                    out=c_(i, 0, 1), in_=xv(i), axis=AX.X))
                p.tick("scalar", lambda e, i=i: e.activation(
                    out=sv(i), in_=xv(i), func=AF.Square, accum_out=c_(i, 1, 2)))
            tS = p.mark("scalar")
            p.wait("vector", tS)
            p.drain("vector")
            for i in range(TPQ):
                p.tick("vector", lambda e, i=i: e.tensor_scalar(
                    out=c_(i, 2, 3), in0=c_(i, 0, 1), scalar1=1.0 / D,
                    scalar2=None, op0=OP.mult))
            p.drain("vector")
            for i in range(TPQ):
                p.tick("vector", lambda e, i=i: e.tensor_tensor(
                    out=c_(i, 3, 4), in0=c_(i, 2, 3), in1=c_(i, 2, 3), op=OP.mult))
            p.drain("vector")
            tV = None
            for i in range(TPQ):
                tV = p.tick("vector", lambda e, i=i: e.tensor_scalar(
                    out=c_(i, 4, 5), in0=c_(i, 1, 2), scalar1=1.0 / D,
                    scalar2=c_(i, 3, 4), op0=OP.mult, op1=OP.subtract))
            p.wait("scalar", tV)
            p.drain("scalar")
            tS = None
            for i in range(TPQ):
                tS = p.tick("scalar", lambda e, i=i: e.activation(
                    out=c_(i, 5, 6), in_=c_(i, 4, 5), func=AF.Sqrt,
                    bias=float(EPS_LN)))
            p.wait("vector", tS)
            p.drain("vector")
            for i in range(TPQ):
                p.tick("vector", lambda e, i=i: e.reciprocal(
                    out=c_(i, 6, 7), in_=c_(i, 5, 6)))
            p.drain("vector")
            tVA = None
            for i in range(TPQ):
                tVA = p.tick("vector", lambda e, i=i: e.tensor_scalar(
                    out=xv(i), in0=xv(i), scalar1=c_(i, 2, 3), scalar2=c_(i, 6, 7),
                    op0=OP.subtract, op1=OP.mult))
            p.wait("scalar", tVA)
            p.drain("scalar")
            gtok = {}
            sgp = {}
            for i in range(TPQ):
                gtok[i] = p.tick("scalar", lambda e, i=i: e.activation(
                    out=xv(i), in_=xv(i), func=AF.Gelu_apprx_tanh))
            for i in range(TPQ):
                tt = q * TPQ + i
                p.wait("gpsimd", gtok[i])
                sgp[i] = p.dma("gpsimd", sp_g[ts(tt, 128), :], xv(i),
                               stream=f"spg{i}")
            p.drain("scalar")
            tS = None
            for i in range(TPQ):
                tS = p.tick("scalar", lambda e, i=i: e.activation(
                    out=sv(i), in_=xv(i), func=AF.Square, accum_out=c_(i, 1, 2)))
            p.drain("scalar")
            tS2 = None
            for i in range(TPQ):
                tS2 = p.tick("scalar", lambda e, i=i: e.activation(
                    out=c_(i, 5, 6), in_=c_(i, 1, 2), func=AF.Sqrt))
            p.wait("vector", tS2)
            p.drain("vector")
            for i in range(TPQ):
                p.tick("vector", lambda e, i=i: e.tensor_scalar(
                    out=c_(i, 5, 6), in0=c_(i, 5, 6), scalar1=1e-12, scalar2=None,
                    op0=OP.max))
            p.drain("vector")
            for i in range(TPQ):
                tt = q * TPQ + i
                p.tick("vector", lambda e, i=i, tt=tt: e.reciprocal(
                    out=fap(INVN + tt, [[1, 1]]), in_=c_(i, 5, 6)))
            p.drain("vector")
            for i in range(TPQ):
                p.tick("vector", lambda e, i=i: e.tensor_tensor(
                    out=sv(i), in0=xv(i), in1=ema_bc, op=OP.mult))
            p.drain("vector")
            for i in range(TPQ):
                p.tick("vector", lambda e, i=i: e.reduce_sum(
                    out=c_(i, 7, 8), in_=sv(i), axis=AX.X))
            p.drain("vector")
            for i in range(TPQ):
                tt = q * TPQ + i
                p.tick("vector", lambda e, i=i, tt=tt: e.tensor_tensor(
                    out=fap(EMAS + tt, [[1, 1]]), in0=c_(i, 7, 8),
                    in1=fap(INVN + tt, [[1, 1]]), op=OP.mult))
            p.drain("vector")
            for i in range(TPQ):
                tt = q * TPQ + i
                p.wait("vector", sgp[i])
                t5 = p.tick("vector", lambda e, i=i, tt=tt: e.tensor_scalar(
                    out=xv(i), in0=xv(i), scalar1=fap(INVN + tt, [[1, 1]]),
                    scalar2=None, op0=OP.mult))
                p.drain("vector")
                p.wait("tensor", t5)
                prev_ts[0] = transpose_split(
                    fap(XBS[i], [[1, 1024]]), 8,
                    lambda c, ti=i: (
                        bap(BFM + c * 2 * QTOK + ti * 128, [[1, 128]]),
                        bap(BFM + c * 2 * QTOK + QTOK + ti * 128, [[1, 128]])),
                    guard=prev_ts[0])
                tsv[i] = prev_ts[0]
                qguard[i] = prev_ts[0]
            p.wait("gpsimd", tsv[TPQ - 1])
            for c in range(8):
                p.dma("gpsimd", sp_fm["h"][c, :, ts(q, QTOK)],
                      bap(BFM + c * 2 * QTOK, [[1, QTOK]]))
                p.dma("gpsimd", sp_fm["l"][c, :, ts(q, QTOK)],
                      bap(BFM + c * 2 * QTOK + QTOK, [[1, QTOK]]))
            if q == 0:
                for tt in range(NT):
                    p.tick("vector", lambda e, tt=tt: e.memset(
                        fap(SEQS + tt, [[1, 1]]), -2.0))
                p.drain("vector")
                tvz = p.mark("vector")
                p.wait("tensor", tvz)
            last_blk = [None]
            rhs_tok = {}

            def load_rhs(sh):
                par = sh % 2
                base = BSH1 + par * 8192
                trr = []
                for c in range(8):
                    trr.append(p.dma("sync", bap(base + c * 1024, [[1, QW]]),
                                     sp_fm["h"][c, :, ts(sh, QW)],
                                     stream=f"simr{par}"))
                    trr.append(p.dma("sync", bap(base + c * 1024 + 512, [[1, QW]]),
                                     sp_fm["l"][c, :, ts(sh, QW)],
                                     stream=f"simr{par}"))
                rhs_tok[sh] = trr

            if q > 0:
                load_rhs(0)
            for sh in range(q + 1):
                diag = (sh == q)
                RHS = BSH1 + (sh % 2) * 8192
                if not diag:
                    p.wait_max("tensor", rhs_tok[sh])
                    if sh + 1 < q:
                        if last_blk[0] is not None:
                            p.wait("sync", last_blk[0])
                        load_rhs(sh + 1)
                for ti in range(TPQ):
                    tt = q * TPQ + ti
                    bank = ps[tt % 2]
                    tokmm = None
                    for kc in range(8):
                        if diag:
                            rh = bap(BFM + kc * 2 * QTOK, [[1, QTOK]])
                            rl = bap(BFM + kc * 2 * QTOK + QTOK, [[1, QTOK]])
                        else:
                            rh = bap(RHS + kc * 1024, [[1, QW]])
                            rl = bap(RHS + kc * 1024 + 512, [[1, QW]])
                        tokmm = mm3(bank[:, 0:QW],
                                    bap(BFM + kc * 2 * QTOK + ti * 128, [[1, 128]]),
                                    bap(BFM + kc * 2 * QTOK + QTOK + ti * 128,
                                        [[1, 128]]),
                                    rh, rl,
                                    start=(kc == 0), stop=(kc == 7), inc=(kc == 7))
                    tmm = tokmm
                    last_blk[0] = tmm
                    p.wait("vector", tmm)
                    off = tt * 128 - sh * QW
                    if off >= QW:
                        p.tick("vector", lambda e, b=bank: e.reduce_max(
                            out=colv(8, 9), in_=b[:, 0:QW], axis=AX.X))
                    else:
                        mi = off // 128
                        p.tick("vector", lambda e, b=bank, mi=mi: e.tensor_tensor(
                            out=fap(SS + 5248, [[1, QW]]), in0=b[:, 0:QW],
                            in1=mask_ap(mi), op=OP.add))
                        p.drain("vector")
                        p.tick("vector", lambda e: e.reduce_max(
                            out=colv(8, 9), in_=fap(SS + 5248, [[1, QW]]),
                            axis=AX.X))
                    p.drain("vector")
                    p.tick("vector", lambda e, tt=tt: e.tensor_tensor(
                        out=fap(SEQS + tt, [[1, 1]]), in0=fap(SEQS + tt, [[1, 1]]),
                        in1=colv(8, 9), op=OP.max))
                    p.drain("vector")
                    tvv = p.mark("vector")
                    p.wait("tensor", tvv)
            p.barrier(f"sim_q{q}")
        p.tick("vector", lambda e: e.tensor_copy(
            out=bass.AP(af, SEQS, [[AF32W, 1], [1, 1]]),
            in_=bass.AP(af, EMAS, [[AF32W, 1], [1, 1]])))
        p.drain("vector")
        p.barrier("sim_done")

        p.barrier("as_done")

    if stages >= 3:
        ffn_call(0, x_in, sp_x)
        attn_slot()
        ffn_call(1, sp_x, out_t, gated=True)
    elif stages == 2:
        ffn_call(0, x_in, sp_x)
        attn_slot()
        for tt in range(NT):
            tok = p.dma("sync", tA(D), sp_x[ts(tt, 128), :], stream="ldx")
            p.wait("gpsimd", tok)
            p.dma("gpsimd", out_t[ts(tt, 128), :], tA(D))
            p.barrier(f"st_{tt}")
    else:
        ffn_call(0, x_in, out_t)
    p.barrier("end", light=False)

    with nc.Block() as block:
        p.finalize(block)
    es.close()
    return nc


def _consts(T):
    NT = T // 128
    QW = min(512, T)
    iden = np.eye(128, dtype=np.float32)
    inv_freq = (1.0 / 10000.0 ** (np.arange(0, DH, 2, dtype=np.float32) / DH)
                ).astype(np.float32)
    t_idx = np.arange(T, dtype=np.float32)
    ang = (t_idx[:, None] * inv_freq[None, :]).astype(np.float32)
    sin, cos = np.sin(ang).astype(np.float32), np.cos(ang).astype(np.float32)
    trig = np.zeros((128, 2 * NT * 32), dtype=np.float32)
    for tt in range(NT):
        trig[:, tt * 64:tt * 64 + 32] = cos[tt * 128:(tt + 1) * 128]
        trig[:, tt * 64 + 32:tt * 64 + 64] = sin[tt * 128:(tt + 1) * 128]
    masks = np.zeros((128, 4 * QW), dtype=np.float32)
    for mi in range(4):
        off = mi * 128
        pp = np.arange(128)[:, None]
        cc = np.arange(QW)[None, :]
        masks[:, mi * QW:(mi + 1) * QW] = np.where(cc < off + pp, 0.0, -1e9)
    return iden, trig, masks


_NC_CACHE = {}


def _get_nc(T):
    if T not in _NC_CACHE:
        _NC_CACHE[T] = build_program(T)
    return _NC_CACHE[T]


def kernel(L0_attn_x, L0_attn_ema_out, L0_attn_logit_decay, L0_attn_log_tau,
           L0_attn_log_blend, L0_attn_logit_blend_seq, L0_ffn_Wq, L0_ffn_Wk_c,
           L0_ffn_Wv_c, L0_ffn_Wk, L0_ffn_Wv, L0_ffn_Wo, L0_ffn_W_shared_in,
           L0_ffn_W_shared_out, L0_ffn_W1_shared, L0_ffn_W2_experts,
           L0_ffn_Wg_gate, L0_ffn_We_gate, L0_ffn_group_bias,
           L0_ffn_expert_bias, L0_ffn_norm1_w, L0_ffn_norm2_w):
    T = L0_attn_x.shape[1]
    nc = _get_nc(T)
    iden, trig, masks = _consts(T)
    scal = np.zeros((1, 4), dtype=np.float32)
    scal[0, 0] = np.float32(L0_attn_log_tau)
    scal[0, 1] = np.float32(L0_attn_log_blend)
    scal[0, 2] = np.float32(L0_attn_logit_blend_seq)
    wgate = np.zeros((D, 16), dtype=np.float32)
    wgate[:, 0:2] = np.asarray(L0_ffn_Wg_gate, dtype=np.float32)
    wgate[:, 2:10] = np.asarray(L0_ffn_We_gate, dtype=np.float32)
    gbias = np.zeros((1, 16), dtype=np.float32)
    gbias[0, 0:2] = np.asarray(L0_ffn_group_bias, dtype=np.float32)
    gbias[0, 2:10] = np.asarray(L0_ffn_expert_bias, dtype=np.float32)
    base = {
        "ema_in": np.asarray(L0_attn_ema_out, dtype=np.float32).reshape(1, D),
        "scal_in": scal,
        "wq_in": np.ascontiguousarray(L0_ffn_Wq, dtype=np.float32),
        "wkc_in": np.ascontiguousarray(L0_ffn_Wk_c, dtype=np.float32),
        "wvc_in": np.ascontiguousarray(L0_ffn_Wv_c, dtype=np.float32),
        "wk_in": np.ascontiguousarray(L0_ffn_Wk, dtype=np.float32),
        "wv_in": np.ascontiguousarray(L0_ffn_Wv, dtype=np.float32),
        "wo_in": np.ascontiguousarray(L0_ffn_Wo, dtype=np.float32),
        "wsi_in": np.ascontiguousarray(L0_ffn_W_shared_in, dtype=np.float32),
        "wso_in": np.ascontiguousarray(L0_ffn_W_shared_out, dtype=np.float32),
        "w1_in": np.ascontiguousarray(L0_ffn_W1_shared, dtype=np.float32),
        "w2_in": np.ascontiguousarray(L0_ffn_W2_experts, dtype=np.float32),
        "wgate_in": wgate,
        "gbias_in": gbias,
        "n1w_in": np.asarray(L0_ffn_norm1_w, dtype=np.float32).reshape(1, D),
        "n2w_in": np.asarray(L0_ffn_norm2_w, dtype=np.float32).reshape(1, D),
        "iden_in": iden,
        "trig_in": trig,
        "mask_in": masks,
    }
    xs = np.ascontiguousarray(L0_attn_x, dtype=np.float32)
    in_maps = [dict(base, x_in=xs[b]) for b in range(xs.shape[0])]
    res = run_bass_kernel_spmd(nc, in_maps, core_ids=list(range(len(in_maps))))
    out = np.stack([r["out_t"] for r in res.results], axis=0).astype(np.float32)
    return out


# version 25
# speedup vs baseline: 1.0199x; 1.0199x over previous
from collections import defaultdict
from contextlib import ExitStack

import numpy as np

import concourse.bass as bass
import concourse.mybir as mybir
from concourse.alu_op_type import AluOpType
from concourse.bass_utils import run_bass_kernel_spmd

F32 = mybir.dt.float32
BF16 = mybir.dt.bfloat16
AF = mybir.ActivationFunctionType
AX = mybir.AxisListType
OP = AluOpType

D = 1024
H = 16
DH = 64
DC = 256
HID = 2048
NEXP = 8
EPS_RMS = 1e-6
EPS_LN = 1e-5
NEGBIG = -1e30

ENGINES = ("sync", "tensor", "vector", "scalar", "gpsimd")


class Prog:

    def __init__(self, nc):
        self.nc = nc
        self.ops = {e: [] for e in ENGINES}
        self.sem = {}
        self.cnt = defaultdict(int)
        self.waited = {e: defaultdict(int) for e in ENGINES}

    def make_sems(self, es):
        self._es = es
        for name in ("bar_sync", "bar_tensor", "bar_vector", "bar_scalar",
                     "bar_gpsimd"):
            self.sem[name] = es.enter_context(self.nc.semaphore(name))
        self.dma_sems = {}

    def dma_sem(self, name, q):
        if name not in self.sem:
            self.sem[name] = self._es.enter_context(self.nc.semaphore(name))
            self.dma_sems[name] = q
        assert self.dma_sems[name] == q, f"dma sem {name} used from two queues"
        return name

    def emit(self, eng, fn, inc=None, inc_amt=1):
        if inc is None:
            self.ops[eng].append(fn)
            return None
        self.cnt[inc] += inc_amt
        val = self.cnt[inc]

        def run(e, fn=fn, s=inc, a=inc_amt):
            fn(e).then_inc(self.sem[s], a)

        self.ops[eng].append(run)
        return (inc, val)

    def wait_max(self, eng, tokens):
        best = {}
        for t in tokens:
            if t is None:
                continue
            s, v = t
            best[s] = max(best.get(s, 0), v)
        for s, v in best.items():
            self.wait(eng, (s, v))

    def wait(self, eng, token):
        if token is None:
            return
        s, v = token
        if self.waited[eng][s] >= v:
            return
        self.waited[eng][s] = v
        self.ops[eng].append(lambda e, s=s, v=v: e.wait_ge(self.sem[s], v))

    def drain(self, eng):
        self.ops[eng].append(lambda e: e.drain())

    def dma(self, q, out, in_, stream=None):
        name = self.dma_sem("dq_" + (stream or q), q)
        return self.emit(q, lambda e, o=out, i=in_: e.dma_start(out=o, in_=i),
                         inc=name, inc_amt=16)

    def tick(self, eng, fn):
        return self.emit(eng, fn, inc="bar_" + eng)

    def mark(self, eng):
        self.drain(eng)
        return self.tick(eng, lambda e: e.nop())

    BAR_SKIP = ("dq_wsa", "dq_wsb", "dq_w2s", "dq_wso", "dq_wq", "dq_fmld",
                "dq_simr", "dq_wsl", "dq_wc_")

    def barrier(self, note="", light=True):
        for name, q in self.dma_sems.items():
            if light and name.startswith(self.BAR_SKIP):
                continue
            self.wait(q, (name, self.cnt[name]))
        toks = {}
        for e in ENGINES:
            self.drain(e)
            toks[e] = self.emit(e, lambda x: x.nop(), inc="bar_" + e)
        for e in ENGINES:
            for o in ENGINES:
                if o != e:
                    self.wait(e, toks[o])

    def finalize(self, block):
        for e in ENGINES:
            lst = self.ops[e]
            body = getattr(block, e)

            def make(lst):
                def _(eng):
                    for fn in lst:
                        fn(eng)
                return _
            body(make(lst))


def ts(i, w):
    return slice(i * w, (i + 1) * w)


def build_program(T=2048, stages=3):
    NT = T // 128
    QW = min(512, T)
    NQT = T // QW
    NKT = T // 128
    NQ = max(1, T // 512)
    QTOK = T // NQ
    TPQ = QTOK // 128
    NHP = H // 2

    nc = bass.Bass("TRN2", target_bir_lowering=False)

    x_in = nc.dram_tensor("x_in", [T, D], F32, kind="ExternalInput").ap()
    ema_in = nc.dram_tensor("ema_in", [1, D], F32, kind="ExternalInput").ap()
    scal_in = nc.dram_tensor("scal_in", [1, 4], F32, kind="ExternalInput").ap()
    wq_in = nc.dram_tensor("wq_in", [D, D], F32, kind="ExternalInput").ap()
    wkc_in = nc.dram_tensor("wkc_in", [D, DC], F32, kind="ExternalInput").ap()
    wvc_in = nc.dram_tensor("wvc_in", [D, DC], F32, kind="ExternalInput").ap()
    wk_in = nc.dram_tensor("wk_in", [DC, D], F32, kind="ExternalInput").ap()
    wv_in = nc.dram_tensor("wv_in", [DC, D], F32, kind="ExternalInput").ap()
    wo_in = nc.dram_tensor("wo_in", [D, D], F32, kind="ExternalInput").ap()
    wsi_in = nc.dram_tensor("wsi_in", [D, 2 * HID], F32, kind="ExternalInput").ap()
    wso_in = nc.dram_tensor("wso_in", [HID, D], F32, kind="ExternalInput").ap()
    w1_in = nc.dram_tensor("w1_in", [D, 2 * HID], F32, kind="ExternalInput").ap()
    w2_in = nc.dram_tensor("w2_in", [NEXP, HID, D], F32, kind="ExternalInput").ap()
    wgate_in = nc.dram_tensor("wgate_in", [D, 16], F32, kind="ExternalInput").ap()
    gbias_in = nc.dram_tensor("gbias_in", [1, 16], F32, kind="ExternalInput").ap()
    n1w_in = nc.dram_tensor("n1w_in", [1, D], F32, kind="ExternalInput").ap()
    n2w_in = nc.dram_tensor("n2w_in", [1, D], F32, kind="ExternalInput").ap()
    iden_in = nc.dram_tensor("iden_in", [128, 128], F32, kind="ExternalInput").ap()
    trig_in = nc.dram_tensor("trig_in", [128, 2 * NT * 32], F32,
                             kind="ExternalInput").ap()
    mask_in = nc.dram_tensor("mask_in", [128, 4 * QW], F32, kind="ExternalInput").ap()
    out_t = nc.dram_tensor("out_t", [T, D], F32, kind="ExternalOutput").ap()

    sp_x = nc.dram_tensor("sp_x", [T, D], F32).ap()
    sp_y = nc.dram_tensor("sp_y", [T, D], F32).ap()
    sp_flat = nc.dram_tensor("sp_flat", [T, D], F32).ap()
    sp_g = nc.dram_tensor("sp_g", [T, D], F32).ap()
    sp_q = {s: nc.dram_tensor(f"sp_q_{s}", [NHP, 128, T], BF16).ap() for s in "hl"}
    sp_k = {s: nc.dram_tensor(f"sp_k_{s}", [NHP, 128, T], BF16).ap() for s in "hl"}
    sp_sa = {s: nc.dram_tensor(f"sp_sa_{s}", [NHP, 128, T], BF16).ap() for s in "hl"}
    sp_v = nc.dram_tensor("sp_v", [NHP, T, 130], F32).ap()
    sp_fm = {s: nc.dram_tensor(f"sp_fm_{s}", [8, 128, T], BF16).ap() for s in "hl"}
    bounce = nc.dram_tensor("bounce", [1, 4096], F32).ap()
    wc = {}
    for nm, w in (("wq", 8 * D), ("wkc", 2 * D), ("wvc", 2 * D), ("wk", 2 * D),
                  ("wv", 2 * D), ("wo", 8 * D), ("wgate", 128)):
        wc[nm] = {s: nc.dram_tensor(f"wc_{nm}_{s}", [128, w], BF16).ap() for s in "hl"}
    wcm = {"wq": nc.dram_tensor("wcm_wq", [8, 128, 2048], BF16).ap(),
           "wsi": nc.dram_tensor("wcm_wsi", [32, 128, 2048], BF16).ap(),
           "w1": nc.dram_tensor("wcm_w1", [32, 128, 2048], BF16).ap(),
           "wso": nc.dram_tensor("wcm_wso", [16, 128, 2048], BF16).ap(),
           "w2": nc.dram_tensor("wcm_w2", [NEXP * 16, 128, 2048], BF16).ap()}

    es = ExitStack()
    p = Prog(nc)
    p.make_sems(es)

    ACCQ = 0
    TA = ACCQ + 4096
    TB = TA + 1024
    TC = TB + 1024
    SS = TC + 1024
    TRIG = SS + 8448
    MASK = TRIG + 2 * NT * 32
    BCN = MASK + 4 * QW
    MISC = BCN + 3072
    AF32W = MISC + 1792
    af = es.enter_context(nc.sbuf_tensor("af", [128, AF32W], F32))

    BFM = 0
    BSH1 = 16384
    BSH2 = 32768
    BST = 49152
    ABFW = BST + 8192
    ab = es.enter_context(nc.sbuf_tensor("ab", [128, ABFW], BF16))

    ps = [es.enter_context(nc.psum_tensor(f"ps{i}", [128, 512], F32))
          for i in range(8)]

    def fap(off, pattern):
        return bass.AP(af, off, [[AF32W, 128]] + pattern)

    def bap(off, pattern, npart=128):
        return bass.AP(ab, off, [[ABFW, npart]] + pattern)

    def tA(n=1024):
        return fap(TA, [[1, n]])

    def tB(n=1024):
        return fap(TB, [[1, n]])

    def tC(n=1024):
        return fap(TC, [[1, n]])

    def accq(ti, n=1024, off=0):
        return fap(ACCQ + ti * 1024 + off, [[1, n]])

    trig_ap = fap(TRIG, [[1, 2 * NT * 32]])
    mask_full = fap(MASK, [[1, 4 * QW]])

    def mask_ap(mi):
        return fap(MASK + mi * QW, [[1, QW]])

    n1w_bc = fap(BCN, [[1, D]])
    n2w_bc = fap(BCN + 1024, [[1, D]])
    ema_bc = fap(BCN + 2048, [[1, D]])
    COLV = MISC
    WGT = MISC + 32
    SEQS = MISC + 160
    EMAS = MISC + 176
    INVN = MISC + 192
    SCBC = MISC + 208
    GBBC = MISC + 216
    ROWS = MISC + 232

    def colv(a, b):
        return fap(COLV + a, [[1, b - a]])

    def row(off, n, parts=1):
        return bass.AP(af, ROWS + off, [[AF32W, parts], [1, n]])

    CPOOL = ROWS + 1200
    _cvals = [0.0, float(EPS_LN), float(EPS_RMS)]
    for _i, _v in enumerate(_cvals):
        _ap = fap(CPOOL + _i, [[1, 1]])
        p.emit("vector", lambda e, a=_ap, v=_v: e.memset(a, v))
        nc.const_aps.register(F32, _v, _ap)
    p.drain("vector")
    _tcp = p.mark("vector")
    for _e in ("scalar", "tensor", "gpsimd", "sync"):
        p.wait(_e, _tcp)

    def split_pair(eng, dst_h, dst_l, src_ap, hi_scalar=False):
        if hi_scalar:
            th = p.tick("scalar", lambda e, a=dst_h, b=src_ap:
                        e.activation(out=a, in_=b, func=AF.Copy))
            p.wait(eng, th)
            return p.tick(eng, lambda e, a=dst_l, b=src_ap, c=dst_h:
                          e.tensor_tensor(out=a, in0=b, in1=c, op=OP.subtract))
        p.tick(eng, lambda e, a=dst_h, b=src_ap: e.tensor_copy(out=a, in_=b))
        p.drain(eng)
        return p.tick(eng, lambda e, a=dst_l, b=src_ap, c=dst_h:
                      e.tensor_tensor(out=a, in0=b, in1=c, op=OP.subtract))

    def mm3(psum_ap, lh, ll, rh, rl, start, stop, tile_position=None, inc=False):
        p.emit("tensor", lambda e, o=psum_ap, a=lh, b=rh, tp=tile_position:
               e.matmul(out=o, lhsT=a, rhs=b, start=start, stop=False,
                        tile_position=tp))
        p.emit("tensor", lambda e, o=psum_ap, a=lh, b=rl, tp=tile_position:
               e.matmul(out=o, lhsT=a, rhs=b, start=False, stop=False,
                        tile_position=tp))
        return p.emit("tensor", lambda e, o=psum_ap, a=ll, b=rh, tp=tile_position:
                      e.matmul(out=o, lhsT=a, rhs=b, start=False, stop=stop,
                               tile_position=tp),
                      inc="bar_tensor" if inc else None)

    def ln_tile(src_ap, dst_ap, sq=None, cv=None):
        sq = tB(D) if sq is None else sq
        cb = COLV if cv is None else cv

        def c(a, b):
            return fap(cb + a, [[1, b - a]])

        p.drain("vector")
        p.drain("scalar")
        t1 = p.tick("vector", lambda e: e.reduce_sum(out=c(0, 1), in_=src_ap,
                                                     axis=AX.X))
        t2 = p.tick("scalar", lambda e: e.activation(out=sq, in_=src_ap,
                                                     func=AF.Square,
                                                     accum_out=c(1, 2)))
        p.wait("vector", t2)
        p.drain("vector")
        p.tick("vector", lambda e: e.tensor_scalar(out=c(2, 3), in0=c(0, 1),
                                                   scalar1=1.0 / D, scalar2=None,
                                                   op0=OP.mult))
        p.drain("vector")
        p.tick("vector", lambda e: e.tensor_tensor(out=c(3, 4), in0=c(2, 3),
                                                   in1=c(2, 3), op=OP.mult))
        p.drain("vector")
        t3 = p.tick("vector", lambda e: e.tensor_scalar(out=c(4, 5), in0=c(1, 2),
                                                        scalar1=1.0 / D,
                                                        scalar2=c(3, 4),
                                                        op0=OP.mult, op1=OP.subtract))
        p.wait("scalar", t3)
        t4 = p.tick("scalar", lambda e: e.activation(out=c(5, 6), in_=c(4, 5),
                                                     func=AF.Sqrt, bias=float(EPS_LN)))
        p.wait("vector", t4)
        p.tick("vector", lambda e: e.reciprocal(out=c(6, 7), in_=c(5, 6)))
        p.drain("vector")
        t5 = p.tick("vector", lambda e, s=src_ap, d=dst_ap:
                    e.tensor_scalar(out=d, in0=s, scalar1=c(2, 3),
                                    scalar2=c(6, 7),
                                    op0=OP.subtract, op1=OP.mult))
        p.drain("vector")
        return t5

    def rms_tile(src_ap, dst_ap, w_bc, sq=None, cv=None):
        sq = tB(D) if sq is None else sq
        cb = COLV if cv is None else cv

        def c(a, b):
            return fap(cb + a, [[1, b - a]])

        p.drain("scalar")
        t2 = p.tick("scalar", lambda e, s=src_ap: e.activation(
            out=sq, in_=s, func=AF.Square, accum_out=c(1, 2)))
        p.wait("scalar", t2)
        t4 = p.tick("scalar", lambda e: e.activation(
            out=c(5, 6), in_=c(1, 2), func=AF.Sqrt,
            scale=1.0 / D, bias=float(EPS_RMS)))
        p.wait("vector", t4)
        p.tick("vector", lambda e: e.reciprocal(out=c(6, 7), in_=c(5, 6)))
        p.drain("vector")
        p.tick("vector", lambda e, s=src_ap: e.tensor_scalar(
            out=sq, in0=s, scalar1=c(6, 7), scalar2=None, op0=OP.mult))
        p.drain("vector")
        t = p.tick("vector", lambda e, d=dst_ap, w=w_bc, s2=sq: e.tensor_tensor(
            out=d, in0=s2, in1=w, op=OP.mult))
        p.drain("vector")
        return t

    iden_sb = es.enter_context(nc.sbuf_tensor("iden_sb", [128, 128], F32))

    def transpose_split(src_f32, nblk, dst_fn, banks=(4, 5, 6, 7), guard=None):
        nb = len(banks)
        tvs = {}
        last = None
        if guard is not None:
            p.wait("tensor", guard)
        for c in range(nblk):
            bank = ps[banks[c % nb]]
            if c - nb >= 0:
                p.wait("tensor", tvs[c - nb])
            ttok = p.tick("tensor", lambda e, s=src_f32, c=c, b=bank:
                          e.transpose(out=b[:, 0:128],
                                      in_=bass.AP(s.tensor, s.offset + c * 128,
                                                  [[AF32W, 128], [1, 128]]),
                                      identity=iden_sb[:]))
            p.wait("vector", ttok)
            p.wait("scalar", ttok)
            dh, dl = dst_fn(c)
            tv = split_pair("vector", dh, dl, bank[:, 0:128], hi_scalar=True)
            tvs[c] = tv
            last = tv
        p.wait("tensor", last)
        return last

    stage_guard = [None]

    def load_warena(nm, in_ap, base, kchunks, N, first):
        W = kchunks * N
        if first:
            step = max(1, 2048 // N) if N <= 2048 else 1
            prev = stage_guard[0]
            for c0 in range(0, kchunks, step):
                cn = min(step, kchunks - c0)
                stg_ap = fap(SS, [[1, cn * N]])
                stg3 = fap(SS, [[N, cn], [1, N]])
                if prev is not None:
                    p.wait("sync", prev)
                tok = p.dma("sync", stg3,
                            in_ap.rearrange("(c p) n -> p c n", p=128)[:, c0:c0 + cn, :],
                            stream="wst")
                p.wait("vector", tok)
                prev = split_pair("vector",
                                  bap(base + c0 * N, [[1, cn * N]]),
                                  bap(base + W + c0 * N, [[1, cn * N]]),
                                  stg_ap)
            stage_guard[0] = prev
            p.drain("vector")
            tv = p.mark("vector")
            p.wait("gpsimd", tv)
            t1 = p.dma("gpsimd", wc[nm]["h"][:, :], bap(base, [[1, W]]))
            t2 = p.dma("gpsimd", wc[nm]["l"][:, :], bap(base + W, [[1, W]]))
            return (tv, t2)
        th = p.dma("sync", bap(base, [[1, W]]), wc[nm]["h"][:, :])
        tl = p.dma("sync", bap(base + W, [[1, W]]), wc[nm]["l"][:, :])
        return (tl, tl)

    cache_guard = {}

    def load_slice(nm, in_ap, j, slot_base, first, kind, stream="wsl"):
        if kind == "w2e":
            e_i, jj = j
            rec = wcm["w2"][e_i * 16 + jj, :, :]
            srcw = w2_in[e_i, ts(jj, 128), :]
            colmode = False
        elif kind == "row":
            rec = wcm[nm][j, :, :]
            srcw = in_ap[ts(j, 128), :]
            colmode = False
        else:
            rec = wcm[nm][j, :, :]
            srcw = in_ap.rearrange("(c p) n -> p c n", p=128)[:, :, ts(j, 128)]
            colmode = True
        if first:
            stg_ap = fap(SS, [[1, 1024]])
            if stage_guard[0] is not None:
                p.wait("sync", stage_guard[0])
            tok = p.dma("sync", fap(SS, [[128, 8], [1, 128]]) if colmode else stg_ap,
                        srcw, stream="wst")
            p.wait("vector", tok)
            if cache_guard.get(slot_base) is not None:
                p.wait("vector", cache_guard[slot_base])
            p.drain("vector")
            p.tick("vector", lambda e: e.tensor_copy(
                out=bap(slot_base, [[1, 1024]]), in_=stg_ap))
            p.drain("vector")
            tg = p.tick("vector", lambda e: e.tensor_tensor(
                out=bap(slot_base + 1024, [[1, 1024]]), in0=stg_ap,
                in1=bap(slot_base, [[1, 1024]]), op=OP.subtract))
            p.wait("gpsimd", tg)
            tl = p.dma("gpsimd", rec, bap(slot_base, [[1, 2048]]),
                       stream="wc_" + stream)
            stage_guard[0] = tg
            cache_guard[slot_base] = tl
            return tg
        if cache_guard.get(slot_base) is not None:
            p.wait("sync", cache_guard[slot_base])
            cache_guard.pop(slot_base, None)
        return p.dma("sync", bap(slot_base, [[1, 2048]]), rec, stream=stream)

    p.dma("sync", iden_sb[:], iden_in[:, :])
    p.dma("sync", trig_ap, trig_in[:, :])
    p.dma("sync", mask_full, mask_in[:, :])
    tok_e = p.dma("sync", row(0, D), ema_in[:, :], stream="lde")
    tok_s = p.dma("sync", row(1024, 4), scal_in[:, :], stream="lds")
    tok_g = p.dma("sync", row(1056, 16), gbias_in[:, :], stream="ldg")
    tok_n1 = p.dma("sync", bass.AP(af, SS, [[AF32W, 1], [1, D]]), n1w_in[:, :], stream="ldn1")
    tok_n2 = p.dma("sync", bass.AP(af, SS + 1024, [[AF32W, 1], [1, D]]),
                   n2w_in[:, :], stream="ldn2")

    def srow_in(i):
        return bass.AP(af, ROWS + 1024 + i, [[AF32W, 1], [1, 1]])

    def sout(i, n=1):
        return bass.AP(af, ROWS + 1104 + i, [[AF32W, 1], [1, n]])

    p.wait("scalar", tok_s)
    p.tick("scalar", lambda e: e.activation(out=sout(5), in_=srow_in(0), func=AF.Exp))
    p.drain("scalar")
    p.tick("scalar", lambda e: e.activation(out=sout(1), in_=srow_in(1),
                                            func=AF.Sigmoid))
    p.drain("scalar")
    t_sc = p.tick("scalar", lambda e: e.activation(out=sout(3), in_=srow_in(2),
                                                   func=AF.Sigmoid))
    p.wait("vector", t_sc)
    p.tick("vector", lambda e: e.tensor_scalar(out=sout(0), in0=sout(5),
                                               scalar1=-1.0, scalar2=None,
                                               op0=OP.mult))
    p.tick("vector", lambda e: e.tensor_scalar(out=sout(2), in0=sout(1),
                                               scalar1=-1.0, scalar2=1.0,
                                               op0=OP.mult, op1=OP.add))
    t_sc2 = p.tick("vector", lambda e: e.tensor_scalar(out=sout(4), in0=sout(3),
                                                       scalar1=-1.0, scalar2=1.0,
                                                       op0=OP.mult, op1=OP.add))
    p.wait("scalar", tok_e)
    t_e1 = p.tick("scalar", lambda e: e.activation(out=bass.AP(af, SS + 2048,
                                                               [[AF32W, 1], [1, D]]),
                                                   in_=row(0, D), func=AF.Square,
                                                   accum_out=sout(8)))
    p.wait("scalar", t_e1)
    t_e2 = p.tick("scalar", lambda e: e.activation(out=sout(9), in_=sout(8),
                                                   func=AF.Sqrt))
    p.wait("vector", t_e2)
    p.tick("vector", lambda e: e.tensor_scalar(out=sout(10), in0=sout(9),
                                               scalar1=1e-12, scalar2=None,
                                               op0=OP.max))
    p.drain("vector")
    p.tick("vector", lambda e: e.reciprocal(out=sout(11), in_=sout(10)))
    p.drain("vector")
    t_e3 = p.tick("vector", lambda e: e.tensor_scalar(out=row(0, D), in0=row(0, D),
                                                      scalar1=sout(11),
                                                      scalar2=None, op0=OP.mult))
    p.wait("gpsimd", t_e3)
    tb1 = p.dma("gpsimd", bounce[0:1, 0:D], row(0, D))
    p.wait("gpsimd", tok_n1)
    tb2 = p.dma("gpsimd", bounce[0:1, D:2 * D],
                bass.AP(af, SS, [[AF32W, 1], [1, D]]))
    p.wait("gpsimd", tok_n2)
    tb3 = p.dma("gpsimd", bounce[0:1, 2 * D:3 * D],
                bass.AP(af, SS + 1024, [[AF32W, 1], [1, D]]))
    p.wait("gpsimd", t_sc2)
    p.wait("gpsimd", tok_g)
    tb4 = p.dma("gpsimd", bounce[0:1, 3 * D:3 * D + 5], sout(0, 5))
    tb5 = p.dma("gpsimd", bounce[0:1, 3 * D + 16:3 * D + 32], row(1056, 16))
    p.wait("gpsimd", tb5)
    p.dma("gpsimd", ema_bc, bass.AP(bounce.tensor, 0, [[0, 128], [1, D]]))
    p.dma("gpsimd", n1w_bc, bass.AP(bounce.tensor, D, [[0, 128], [1, D]]))
    p.dma("gpsimd", n2w_bc, bass.AP(bounce.tensor, 2 * D, [[0, 128], [1, D]]))
    p.dma("gpsimd", fap(SCBC, [[1, 5]]),
          bass.AP(bounce.tensor, 3 * D, [[0, 128], [1, 5]]))
    p.dma("gpsimd", fap(GBBC, [[1, 16]]),
          bass.AP(bounce.tensor, 3 * D + 16, [[0, 128], [1, 16]]))
    p.barrier("consts", light=False)

    scbc = fap(SCBC, [[1, 5]])

    def scb(i):
        return fap(SCBC + i, [[1, 1]])

    gbbc = fap(GBBC, [[1, 16]])

    def ffn_call(call, src_x, dst_x, gated=False):
        first = (call == 0)

        a1_pre = []
        spv_prev = {}
        qspl_prev = None
        load_warena("wkc", wkc_in, BSH2, 8, DC, first)
        load_warena("wvc", wvc_in, BSH2 + 4096, 8, DC, first)
        load_warena("wk", wk_in, BSH2 + 8192, 2, D, first)
        load_warena("wv", wv_in, BSH2 + 12288, 2, D, first)
        p.barrier(f"aw_{call}")
        for q in range(NQ):
            firstq = first and q == 0
            prev_ts = [None]
            tsv = {}
            XBS = (TA, TC, SS, SS + 1024)
            SQS = (TB, SS + 2080, SS + 3104, SS + 4224)
            CVS = (COLV, COLV + 8, COLV + 16, COLV + 24)
            GT = MISC + 1768

            def c_(i, a, b):
                return fap(CVS[i] + a, [[1, b - a]])

            def xv(i, n=D):
                return fap(XBS[i], [[1, n]])

            def sv(i, n=D):
                return fap(SQS[i], [[1, n]])

            def gt(a, b):
                return fap(GT + a, [[1, b - a]])

            ltok = {}
            gtok = {}
            if a1_pre:
                p.wait_max("sync", a1_pre)
            for i in range(TPQ):
                tt = q * TPQ + i
                if gated:
                    gtok[i] = p.dma("sync", sv(i), sp_g[ts(tt, 128), :],
                                    stream=f"ldg{i}")
                ltok[i] = p.dma("sync", xv(i), src_x[ts(tt, 128), :],
                                stream=f"ldx{i}")
            if gated:
                p.drain("vector")
                p.tick("vector", lambda e, q=q: e.tensor_scalar(
                    out=gt(0, TPQ), in0=fap(EMAS + q * TPQ, [[1, TPQ]]),
                    scalar1=scb(4), scalar2=None, op0=OP.mult))
                p.drain("vector")
                tg = p.tick("vector", lambda e, q=q: e.scalar_tensor_tensor(
                    out=gt(4, 4 + TPQ), in0=fap(SEQS + q * TPQ, [[1, TPQ]]),
                    scalar=scb(3), in1=gt(0, TPQ), op0=OP.mult, op1=OP.add))
                p.wait("scalar", tg)
                p.drain("scalar")
                tg = p.tick("scalar", lambda e: e.activation(
                    out=gt(8, 8 + TPQ), in_=gt(4, 4 + TPQ), func=AF.Exp,
                    scale=scb(0)))
                p.wait("vector", tg)
                p.tick("vector", lambda e: e.scalar_tensor_tensor(
                    out=gt(12, 12 + TPQ), in0=gt(8, 8 + TPQ), scalar=scb(1),
                    in1=bass.AP(af, SCBC + 2, [[AF32W, 128], [0, TPQ]]),
                    op0=OP.mult, op1=OP.add))
                p.drain("vector")
                for i in range(TPQ):
                    p.wait("vector", gtok[i])
                    p.tick("vector", lambda e, i=i: e.tensor_scalar(
                        out=sv(i), in0=sv(i), scalar1=fap(GT + 12 + i, [[1, 1]]),
                        scalar2=None, op0=OP.mult))
                p.drain("vector")
                tres = None
                sp2 = {}
                for i in range(TPQ):
                    tt = q * TPQ + i
                    p.wait("vector", ltok[i])
                    tres = p.tick("vector", lambda e, i=i: e.tensor_tensor(
                        out=xv(i), in0=xv(i), in1=sv(i), op=OP.add))
                    p.drain("vector")
                    p.wait("gpsimd", tres)
                    sp2[i] = p.dma("gpsimd", sp_x[ts(tt, 128), :], xv(i),
                                   stream=f"spx2{i}")
                for i in range(TPQ):
                    p.wait("vector", sp2[i])
                p.wait("scalar", tres)
            p.drain("vector")
            p.drain("scalar")
            for i in range(TPQ):
                p.wait("vector", ltok[i])
                p.wait("scalar", ltok[i])
                p.tick("vector", lambda e, i=i: e.reduce_sum(
                    out=c_(i, 0, 1), in_=xv(i), axis=AX.X))
                p.tick("scalar", lambda e, i=i: e.activation(
                    out=sv(i), in_=xv(i), func=AF.Square, accum_out=c_(i, 1, 2)))
            tS = p.mark("scalar")
            p.wait("vector", tS)
            p.drain("vector")
            for i in range(TPQ):
                p.tick("vector", lambda e, i=i: e.tensor_scalar(
                    out=c_(i, 2, 3), in0=c_(i, 0, 1), scalar1=1.0 / D,
                    scalar2=None, op0=OP.mult))
            p.drain("vector")
            for i in range(TPQ):
                p.tick("vector", lambda e, i=i: e.tensor_tensor(
                    out=c_(i, 3, 4), in0=c_(i, 2, 3), in1=c_(i, 2, 3),
                    op=OP.mult))
            p.drain("vector")
            tV = None
            for i in range(TPQ):
                tV = p.tick("vector", lambda e, i=i: e.tensor_scalar(
                    out=c_(i, 4, 5), in0=c_(i, 1, 2), scalar1=1.0 / D,
                    scalar2=c_(i, 3, 4), op0=OP.mult, op1=OP.subtract))
            p.wait("scalar", tV)
            p.drain("scalar")
            tS = None
            for i in range(TPQ):
                tS = p.tick("scalar", lambda e, i=i: e.activation(
                    out=c_(i, 5, 6), in_=c_(i, 4, 5), func=AF.Sqrt,
                    bias=float(EPS_LN)))
            p.wait("vector", tS)
            p.drain("vector")
            for i in range(TPQ):
                p.tick("vector", lambda e, i=i: e.reciprocal(
                    out=c_(i, 6, 7), in_=c_(i, 5, 6)))
            p.drain("vector")
            sy_tok = {}
            for i in range(TPQ):
                tt = q * TPQ + i
                t5 = p.tick("vector", lambda e, i=i: e.tensor_scalar(
                    out=xv(i), in0=xv(i), scalar1=c_(i, 2, 3), scalar2=c_(i, 6, 7),
                    op0=OP.subtract, op1=OP.mult))
                p.drain("vector")
                p.wait("gpsimd", t5)
                sy_tok[i] = p.dma("gpsimd", sp_y[ts(tt, 128), :], xv(i),
                                  stream=f"spy{i % 2}")
                p.wait("tensor", t5)
                prev_ts[0] = transpose_split(
                    fap(XBS[i], [[1, 1024]]), 8,
                    lambda c, ti=i: (
                        bap(BFM + c * 2 * QTOK + ti * 128, [[1, 128]]),
                        bap(BFM + c * 2 * QTOK + QTOK + ti * 128, [[1, 128]])),
                    guard=prev_ts[0])
                tsv[i] = prev_ts[0]
            p.barrier(f"a1_{call}_{q}")
            def fmh(c, w0, wn):
                return bap(BFM + c * 2 * QTOK + w0, [[1, wn]])

            def fml(c, w0, wn):
                return bap(BFM + c * 2 * QTOK + QTOK + w0, [[1, wn]])

            NW = QTOK // 512 if QTOK >= 512 else 1
            W5 = min(512, QTOK)
            bank_guard = {}
            for (wb, vb) in ((BSH2, BSH1), (BSH2 + 4096, BSH1 + 4 * QTOK)):
                for dc in range(2):
                    for w in range(NW):
                        bi = dc * NW + w
                        bank = ps[bi]
                        if bi in bank_guard:
                            p.wait("tensor", bank_guard[bi])
                        tokmm = None
                        for kc in range(8):
                            tokmm = mm3(bank[:, 0:W5],
                                        bap(wb + kc * DC + ts(dc, 128).start, [[1, 128]]),
                                        bap(wb + 8 * DC + kc * DC + dc * 128, [[1, 128]]),
                                        fmh(kc, w * W5, W5), fml(kc, w * W5, W5),
                                        start=(kc == 0), stop=(kc == 7),
                                        inc=(kc == 7))
                        tmm = tokmm
                        p.wait("vector", tmm)
                        p.wait("scalar", tmm)
                        tv = split_pair("vector",
                                        bap(vb + dc * QTOK + w * W5, [[1, W5]]),
                                        bap(vb + 2 * QTOK + dc * QTOK + w * W5,
                                            [[1, W5]]),
                                        bank[:, 0:W5], hi_scalar=True)
                        bank_guard[bi] = tv

            def kch(dc, w0, wn):
                return bap(BSH1 + dc * QTOK + w0, [[1, wn]])

            def kcl(dc, w0, wn):
                return bap(BSH1 + 2 * QTOK + dc * QTOK + w0, [[1, wn]])

            def vch(dc, w0, wn):
                return bap(BSH1 + 4 * QTOK + dc * QTOK + w0, [[1, wn]])

            def vcl(dc, w0, wn):
                return bap(BSH1 + 6 * QTOK + dc * QTOK + w0, [[1, wn]])

            WVB = BSH2 + 12288
            for _t in spv_prev.values():
                p.wait("vector", _t)
            for bi in range(2):
                p.tick("vector", lambda e, bi=bi: e.memset(
                    bass.AP(af, SS + bi * 1040 + 64, [[AF32W, 128], [65, 16]]),
                    1.0))
            p.drain("vector")
            tvz = p.mark("vector")
            p.wait("tensor", tvz)
            spv_last = {}
            for ti in range(TPQ):
                tt = q * TPQ + ti
                vboff = SS + (ti % 2) * 1040
                if ti - 2 >= 0:
                    p.wait("vector", spv_last[ti - 2])
                for half in range(2):
                    bank = ps[half]
                    if half in bank_guard:
                        p.wait("tensor", bank_guard[half])
                    tokmm = None
                    for dc in range(2):
                        tokmm = mm3(bank[:, 0:512],
                                    vch(dc, ti * 128, 128), vcl(dc, ti * 128, 128),
                                    bap(WVB + dc * D + half * 512, [[1, 512]]),
                                    bap(WVB + 2 * D + dc * D + half * 512, [[1, 512]]),
                                    start=(dc == 0), stop=(dc == 1),
                                    inc=(dc == 1))
                    tmm = tokmm
                    p.wait("vector", tmm)
                    tv = p.tick("vector", lambda e, b=bank, o=vboff, hf=half:
                                e.tensor_copy(
                                    out=bass.AP(af, o + hf * 8 * 65,
                                                [[AF32W, 128], [65, 8], [1, 64]]),
                                    in_=b[:, 0:512].rearrange("p (h d) -> p h d", h=8)))
                    bank_guard[half] = tv
                p.drain("vector")
                tv2 = p.mark("vector")
                p.wait("gpsimd", tv2)
                for hp in range(NHP):
                    spv_last[ti] = p.dma(
                        "gpsimd", sp_v[hp, ts(tt, 128), :],
                        bass.AP(af, vboff + hp * 130, [[AF32W, 128], [1, 130]]),
                        stream=f"spv{ti % 2}")

            WKB = BSH2 + 8192

            def qk_post(sp_dst, tag, tmm, prev_spl=None, bar=True):
                p.wait("vector", tmm)
                raw = SS + 2080
                for ti in range(TPQ):
                    for half in range(2):
                        bank = ps[ti * 2 + half]
                        p.tick("vector", lambda e, b=bank, ti=ti, hf=half:
                               e.tensor_copy(out=bass.AP(af,
                                                         raw + ti * 1024 + hf * 512,
                                                         [[AF32W, 128], [1, 512]]),
                                             in_=b[:, 0:512]))
                p.drain("vector")
                tv = p.mark("vector")
                p.wait("tensor", tv)
                spl_tok = {}
                prev_ts = [None]
                for ti in range(TPQ):
                    tt = q * TPQ + ti
                    par = ti % 2
                    robase = TB if par == 0 else TC
                    stg_b = BSH1 + 8 * QTOK + par * 2048
                    if ti - 2 >= 0 and (ti - 2) in spl_tok:
                        p.wait("vector", spl_tok[ti - 2])
                        p.wait("scalar", spl_tok[ti - 2])
                    elif prev_spl is not None and par in prev_spl:
                        p.wait("vector", prev_spl[par])
                        p.wait("scalar", prev_spl[par])
                    trp = rope_tile(raw + ti * 1024, tt, out_off=robase)
                    p.wait("tensor", trp)
                    prev_ts[0] = transpose_split(
                        fap(robase, [[1, 1024]]), 8,
                        lambda c, sb=stg_b: (bap(sb + c * 128, [[1, 128]]),
                                             bap(sb + 1024 + c * 128, [[1, 128]])),
                        guard=prev_ts[0])
                    p.drain("vector")
                    tvx = p.mark("vector")
                    p.wait("gpsimd", tvx)
                    tsp = None
                    for c in range(8):
                        p.dma("gpsimd", sp_dst["h"][c, :, ts(tt, 128)],
                              bap(stg_b + c * 128, [[1, 128]]), stream=f"spqk{par}")
                        tsp = p.dma("gpsimd", sp_dst["l"][c, :, ts(tt, 128)],
                                    bap(stg_b + 1024 + c * 128, [[1, 128]]),
                                    stream=f"spqk{par}")
                    spl_tok[ti] = tsp
                if bar:
                    p.barrier(f"qk_{tag}_{call}_{q}")
                return {ti % 2: t for ti, t in spl_tok.items()}

            for _bi, _t in bank_guard.items():
                p.wait("tensor", _t)
            bank_guard = {}
            qk_last = [None]
            for kc in range(2):
                for ti in range(TPQ):
                    for half in range(2):
                        bank = ps[ti * 2 + half]
                        last = (kc == 1 and ti == TPQ - 1 and half == 1)
                        qk_last[0] = mm3(bank[:, 0:512],
                            kch(kc, ti * 128, 128), kcl(kc, ti * 128, 128),
                            bap(WKB + kc * D + half * 512, [[1, 512]]),
                            bap(WKB + 2 * D + kc * D + half * 512, [[1, 512]]),
                            start=(kc == 0), stop=(kc == 1), inc=last)
            spk_tok = qk_post(sp_k, "k", qk_last[0], prev_spl=qspl_prev,
                              bar=False)

            wq_tok = {}

            def load_wq(kc):
                wq_tok[kc] = load_slice("wq", wq_in, kc, BST + (kc % 4) * 2048,
                                        firstq, "row", stream=f"wq{kc % 4}")

            if firstq:
                p.wait_max("sync", [spv_last[ti] for ti in spv_last])
            load_wq(0)
            load_wq(1)
            load_wq(2)
            load_wq(3)
            for kc in range(8):
                p.wait("tensor", wq_tok[kc])
                for ti in range(TPQ):
                    for half in range(2):
                        bank = ps[ti * 2 + half]
                        last = (ti == TPQ - 1 and half == 1)
                        qk_last[0] = mm3(bank[:, 0:512],
                            fmh(kc, ti * 128, 128), fml(kc, ti * 128, 128),
                            bap(BST + (kc % 4) * 2048 + half * 512, [[1, 512]]),
                            bap(BST + (kc % 4) * 2048 + 1024 + half * 512,
                                [[1, 512]]),
                            start=(kc == 0), stop=(kc == 7), inc=last)
                if kc < 7:
                    tmk = qk_last[0]
                    nxt = kc + 4
                    if nxt < 8:
                        p.wait("sync", tmk)
                        if firstq:
                            p.wait("vector", tmk)
                        load_wq(nxt)
            qspl_prev = qk_post(sp_q, "q", qk_last[0], prev_spl=spk_tok,
                                bar=False)
            spv_prev = {ti: spv_last[ti] for ti in spv_last if ti >= TPQ - 2}
            a1_pre = list(qspl_prev.values()) + list(spv_prev.values())
            a1_pre += [sy_tok[i] for i in sy_tok if i >= TPQ - 2]
            a1_pre += list(sp2.values()) if gated else []
            if stage_guard[0] is not None:
                a1_pre.append(stage_guard[0])
        p.barrier(f"a4f_{call}")

        attn_phase()
        p.barrier(f"attn_{call}", light=False)

        t_wo = load_warena("wo", wo_in, BSH2, 8, D, first)
        t_wg = load_warena("wgate", wgate_in, BST, 8, 16, first)
        p.barrier(f"wo_w_{call}")
        for q in range(NQ):
            prev_ts = [None]
            sp_tok = {}
            tsv = {}
            add_tok = {}
            for ti in range(TPQ):
                tt = q * TPQ + ti
                par = ti % 2
                sab = BSH1 + 8 * QTOK + par * 2048
                x1b = TA if par == 0 else SS
                yb = TC if par == 0 else SS + 1024
                sqb = TB if par == 0 else SS + 2080
                cvb = COLV if par == 0 else COLV + 8
                if ti - 2 >= 0:
                    p.wait("sync", tsv[ti - 2])
                    p.wait("sync", sp_tok[ti - 2])
                tok1 = p.dma("sync", bap(sab, [[128, 8], [1, 128]]),
                             sp_sa["h"][:, :, ts(tt, 128)].rearrange("c p j -> p c j"),
                             stream=f"ldsa{par}")
                tok2 = p.dma("sync", bap(sab + 1024, [[128, 8], [1, 128]]),
                             sp_sa["l"][:, :, ts(tt, 128)].rearrange("c p j -> p c j"),
                             stream=f"ldsa{par}")
                toky = p.dma("sync", fap(yb, [[1, D]]), sp_y[ts(tt, 128), :],
                             stream=f"ldy{par}")
                p.wait_max("tensor", [tok1, tok2])
                for half in range(2):
                    bank = ps[half + 2 * par]
                    if ti - 2 >= 0:
                        p.wait("tensor", add_tok[(ti - 2, half)])
                    for kc in range(8):
                        tokmm = mm3(bank[:, 0:512],
                                    bap(sab + kc * 128, [[1, 128]]),
                                    bap(sab + 1024 + kc * 128, [[1, 128]]),
                                    bap(BSH2 + kc * D + half * 512, [[1, 512]]),
                                    bap(BSH2 + 8 * D + kc * D + half * 512, [[1, 512]]),
                                    start=(kc == 0), stop=(kc == 7),
                                    inc=(kc == 7))
                    tmm = tokmm
                    p.wait("vector", tmm)
                    p.wait("vector", toky)
                    add_tok[(ti, half)] = p.tick(
                        "vector", lambda e, b=bank, hf=half, xb=x1b, yb2=yb:
                        e.tensor_tensor(out=fap(xb + hf * 512, [[1, 512]]),
                                        in0=fap(yb2 + hf * 512, [[1, 512]]),
                                        in1=b[:, 0:512], op=OP.add))
                p.drain("vector")
                tvv = p.mark("vector")
                p.wait("scalar", tvv)
                trms = rms_tile(fap(x1b, [[1, D]]), fap(x1b, [[1, D]]), n1w_bc,
                                sq=fap(sqb, [[1, D]]), cv=cvb)
                p.wait("gpsimd", trms)
                sp_tok[ti] = p.dma("gpsimd", sp_flat[ts(tt, 128), :],
                                   fap(x1b, [[1, D]]), stream=f"spfl{par}")
                p.wait("tensor", trms)
                prev_ts[0] = transpose_split(
                    fap(x1b, [[1, 1024]]), 8,
                    lambda c, ti=ti: (
                        bap(BFM + c * 2 * QTOK + ti * 128, [[1, 128]]),
                        bap(BFM + c * 2 * QTOK + QTOK + ti * 128, [[1, 128]])),
                    guard=prev_ts[0])
                tsv[ti] = prev_ts[0]

            def fmh(c, w0, wn):
                return bap(BFM + c * 2 * QTOK + w0, [[1, wn]])

            def fml(c, w0, wn):
                return bap(BFM + c * 2 * QTOK + QTOK + w0, [[1, wn]])

            p.wait("gpsimd", tsv[TPQ - 1])
            for c in range(8):
                p.dma("gpsimd", sp_fm["h"][c, :, ts(q, QTOK)], fmh(c, 0, QTOK))
                p.dma("gpsimd", sp_fm["l"][c, :, ts(q, QTOK)], fml(c, 0, QTOK))
            gating_quarter(q, fmh, fml)
            p.barrier(f"cg_{call}_{q}")

        fm_war = None
        ps_war = None

        def reload_fm(q):
            toks = []
            for c in range(8):
                toks.append(p.dma("sync", bap(BFM + c * 2 * QTOK, [[1, QTOK]]),
                                  sp_fm["h"][c, :, ts(q, QTOK)], stream="fmld"))
                toks.append(p.dma("sync",
                                  bap(BFM + c * 2 * QTOK + QTOK, [[1, QTOK]]),
                                  sp_fm["l"][c, :, ts(q, QTOK)], stream="fmld"))
            return toks

        fm_pre = {}
        for q in range(NQ):
            firstq = first and q == 0
            if q in fm_pre:
                fm_toks = fm_pre.pop(q)
            else:
                if fm_war is not None:
                    p.wait("sync", fm_war)
                fm_toks = reload_fm(q)

            def fmh(c, w0, wn):
                return bap(BFM + c * 2 * QTOK + w0, [[1, wn]])

            def fml(c, w0, wn):
                return bap(BFM + c * 2 * QTOK + QTOK + w0, [[1, wn]])

            wr = wso_ring(firstq)
            if not firstq:
                for _j in range(4):
                    wr[1](_j)
            sw_mm, sw_spl = produce_hidden("wsi", wsi_in, q, fmh, fml, firstq,
                                           fm_guard=fm_toks, psum_guard=ps_war)
            if firstq:
                for _j in range(4):
                    wr[1](_j)
            for _t in sw_spl.values():
                p.wait("tensor", _t)
            consume_hidden_shared(q, firstq, wr)
            fm_war, h_spl = produce_hidden("w1", w1_in, q, fmh, fml, firstq,
                                           ring_guard=sw_mm)
            if q + 1 < NQ:
                p.wait("sync", fm_war)
                fm_pre[q + 1] = reload_fm(q + 1)
            ps_war = experts_quarter(q, firstq, pre_splits=h_spl,
                                     ring_guard=fm_war)
            sp_tok = {}
            res_tok = {}
            p.drain("vector")
            for ti in range(TPQ):
                tt = q * TPQ + ti
                par = ti % 2
                fb = TA if par == 0 else SS + 4224
                xb = TC if par == 0 else SS + 5248
                sqb = SS + 3104 if par == 0 else SS + 6272
                cvb = COLV + 16 if par == 0 else COLV + 24
                if ti - 2 >= 0:
                    p.wait("sync", sp_tok[ti - 2])
                tokf = p.dma("sync", fap(fb, [[1, D]]), sp_flat[ts(tt, 128), :],
                             stream=f"ldy{par}")
                tokx = p.dma("sync", fap(xb, [[1, D]]), src_x[ts(tt, 128), :],
                             stream=f"ldx{par}")
                p.wait("vector", tokf)
                p.tick("vector", lambda e, ti=ti, fb=fb: e.tensor_tensor(
                    out=fap(fb, [[1, D]]), in0=fap(fb, [[1, D]]), in1=accq(ti),
                    op=OP.add))
                p.drain("vector")
                tvv = p.mark("vector")
                p.wait("scalar", tvv)
                trms = rms_tile(fap(fb, [[1, D]]), fap(fb, [[1, D]]), n2w_bc,
                                sq=fap(sqb, [[1, D]]), cv=cvb)
                p.wait("vector", tokx)
                tres = p.tick("vector", lambda e, fb=fb, xb=xb:
                              e.scalar_tensor_tensor(
                    out=fap(fb, [[1, D]]), in0=fap(fb, [[1, D]]), scalar=0.5,
                    in1=fap(xb, [[1, D]]), op0=OP.mult, op1=OP.add))
                p.drain("vector")
                res_tok[ti] = tres
                p.wait("gpsimd", tres)
                sp_tok[ti] = p.dma("gpsimd", dst_x[ts(tt, 128), :],
                                   fap(fb, [[1, D]]), stream=f"spx{par}")
        p.barrier(f"ffn_{call}", light=False)

    def rope_tile(raw_off, tt, out_off=None):
        out_off = TB if out_off is None else out_off
        sc = SS + 2080 + TPQ * 1024
        cosap = bass.AP(af, TRIG + tt * 64, [[AF32W, 128], [0, 16], [1, 32]])
        sinap = bass.AP(af, TRIG + tt * 64 + 32,
                        [[AF32W, 128], [0, 16], [1, 32]])
        ev = bass.AP(af, raw_off, [[AF32W, 128], [64, 16], [2, 32]])
        od = bass.AP(af, raw_off + 1, [[AF32W, 128], [64, 16], [2, 32]])
        top = bass.AP(af, out_off, [[AF32W, 128], [64, 16], [1, 32]])
        bot = bass.AP(af, out_off + 32, [[AF32W, 128], [64, 16], [1, 32]])
        sc1 = bass.AP(af, sc, [[AF32W, 128], [64, 16], [1, 32]])
        sc2 = bass.AP(af, sc + 32, [[AF32W, 128], [64, 16], [1, 32]])
        p.tick("vector", lambda e: e.tensor_tensor(out=sc1, in0=ev, in1=cosap, op=OP.mult))
        p.tick("vector", lambda e: e.tensor_tensor(out=sc2, in0=od, in1=sinap, op=OP.mult))
        p.drain("vector")
        p.tick("vector", lambda e: e.tensor_tensor(out=top, in0=sc1, in1=sc2,
                                                   op=OP.subtract))
        p.drain("vector")
        p.tick("vector", lambda e: e.tensor_tensor(out=sc1, in0=ev, in1=sinap, op=OP.mult))
        p.tick("vector", lambda e: e.tensor_tensor(out=sc2, in0=od, in1=cosap, op=OP.mult))
        p.drain("vector")
        t = p.tick("vector", lambda e: e.tensor_tensor(out=bot, in0=sc1, in1=sc2,
                                                       op=OP.add))
        p.drain("vector")
        return t

    def attn_phase():
        KB = BSH2
        QB = BSH2 + 4 * T
        SAB = QB + 4 * QW
        VS = NKT * 130
        EXPS = SS + 2 * VS
        BCD = EXPS + 6 * QW
        RR = BCD + QW
        tpv = {}
        tfin = {}
        kv_tok = {}
        q_tok = {}

        def load_kv(hp):
            db = hp % 2
            kst = f"k{db}"
            t1 = p.dma("sync", bap(KB + db * T, [[1, T]]), sp_k["h"][hp, :, :],
                       stream=kst)
            t2 = p.dma("sync", bap(KB + 2 * T + db * T, [[1, T]]),
                       sp_k["l"][hp, :, :], stream=kst)
            t3 = p.dma("sync",
                       bass.AP(af, SS + db * VS,
                               [[AF32W, 128], [130, NKT], [1, 130]]),
                       sp_v[hp, :, :].rearrange("(k p) n -> p k n", p=128),
                       stream=f"v{db}")
            kv_tok[hp] = [t1, t2, t3]

        def load_q(hp, qt, qdb):
            qst = f"q{qdb}"
            t1 = p.dma("sync", bap(QB + qdb * QW, [[1, QW]]),
                       sp_q["h"][hp, :, ts(qt, QW)], stream=qst)
            t2 = p.dma("sync", bap(QB + 2 * QW + qdb * QW, [[1, QW]]),
                       sp_q["l"][hp, :, ts(qt, QW)], stream=qst)
            q_tok[(hp, qt)] = [t1, t2]

        load_kv(0)
        load_q(0, 0, 0)
        last_pv_user = {}
        seq = [(hp, qt) for hp in range(NHP) for qt in range(NQT)]
        for idx, (hp, qt) in enumerate(seq):
            db = hp % 2
            qdb = idx % 2
            pvpar = idx % 2
            pvb = (ps[4], ps[5]) if pvpar == 0 else (ps[6], ps[7])
            if idx + 1 < len(seq):
                nhp, nqt = seq[idx + 1]
                if idx >= 1:
                    p.wait("sync", tpv[seq[idx - 1]])
                if nqt == 0:
                    load_kv(nhp)
                load_q(nhp, nqt, (idx + 1) % 2)
            p.wait_max("tensor", kv_tok[hp] + q_tok[(hp, qt)])
            texp = {}

            def scores(kt):
                pa = ps[kt % 2]
                pb = ps[2 + kt % 2]
                if kt >= 2 and (kt - 2) in texp:
                    for t in texp[kt - 2]:
                        p.wait("tensor", t)
                tmm = None
                for (base, bank, tp) in ((0, pa, (0, 0)), (64, pb, (64, 0))):
                    tmm = mm3(bank[:, 0:QW],
                        bass.AP(ab, base * ABFW + KB + db * T + kt * 128,
                                [[ABFW, 64], [1, 128]]),
                        bass.AP(ab, base * ABFW + KB + 2 * T + db * T
                                + kt * 128, [[ABFW, 64], [1, 128]]),
                        bass.AP(ab, base * ABFW + QB + qdb * QW,
                                [[ABFW, 64], [1, QW]]),
                        bass.AP(ab, base * ABFW + QB + 2 * QW + qdb * QW,
                                [[ABFW, 64], [1, QW]]),
                        start=True, stop=True, tile_position=tp,
                        inc=(base == 64))
                p.wait("scalar", tmm)
                if qt >= 1 or (hp, qt) != (0, 0):
                    prev = seq[idx - 1] if kt == 0 else None
                    if prev is not None and prev in tpv:
                        p.wait("scalar", tpv[prev])
                es = []
                for (i, bank) in ((0, pa), (1, pb)):
                    es.append(p.tick("scalar", lambda e, i=i, b=bank, kt=kt:
                                     e.activation(
                                         out=fap(EXPS + (3 * i + kt % 3) * QW,
                                                 [[1, QW]]),
                                         in_=b[:, 0:QW], func=AF.Exp,
                                         scale=0.125)))
                texp[kt] = es

            def pv(kt):
                for t in texp[kt]:
                    p.wait("tensor", t)
                if kt == 0 and last_pv_user.get(pvpar) is not None:
                    p.wait("tensor", tfin[last_pv_user[pvpar]])
                for (i, bank) in ((0, pvb[0]), (1, pvb[1])):
                    voff = SS + db * VS + kt * 130 + i * 65
                    pv_last[0] = p.emit(
                        "tensor", lambda e, b=bank, i=i, kt=kt, voff=voff:
                        e.matmul(out=b[0:65, 0:QW],
                                 lhsT=bass.AP(af, voff,
                                              [[AF32W, 128], [1, 65]]),
                                 rhs=fap(EXPS + (3 * i + kt % 3) * QW,
                                         [[1, QW]]),
                                 start=(kt == 0), stop=(kt == NKT - 1)),
                        inc="bar_tensor" if (kt == NKT - 1 and i == 1) else None)

            pv_last = [None]
            scores(0)
            if NKT > 1:
                scores(1)
            for kt in range(2, NKT):
                scores(kt)
                pv(kt - 2)
            if NKT > 1:
                pv(NKT - 2)
            pv(NKT - 1)
            tpv[(hp, qt)] = pv_last[0]
            p.wait("vector", tpv[(hp, qt)])
            for (i, bank) in ((0, pvb[0]), (1, pvb[1])):
                p.tick("vector", lambda e, b=bank: e.reciprocal(
                    out=bass.AP(af, 64 * AF32W + RR, [[AF32W, 1], [1, QW]]),
                    in_=b[64:65, 0:QW]))
                p.drain("vector")
                tv = p.mark("vector")
                p.wait("gpsimd", tv)
                t1 = p.dma("gpsimd", bounce[0:1, 0:QW],
                           bass.AP(af, 64 * AF32W + RR,
                                   [[AF32W, 1], [1, QW]]), stream="bnc")
                p.wait("gpsimd", t1)
                t2 = p.dma("gpsimd",
                           bass.AP(af, BCD, [[AF32W, 64], [1, QW]]),
                           bass.AP(bounce.tensor, 0, [[0, 64], [1, QW]]),
                           stream="bnc")
                p.wait("vector", t2)
                p.tick("vector", lambda e, b=bank: e.tensor_tensor(
                    out=bass.AP(af, RR, [[AF32W, 64], [1, QW]]),
                    in0=b[0:64, 0:QW],
                    in1=bass.AP(af, BCD, [[AF32W, 64], [1, QW]]),
                    op=OP.mult))
                p.drain("vector")
                tv2 = split_pair("vector",
                                 bass.AP(ab, SAB, [[ABFW, 64], [1, QW]]),
                                 bass.AP(ab, SAB + QW, [[ABFW, 64], [1, QW]]),
                                 bass.AP(af, RR, [[AF32W, 64], [1, QW]]))
                p.wait("gpsimd", tv2)
                rb = 0 if i == 0 else 64
                td1 = p.dma("gpsimd", sp_sa["h"][hp, rb:rb + 64, ts(qt, QW)],
                            bass.AP(ab, SAB, [[ABFW, 64], [1, QW]]), stream="spsa")
                td2 = p.dma("gpsimd", sp_sa["l"][hp, rb:rb + 64, ts(qt, QW)],
                            bass.AP(ab, SAB + QW, [[ABFW, 64], [1, QW]]),
                            stream="spsa")
                p.wait("vector", td2)
            tfin[(hp, qt)] = p.mark("vector")
            last_pv_user[pvpar] = (hp, qt)

    def gating_quarter(q, fmh, fml):
        NW = QTOK // 512 if QTOK >= 512 else 1
        W5 = min(512, QTOK)
        LOG = ACCQ
        for w in range(NW):
            bank = ps[w % 2]
            tokmm = None
            for kc in range(8):
                tokmm = mm3(bank[0:10, 0:W5],
                            bap(BST + kc * 16, [[1, 10]]),
                            bap(BST + 8 * 16 + kc * 16, [[1, 10]]),
                            fmh(kc, w * W5, W5), fml(kc, w * W5, W5),
                            start=(kc == 0), stop=(kc == 7), inc=(kc == 7))
            tmm = tokmm
            p.wait("vector", tmm)
            tv = p.tick("vector", lambda e, b=bank, w=w: e.tensor_copy(
                out=bass.AP(af, LOG + w * W5, [[AF32W, 10], [1, W5]]),
                in_=b[0:10, 0:W5]))
            p.wait("tensor", tv)
        p.drain("vector")
        tvx = p.mark("vector")
        p.wait("tensor", tvx)
        tcop = None
        for ti in range(TPQ):
            bank = ps[2 + ti % 2]
            if ti - 2 >= 0 and tcop is not None:
                p.wait("tensor", tcop)
            ttok = p.tick("tensor", lambda e, ti=ti, b=bank: e.transpose(
                out=b[:, 0:10], in_=bass.AP(af, LOG + ti * 128,
                                            [[AF32W, 10], [1, 128]]),
                identity=iden_sb[0:10, 0:10]))
            p.wait("vector", ttok)
            tcop = p.tick("vector", lambda e, ti=ti, b=bank: e.tensor_copy(
                out=fap(TB + ti * 24, [[1, 10]]), in_=b[:, 0:10]))
        p.drain("vector")
        gate_batch(q)
        p.drain("vector")

    def gate_batch(q):
        NTI = TPQ

        def L(a, w=1):
            return bass.AP(af, TB + a, [[AF32W, 128], [24, NTI], [1, w]])

        def Lb(a, w):
            return bass.AP(af, TB + a, [[AF32W, 128], [24, NTI], [0, w]])

        def vt(fn):
            p.tick("vector", fn)
            p.drain("vector")

        tS0 = p.mark("scalar")
        p.wait("vector", tS0)
        vt(lambda e: e.tensor_tensor(
            out=L(0, 10), in0=L(0, 10),
            in1=bass.AP(af, GBBC, [[AF32W, 128], [0, NTI], [1, 10]]), op=OP.add))
        vt(lambda e: e.tensor_tensor(out=L(10), in0=L(0), in1=L(1), op=OP.max))
        tsub = p.tick("vector", lambda e: e.tensor_tensor(
            out=L(11, 2), in0=L(0, 2), in1=Lb(10, 2), op=OP.subtract))
        p.drain("vector")
        p.wait("scalar", tsub)
        p.drain("scalar")
        t = p.tick("scalar", lambda e: e.activation(out=L(11, 2), in_=L(11, 2),
                                                    func=AF.Exp))
        p.wait("vector", t)
        vt(lambda e: e.tensor_tensor(out=L(13), in0=L(11), in1=L(12), op=OP.add))
        vt(lambda e: e.reciprocal(out=L(14), in_=L(13)))
        vt(lambda e: e.tensor_tensor(out=L(11, 2), in0=L(11, 2), in1=Lb(14, 2),
                                     op=OP.mult))
        vt(lambda e: e.tensor_tensor(out=L(15), in0=L(11), in1=L(12), op=OP.max))
        vt(lambda e: e.tensor_tensor(out=L(10), in0=L(12), in1=L(11), op=OP.is_gt))
        vt(lambda e: e.tensor_scalar(out=L(16), in0=L(10), scalar1=float(NEGBIG),
                                     scalar2=None, op0=OP.mult))
        vt(lambda e: e.tensor_scalar(out=L(17), in0=L(10), scalar1=-1.0,
                                     scalar2=1.0, op0=OP.mult, op1=OP.add))
        vt(lambda e: e.tensor_scalar(out=L(17), in0=L(17), scalar1=float(NEGBIG),
                                     scalar2=None, op0=OP.mult))
        vt(lambda e: e.tensor_tensor(out=L(2, 4), in0=L(2, 4), in1=Lb(16, 4),
                                     op=OP.add))
        vt(lambda e: e.tensor_tensor(out=L(6, 4), in0=L(6, 4), in1=Lb(17, 4),
                                     op=OP.add))
        vt(lambda e: e.reduce_max(out=L(10), in_=L(2, 8), axis=AX.X))
        tv = p.tick("vector", lambda e: e.tensor_tensor(
            out=L(2, 8), in0=L(2, 8), in1=Lb(10, 8), op=OP.subtract))
        p.wait("scalar", tv)
        p.drain("scalar")
        t = p.tick("scalar", lambda e: e.activation(out=L(2, 8), in_=L(2, 8),
                                                    func=AF.Exp))
        p.wait("vector", t)
        vt(lambda e: e.reduce_sum(out=L(10), in_=L(2, 8), axis=AX.X))
        vt(lambda e: e.reciprocal(out=L(11), in_=L(10)))
        vt(lambda e: e.tensor_tensor(out=L(2, 8), in0=L(2, 8), in1=Lb(11, 8),
                                     op=OP.mult))
        vt(lambda e: e.tensor_tensor(out=L(2, 8), in0=L(2, 8), in1=Lb(15, 8),
                                     op=OP.mult))
        vt(lambda e: e.reduce_max(out=L(10), in_=L(2, 8), axis=AX.X))
        vt(lambda e: e.tensor_tensor(out=L(16, 8), in0=L(2, 8), in1=Lb(10, 8),
                                     op=OP.is_equal))
        vt(lambda e: e.tensor_scalar(out=L(16, 8), in0=L(16, 8), scalar1=-1.0,
                                     scalar2=1.0, op0=OP.mult, op1=OP.add))
        vt(lambda e: e.tensor_tensor(out=L(16, 8), in0=L(16, 8), in1=L(2, 8),
                                     op=OP.mult))
        vt(lambda e: e.reduce_max(out=L(12), in_=L(16, 8), axis=AX.X))
        vt(lambda e: e.tensor_tensor(out=L(16, 8), in0=L(2, 8), in1=Lb(12, 8),
                                     op=OP.is_ge))
        p.tick("vector", lambda e, q=q: e.tensor_tensor(
            out=bass.AP(af, WGT + q * TPQ * 8, [[AF32W, 128], [8, NTI], [1, 8]]),
            in0=L(16, 8), in1=L(2, 8), op=OP.mult))
        p.drain("vector")

    def produce_hidden(nm, in_ap, q, fmh, fml, firstq, fm_guard=None,
                       psum_guard=None, ring_guard=None):
        NW = QTOK // 512 if QTOK >= 512 else 1
        W5 = min(512, QTOK)

        def slot_a(j):
            return BST + (j % 4) * 2048

        def slot_b(j):
            return BSH2 + (j % 4) * 2048

        tok_load = {}

        def load_j(j):
            ta_ = load_slice(nm, in_ap, j, slot_a(j), firstq, "col",
                             stream=f"wsa{j % 4}")
            tb_ = load_slice(nm, in_ap, 16 + j, slot_b(j), firstq, "col",
                             stream=f"wsb{j % 4}")
            tok_load[j] = (ta_, tb_)

        mm_tok = {}
        prev_split = {}
        if ring_guard is not None:
            p.wait("sync", ring_guard)
        load_j(0)
        load_j(1)
        load_j(2)
        load_j(3)
        if psum_guard is not None:
            p.wait("tensor", psum_guard)
        if fm_guard:
            p.wait_max("tensor", fm_guard)
        for j in range(16):
            p.wait_max("tensor", list(tok_load[j]))
            for w in range(NW):
                if (j + w) % 2 in prev_split:
                    p.wait("tensor", prev_split[(j + w) % 2])
                banka = ps[(j + w) % 2]
                bankb = ps[2 + (j + w) % 2]
                for kc in range(8):
                    mm3(banka[:, 0:W5],
                        bap(slot_a(j) + kc * 128, [[1, 128]]),
                        bap(slot_a(j) + 1024 + kc * 128, [[1, 128]]),
                        fmh(kc, w * W5, W5), fml(kc, w * W5, W5),
                        start=(kc == 0), stop=(kc == 7))
                tmm = None
                for kc in range(8):
                    tmm = mm3(bankb[:, 0:W5],
                        bap(slot_b(j) + kc * 128, [[1, 128]]),
                        bap(slot_b(j) + 1024 + kc * 128, [[1, 128]]),
                        fmh(kc, w * W5, W5), fml(kc, w * W5, W5),
                        start=(kc == 0), stop=(kc == 7), inc=(kc == 7))
                if w == NW - 1:
                    mm_tok[j] = tmm
                    nxt = j + 4
                    if nxt < 16:
                        p.wait("sync", tmm)
                        if firstq:
                            p.wait("vector", tmm)
                        load_j(nxt)
                par = (j + w) % 2
                tbo = TB + par * W5 if W5 <= 512 else TB
                if par in prev_split:
                    p.wait("scalar", prev_split[par])
                p.wait("scalar", tmm)
                tsi = p.tick("scalar", lambda e, ba=banka, o=tbo: e.activation(
                    out=fap(o, [[1, W5]]), in_=ba[:, 0:W5], func=AF.Silu))
                p.wait("vector", tsi)
                p.tick("vector", lambda e, bb=bankb, o=tbo: e.tensor_tensor(
                    out=fap(o, [[1, W5]]), in0=fap(o, [[1, W5]]),
                    in1=bb[:, 0:W5], op=OP.mult))
                p.drain("vector")
                tv = split_pair("vector",
                                bap(BSH1 + j * QTOK + w * W5, [[1, W5]]),
                                bap(BSH1 + 8192 + j * QTOK + w * W5, [[1, W5]]),
                                fap(tbo, [[1, W5]]))
                prev_split[par] = tv
        return mm_tok[15], dict(prev_split)
    def wso_ring(firstq):
        tok_load = {}

        def load_j(j):
            tok_load[j] = load_slice("wso", wso_in, j,
                                     BSH2 + 8192 + (j % 4) * 2048,
                                     firstq, "row", stream=f"wso{j % 4}")

        return tok_load, load_j

    def consume_hidden_shared(q, firstq, ring):
        tok_load, load_j = ring
        tmm = None
        for j in range(16):
            p.wait("tensor", tok_load[j])
            for ti in range(TPQ):
                for half in range(2):
                    bank = ps[ti * 2 + half]
                    last = (ti == TPQ - 1 and half == 1)
                    tmm = mm3(bank[:, 0:512],
                        bap(BSH1 + j * QTOK + ti * 128, [[1, 128]]),
                        bap(BSH1 + 8192 + j * QTOK + ti * 128, [[1, 128]]),
                        bap(BSH2 + 8192 + (j % 4) * 2048 + half * 512,
                            [[1, 512]]),
                        bap(BSH2 + 8192 + (j % 4) * 2048 + 1024 + half * 512,
                            [[1, 512]]),
                        start=(j == 0), stop=(j == 15), inc=last)
            nxt = j + 4
            if nxt < 16:
                p.wait("sync", tmm)
                if firstq:
                    p.wait("vector", tmm)
                load_j(nxt)
        p.wait("vector", tmm)
        p.drain("vector")
        for ti in range(TPQ):
            for half in range(2):
                bank = ps[ti * 2 + half]
                tv = p.tick("vector", lambda e, b=bank, ti=ti, hf=half:
                            e.tensor_copy(out=accq(ti, 512, hf * 512),
                                          in_=b[:, 0:512]))
                p.wait("tensor", tv)

    def experts_quarter(q, firstq, pre_splits=None, ring_guard=None):
        tok_load = {}

        NSLOT = 4

        def load_L(L):
            e_i, jj = divmod(L, 16)
            tok_load[L] = load_slice("w2", None, (e_i, jj),
                                     BST + (L % NSLOT) * 2048, firstq, "w2e",
                                     stream=f"w2s{L % NSLOT}")

        total = NEXP * 16
        if ring_guard is not None:
            p.wait("sync", ring_guard)
        for _l in range(NSLOT):
            load_L(_l)
        if pre_splits is not None:
            for _t in pre_splits.values():
                p.wait("tensor", _t)
        prev_evac = {}
        last_evac = [None]
        for e_i in range(NEXP):
            for j in range(16):
                L = e_i * 16 + j
                slot = BST + (L % NSLOT) * 2048
                p.wait("tensor", tok_load[L])
                for ti in range(TPQ):
                    for half in range(2):
                        bank = ps[ti * 2 + half]
                        if j == 0 and (ti * 2 + half) in prev_evac:
                            p.wait("tensor", prev_evac[ti * 2 + half])
                        last = (ti == TPQ - 1 and half == 1)
                        tmm = mm3(bank[:, 0:512],
                            bap(BSH1 + j * QTOK + ti * 128, [[1, 128]]),
                            bap(BSH1 + 8192 + j * QTOK + ti * 128, [[1, 128]]),
                            bap(slot + half * 512, [[1, 512]]),
                            bap(slot + 1024 + half * 512, [[1, 512]]),
                            start=(j == 0), stop=(j == 15), inc=last)
                nxt = L + NSLOT
                if nxt < total:
                    p.wait("sync", tmm)
                    if firstq:
                        p.wait("vector", tmm)
                    load_L(nxt)
            p.wait("vector", tmm)
            for ti in range(TPQ):
                tt = q * TPQ + ti
                for half in range(2):
                    bank = ps[ti * 2 + half]
                    tv = p.tick("vector", lambda e, b=bank, ti=ti, hf=half, ei=e_i,
                                tt=tt: e.scalar_tensor_tensor(
                        out=accq(ti, 512, hf * 512), in0=b[:, 0:512],
                        scalar=fap(WGT + tt * 8 + ei, [[1, 1]]),
                        in1=accq(ti, 512, hf * 512), op0=OP.mult, op1=OP.add))
                    p.drain("vector")
                    prev_evac[ti * 2 + half] = tv
                    last_evac[0] = tv
        return last_evac[0]

    def attn_slot():
        qguard = {}
        for q in range(NQ):
            prev_ts = [None]
            tsv = {}
            XBS = (TA, TC, SS, SS + 1024)
            SQS = (TB, SS + 2080, SS + 3104, SS + 4224)
            CVS = (COLV, COLV + 8, COLV + 16, COLV + 24)

            def c_(i, a, b):
                return fap(CVS[i] + a, [[1, b - a]])

            def xv(i, n=D):
                return fap(XBS[i], [[1, n]])

            def sv(i, n=D):
                return fap(SQS[i], [[1, n]])

            ltok = {}
            for i in range(TPQ):
                tt = q * TPQ + i
                if qguard.get(i) is not None:
                    p.wait("sync", qguard[i])
                ltok[i] = p.dma("sync", xv(i), sp_x[ts(tt, 128), :],
                                stream=f"ldx{i}")
            p.drain("vector")
            p.drain("scalar")
            for i in range(TPQ):
                p.wait("vector", ltok[i])
                p.wait("scalar", ltok[i])
                p.tick("vector", lambda e, i=i: e.reduce_sum(
                    out=c_(i, 0, 1), in_=xv(i), axis=AX.X))
                p.tick("scalar", lambda e, i=i: e.activation(
                    out=sv(i), in_=xv(i), func=AF.Square, accum_out=c_(i, 1, 2)))
            tS = p.mark("scalar")
            p.wait("vector", tS)
            p.drain("vector")
            for i in range(TPQ):
                p.tick("vector", lambda e, i=i: e.tensor_scalar(
                    out=c_(i, 2, 3), in0=c_(i, 0, 1), scalar1=1.0 / D,
                    scalar2=None, op0=OP.mult))
            p.drain("vector")
            for i in range(TPQ):
                p.tick("vector", lambda e, i=i: e.tensor_tensor(
                    out=c_(i, 3, 4), in0=c_(i, 2, 3), in1=c_(i, 2, 3), op=OP.mult))
            p.drain("vector")
            tV = None
            for i in range(TPQ):
                tV = p.tick("vector", lambda e, i=i: e.tensor_scalar(
                    out=c_(i, 4, 5), in0=c_(i, 1, 2), scalar1=1.0 / D,
                    scalar2=c_(i, 3, 4), op0=OP.mult, op1=OP.subtract))
            p.wait("scalar", tV)
            p.drain("scalar")
            tS = None
            for i in range(TPQ):
                tS = p.tick("scalar", lambda e, i=i: e.activation(
                    out=c_(i, 5, 6), in_=c_(i, 4, 5), func=AF.Sqrt,
                    bias=float(EPS_LN)))
            p.wait("vector", tS)
            p.drain("vector")
            for i in range(TPQ):
                p.tick("vector", lambda e, i=i: e.reciprocal(
                    out=c_(i, 6, 7), in_=c_(i, 5, 6)))
            p.drain("vector")
            tVA = None
            for i in range(TPQ):
                tVA = p.tick("vector", lambda e, i=i: e.tensor_scalar(
                    out=xv(i), in0=xv(i), scalar1=c_(i, 2, 3), scalar2=c_(i, 6, 7),
                    op0=OP.subtract, op1=OP.mult))
            p.wait("scalar", tVA)
            p.drain("scalar")
            gtok = {}
            sgp = {}
            for i in range(TPQ):
                gtok[i] = p.tick("scalar", lambda e, i=i: e.activation(
                    out=xv(i), in_=xv(i), func=AF.Gelu_apprx_tanh))
            for i in range(TPQ):
                tt = q * TPQ + i
                p.wait("gpsimd", gtok[i])
                sgp[i] = p.dma("gpsimd", sp_g[ts(tt, 128), :], xv(i),
                               stream=f"spg{i}")
            p.drain("scalar")
            tS = None
            for i in range(TPQ):
                tS = p.tick("scalar", lambda e, i=i: e.activation(
                    out=sv(i), in_=xv(i), func=AF.Square, accum_out=c_(i, 1, 2)))
            p.drain("scalar")
            tS2 = None
            for i in range(TPQ):
                tS2 = p.tick("scalar", lambda e, i=i: e.activation(
                    out=c_(i, 5, 6), in_=c_(i, 1, 2), func=AF.Sqrt))
            p.wait("vector", tS2)
            p.drain("vector")
            for i in range(TPQ):
                p.tick("vector", lambda e, i=i: e.tensor_scalar(
                    out=c_(i, 5, 6), in0=c_(i, 5, 6), scalar1=1e-12, scalar2=None,
                    op0=OP.max))
            p.drain("vector")
            for i in range(TPQ):
                tt = q * TPQ + i
                p.tick("vector", lambda e, i=i, tt=tt: e.reciprocal(
                    out=fap(INVN + tt, [[1, 1]]), in_=c_(i, 5, 6)))
            p.drain("vector")
            for i in range(TPQ):
                p.tick("vector", lambda e, i=i: e.tensor_tensor(
                    out=sv(i), in0=xv(i), in1=ema_bc, op=OP.mult))
            p.drain("vector")
            for i in range(TPQ):
                p.tick("vector", lambda e, i=i: e.reduce_sum(
                    out=c_(i, 7, 8), in_=sv(i), axis=AX.X))
            p.drain("vector")
            for i in range(TPQ):
                tt = q * TPQ + i
                p.tick("vector", lambda e, i=i, tt=tt: e.tensor_tensor(
                    out=fap(EMAS + tt, [[1, 1]]), in0=c_(i, 7, 8),
                    in1=fap(INVN + tt, [[1, 1]]), op=OP.mult))
            p.drain("vector")
            for i in range(TPQ):
                tt = q * TPQ + i
                p.wait("vector", sgp[i])
                t5 = p.tick("vector", lambda e, i=i, tt=tt: e.tensor_scalar(
                    out=xv(i), in0=xv(i), scalar1=fap(INVN + tt, [[1, 1]]),
                    scalar2=None, op0=OP.mult))
                p.drain("vector")
                p.wait("tensor", t5)
                prev_ts[0] = transpose_split(
                    fap(XBS[i], [[1, 1024]]), 8,
                    lambda c, ti=i: (
                        bap(BFM + c * 2 * QTOK + ti * 128, [[1, 128]]),
                        bap(BFM + c * 2 * QTOK + QTOK + ti * 128, [[1, 128]])),
                    guard=prev_ts[0])
                tsv[i] = prev_ts[0]
                qguard[i] = prev_ts[0]
            p.wait("gpsimd", tsv[TPQ - 1])
            for c in range(8):
                p.dma("gpsimd", sp_fm["h"][c, :, ts(q, QTOK)],
                      bap(BFM + c * 2 * QTOK, [[1, QTOK]]))
                p.dma("gpsimd", sp_fm["l"][c, :, ts(q, QTOK)],
                      bap(BFM + c * 2 * QTOK + QTOK, [[1, QTOK]]))
            if q == 0:
                for tt in range(NT):
                    p.tick("vector", lambda e, tt=tt: e.memset(
                        fap(SEQS + tt, [[1, 1]]), -2.0))
                p.drain("vector")
                tvz = p.mark("vector")
                p.wait("tensor", tvz)
            last_blk = [None]
            simg = {}
            rhs_tok = {}

            def load_rhs(sh):
                par = sh % 2
                base = BSH1 + par * 8192
                trr = []
                for c in range(8):
                    trr.append(p.dma("sync", bap(base + c * 1024, [[1, QW]]),
                                     sp_fm["h"][c, :, ts(sh, QW)],
                                     stream=f"simr{par}"))
                    trr.append(p.dma("sync", bap(base + c * 1024 + 512, [[1, QW]]),
                                     sp_fm["l"][c, :, ts(sh, QW)],
                                     stream=f"simr{par}"))
                rhs_tok[sh] = trr

            if q > 0:
                load_rhs(0)
            for sh in range(q + 1):
                diag = (sh == q)
                RHS = BSH1 + (sh % 2) * 8192
                if not diag:
                    p.wait_max("tensor", rhs_tok[sh])
                    if sh + 1 < q:
                        if last_blk[0] is not None:
                            p.wait("sync", last_blk[0])
                        load_rhs(sh + 1)
                for ti in range(TPQ):
                    tt = q * TPQ + ti
                    bank = ps[tt % 2]
                    if (tt % 2) in simg:
                        p.wait("tensor", simg[tt % 2])
                    tokmm = None
                    for kc in range(8):
                        if diag:
                            rh = bap(BFM + kc * 2 * QTOK, [[1, QTOK]])
                            rl = bap(BFM + kc * 2 * QTOK + QTOK, [[1, QTOK]])
                        else:
                            rh = bap(RHS + kc * 1024, [[1, QW]])
                            rl = bap(RHS + kc * 1024 + 512, [[1, QW]])
                        tokmm = mm3(bank[:, 0:QW],
                                    bap(BFM + kc * 2 * QTOK + ti * 128, [[1, 128]]),
                                    bap(BFM + kc * 2 * QTOK + QTOK + ti * 128,
                                        [[1, 128]]),
                                    rh, rl,
                                    start=(kc == 0), stop=(kc == 7), inc=(kc == 7))
                    tmm = tokmm
                    last_blk[0] = tmm
                    p.wait("vector", tmm)
                    off = tt * 128 - sh * QW
                    if off >= QW:
                        p.tick("vector", lambda e, b=bank: e.reduce_max(
                            out=colv(8, 9), in_=b[:, 0:QW], axis=AX.X))
                    else:
                        mi = off // 128
                        p.tick("vector", lambda e, b=bank, mi=mi: e.tensor_tensor(
                            out=fap(SS + 5248, [[1, QW]]), in0=b[:, 0:QW],
                            in1=mask_ap(mi), op=OP.add))
                        p.drain("vector")
                        p.tick("vector", lambda e: e.reduce_max(
                            out=colv(8, 9), in_=fap(SS + 5248, [[1, QW]]),
                            axis=AX.X))
                    p.drain("vector")
                    p.tick("vector", lambda e, tt=tt: e.tensor_tensor(
                        out=fap(SEQS + tt, [[1, 1]]), in0=fap(SEQS + tt, [[1, 1]]),
                        in1=colv(8, 9), op=OP.max))
                    p.drain("vector")
                    tvv = p.mark("vector")
                    simg[tt % 2] = tvv
            p.barrier(f"sim_q{q}")
        p.tick("vector", lambda e: e.tensor_copy(
            out=bass.AP(af, SEQS, [[AF32W, 1], [1, 1]]),
            in_=bass.AP(af, EMAS, [[AF32W, 1], [1, 1]])))
        p.drain("vector")
        p.barrier("sim_done")

        p.barrier("as_done")

    if stages >= 3:
        ffn_call(0, x_in, sp_x)
        attn_slot()
        ffn_call(1, sp_x, out_t, gated=True)
    elif stages == 2:
        ffn_call(0, x_in, sp_x)
        attn_slot()
        for tt in range(NT):
            tok = p.dma("sync", tA(D), sp_x[ts(tt, 128), :], stream="ldx")
            p.wait("gpsimd", tok)
            p.dma("gpsimd", out_t[ts(tt, 128), :], tA(D))
            p.barrier(f"st_{tt}")
    else:
        ffn_call(0, x_in, out_t)
    p.barrier("end", light=False)

    with nc.Block() as block:
        p.finalize(block)
    es.close()
    return nc


def _consts(T):
    NT = T // 128
    QW = min(512, T)
    iden = np.eye(128, dtype=np.float32)
    inv_freq = (1.0 / 10000.0 ** (np.arange(0, DH, 2, dtype=np.float32) / DH)
                ).astype(np.float32)
    t_idx = np.arange(T, dtype=np.float32)
    ang = (t_idx[:, None] * inv_freq[None, :]).astype(np.float32)
    sin, cos = np.sin(ang).astype(np.float32), np.cos(ang).astype(np.float32)
    trig = np.zeros((128, 2 * NT * 32), dtype=np.float32)
    for tt in range(NT):
        trig[:, tt * 64:tt * 64 + 32] = cos[tt * 128:(tt + 1) * 128]
        trig[:, tt * 64 + 32:tt * 64 + 64] = sin[tt * 128:(tt + 1) * 128]
    masks = np.zeros((128, 4 * QW), dtype=np.float32)
    for mi in range(4):
        off = mi * 128
        pp = np.arange(128)[:, None]
        cc = np.arange(QW)[None, :]
        masks[:, mi * QW:(mi + 1) * QW] = np.where(cc < off + pp, 0.0, -1e9)
    return iden, trig, masks


_NC_CACHE = {}


def _get_nc(T):
    if T not in _NC_CACHE:
        _NC_CACHE[T] = build_program(T)
    return _NC_CACHE[T]


def kernel(L0_attn_x, L0_attn_ema_out, L0_attn_logit_decay, L0_attn_log_tau,
           L0_attn_log_blend, L0_attn_logit_blend_seq, L0_ffn_Wq, L0_ffn_Wk_c,
           L0_ffn_Wv_c, L0_ffn_Wk, L0_ffn_Wv, L0_ffn_Wo, L0_ffn_W_shared_in,
           L0_ffn_W_shared_out, L0_ffn_W1_shared, L0_ffn_W2_experts,
           L0_ffn_Wg_gate, L0_ffn_We_gate, L0_ffn_group_bias,
           L0_ffn_expert_bias, L0_ffn_norm1_w, L0_ffn_norm2_w):
    T = L0_attn_x.shape[1]
    nc = _get_nc(T)
    iden, trig, masks = _consts(T)
    scal = np.zeros((1, 4), dtype=np.float32)
    scal[0, 0] = np.float32(L0_attn_log_tau)
    scal[0, 1] = np.float32(L0_attn_log_blend)
    scal[0, 2] = np.float32(L0_attn_logit_blend_seq)
    wgate = np.zeros((D, 16), dtype=np.float32)
    wgate[:, 0:2] = np.asarray(L0_ffn_Wg_gate, dtype=np.float32)
    wgate[:, 2:10] = np.asarray(L0_ffn_We_gate, dtype=np.float32)
    gbias = np.zeros((1, 16), dtype=np.float32)
    gbias[0, 0:2] = np.asarray(L0_ffn_group_bias, dtype=np.float32)
    gbias[0, 2:10] = np.asarray(L0_ffn_expert_bias, dtype=np.float32)
    base = {
        "ema_in": np.asarray(L0_attn_ema_out, dtype=np.float32).reshape(1, D),
        "scal_in": scal,
        "wq_in": np.ascontiguousarray(L0_ffn_Wq, dtype=np.float32),
        "wkc_in": np.ascontiguousarray(L0_ffn_Wk_c, dtype=np.float32),
        "wvc_in": np.ascontiguousarray(L0_ffn_Wv_c, dtype=np.float32),
        "wk_in": np.ascontiguousarray(L0_ffn_Wk, dtype=np.float32),
        "wv_in": np.ascontiguousarray(L0_ffn_Wv, dtype=np.float32),
        "wo_in": np.ascontiguousarray(L0_ffn_Wo, dtype=np.float32),
        "wsi_in": np.ascontiguousarray(L0_ffn_W_shared_in, dtype=np.float32),
        "wso_in": np.ascontiguousarray(L0_ffn_W_shared_out, dtype=np.float32),
        "w1_in": np.ascontiguousarray(L0_ffn_W1_shared, dtype=np.float32),
        "w2_in": np.ascontiguousarray(L0_ffn_W2_experts, dtype=np.float32),
        "wgate_in": wgate,
        "gbias_in": gbias,
        "n1w_in": np.asarray(L0_ffn_norm1_w, dtype=np.float32).reshape(1, D),
        "n2w_in": np.asarray(L0_ffn_norm2_w, dtype=np.float32).reshape(1, D),
        "iden_in": iden,
        "trig_in": trig,
        "mask_in": masks,
    }
    xs = np.ascontiguousarray(L0_attn_x, dtype=np.float32)
    in_maps = [dict(base, x_in=xs[b]) for b in range(xs.shape[0])]
    res = run_bass_kernel_spmd(nc, in_maps, core_ids=list(range(len(in_maps))))
    out = np.stack([r["out_t"] for r in res.results], axis=0).astype(np.float32)
    return out


# version 26
# speedup vs baseline: 1.1151x; 1.0933x over previous
from collections import defaultdict
from contextlib import ExitStack

import numpy as np

import concourse.bass as bass
import concourse.mybir as mybir
from concourse.alu_op_type import AluOpType
from concourse.bass_utils import run_bass_kernel_spmd

F32 = mybir.dt.float32
BF16 = mybir.dt.bfloat16
AF = mybir.ActivationFunctionType
AX = mybir.AxisListType
OP = AluOpType

D = 1024
H = 16
DH = 64
DC = 256
HID = 2048
NEXP = 8
EPS_RMS = 1e-6
EPS_LN = 1e-5
NEGBIG = -1e30

ENGINES = ("sync", "tensor", "vector", "scalar", "gpsimd")


class Prog:

    def __init__(self, nc):
        self.nc = nc
        self.ops = {e: [] for e in ENGINES}
        self.sem = {}
        self.cnt = defaultdict(int)
        self.waited = {e: defaultdict(int) for e in ENGINES}

    def make_sems(self, es):
        self._es = es
        for name in ("bar_sync", "bar_tensor", "bar_vector", "bar_scalar",
                     "bar_gpsimd"):
            self.sem[name] = es.enter_context(self.nc.semaphore(name))
        self.dma_sems = {}

    def dma_sem(self, name, q):
        if name not in self.sem:
            self.sem[name] = self._es.enter_context(self.nc.semaphore(name))
            self.dma_sems[name] = q
        assert self.dma_sems[name] == q, f"dma sem {name} used from two queues"
        return name

    def emit(self, eng, fn, inc=None, inc_amt=1):
        if inc is None:
            self.ops[eng].append(fn)
            return None
        self.cnt[inc] += inc_amt
        val = self.cnt[inc]

        def run(e, fn=fn, s=inc, a=inc_amt):
            fn(e).then_inc(self.sem[s], a)

        self.ops[eng].append(run)
        return (inc, val)

    def wait_max(self, eng, tokens):
        best = {}
        for t in tokens:
            if t is None:
                continue
            s, v = t
            best[s] = max(best.get(s, 0), v)
        for s, v in best.items():
            self.wait(eng, (s, v))

    def wait(self, eng, token):
        if token is None:
            return
        s, v = token
        if self.waited[eng][s] >= v:
            return
        self.waited[eng][s] = v
        self.ops[eng].append(lambda e, s=s, v=v: e.wait_ge(self.sem[s], v))

    def drain(self, eng):
        self.ops[eng].append(lambda e: e.drain())

    def dma(self, q, out, in_, stream=None):
        name = self.dma_sem("dq_" + (stream or q), q)
        return self.emit(q, lambda e, o=out, i=in_: e.dma_start(out=o, in_=i),
                         inc=name, inc_amt=16)

    def tick(self, eng, fn):
        return self.emit(eng, fn, inc="bar_" + eng)

    def mark(self, eng):
        self.drain(eng)
        return self.tick(eng, lambda e: e.nop())

    def barrier(self, note=""):
        for name, q in self.dma_sems.items():
            self.wait(q, (name, self.cnt[name]))
        toks = {}
        for e in ENGINES:
            self.drain(e)
            toks[e] = self.emit(e, lambda x: x.nop(), inc="bar_" + e)
        for e in ENGINES:
            for o in ENGINES:
                if o != e:
                    self.wait(e, toks[o])

    def finalize(self, block):
        for e in ENGINES:
            lst = self.ops[e]
            body = getattr(block, e)

            def make(lst):
                def _(eng):
                    for fn in lst:
                        fn(eng)
                return _
            body(make(lst))


def ts(i, w):
    return slice(i * w, (i + 1) * w)


def build_program(T=2048, stages=3):
    NT = T // 128
    QW = min(512, T)
    NQT = T // QW
    NKT = T // 128
    NQ = max(1, T // 512)
    QTOK = T // NQ
    TPQ = QTOK // 128
    NHP = H // 2

    nc = bass.Bass("TRN2", target_bir_lowering=False)

    x_in = nc.dram_tensor("x_in", [T, D], F32, kind="ExternalInput").ap()
    ema_in = nc.dram_tensor("ema_in", [1, D], F32, kind="ExternalInput").ap()
    scal_in = nc.dram_tensor("scal_in", [1, 4], F32, kind="ExternalInput").ap()
    wq_in = nc.dram_tensor("wq_in", [D, D], F32, kind="ExternalInput").ap()
    wkc_in = nc.dram_tensor("wkc_in", [D, DC], F32, kind="ExternalInput").ap()
    wvc_in = nc.dram_tensor("wvc_in", [D, DC], F32, kind="ExternalInput").ap()
    wk_in = nc.dram_tensor("wk_in", [DC, D], F32, kind="ExternalInput").ap()
    wv_in = nc.dram_tensor("wv_in", [DC, D], F32, kind="ExternalInput").ap()
    wo_in = nc.dram_tensor("wo_in", [D, D], F32, kind="ExternalInput").ap()
    wsi_in = nc.dram_tensor("wsi_in", [D, 2 * HID], F32, kind="ExternalInput").ap()
    wso_in = nc.dram_tensor("wso_in", [HID, D], F32, kind="ExternalInput").ap()
    w1_in = nc.dram_tensor("w1_in", [D, 2 * HID], F32, kind="ExternalInput").ap()
    w2_in = nc.dram_tensor("w2_in", [NEXP, HID, D], F32, kind="ExternalInput").ap()
    wgate_in = nc.dram_tensor("wgate_in", [D, 16], F32, kind="ExternalInput").ap()
    gbias_in = nc.dram_tensor("gbias_in", [1, 16], F32, kind="ExternalInput").ap()
    n1w_in = nc.dram_tensor("n1w_in", [1, D], F32, kind="ExternalInput").ap()
    n2w_in = nc.dram_tensor("n2w_in", [1, D], F32, kind="ExternalInput").ap()
    iden_in = nc.dram_tensor("iden_in", [128, 128], F32, kind="ExternalInput").ap()
    trig_in = nc.dram_tensor("trig_in", [128, 2 * NT * 32], F32,
                             kind="ExternalInput").ap()
    mask_in = nc.dram_tensor("mask_in", [128, 4 * QW], F32, kind="ExternalInput").ap()
    out_t = nc.dram_tensor("out_t", [T, D], F32, kind="ExternalOutput").ap()

    sp_x = nc.dram_tensor("sp_x", [T, D], F32).ap()
    sp_y = nc.dram_tensor("sp_y", [T, D], F32).ap()
    sp_flat = nc.dram_tensor("sp_flat", [T, D], F32).ap()
    sp_g = nc.dram_tensor("sp_g", [T, D], F32).ap()
    sp_q = {s: nc.dram_tensor(f"sp_q_{s}", [NHP, 128, T], BF16).ap() for s in "hl"}
    sp_k = {s: nc.dram_tensor(f"sp_k_{s}", [NHP, 128, T], BF16).ap() for s in "hl"}
    sp_sa = {s: nc.dram_tensor(f"sp_sa_{s}", [NHP, 128, T], BF16).ap() for s in "hl"}
    sp_v = nc.dram_tensor("sp_v", [NHP, T, 130], F32).ap()
    sp_fm = {s: nc.dram_tensor(f"sp_fm_{s}", [8, 128, T], BF16).ap() for s in "hl"}
    bounce = nc.dram_tensor("bounce", [1, 4096], F32).ap()
    wc = {}
    for nm, w in (("wq", 8 * D), ("wkc", 2 * D), ("wvc", 2 * D), ("wk", 2 * D),
                  ("wv", 2 * D), ("wo", 8 * D), ("wgate", 128)):
        wc[nm] = {s: nc.dram_tensor(f"wc_{nm}_{s}", [128, w], BF16).ap() for s in "hl"}
    wcm = {"wq": nc.dram_tensor("wcm_wq", [8, 128, 2048], BF16).ap(),
           "wsi": nc.dram_tensor("wcm_wsi", [32, 128, 2048], BF16).ap(),
           "w1": nc.dram_tensor("wcm_w1", [32, 128, 2048], BF16).ap(),
           "wso": nc.dram_tensor("wcm_wso", [16, 128, 2048], BF16).ap(),
           "w2": nc.dram_tensor("wcm_w2", [NEXP * 16, 128, 2048], BF16).ap()}

    es = ExitStack()
    p = Prog(nc)
    p.make_sems(es)

    ACCQ = 0
    TA = ACCQ + 4096
    TB = TA + 1024
    TC = TB + 1024
    SS = TC + 1024
    TRIG = SS + 8448
    MASK = TRIG + 2 * NT * 32
    BCN = MASK + 4 * QW
    MISC = BCN + 3072
    AF32W = MISC + 1792
    af = es.enter_context(nc.sbuf_tensor("af", [128, AF32W], F32))

    BFM = 0
    BSH1 = 16384
    BSH2 = 32768
    BST = 49152
    ABFW = BST + 8192
    ab = es.enter_context(nc.sbuf_tensor("ab", [128, ABFW], BF16))

    ps = [es.enter_context(nc.psum_tensor(f"ps{i}", [128, 512], F32))
          for i in range(8)]

    def fap(off, pattern):
        return bass.AP(af, off, [[AF32W, 128]] + pattern)

    def bap(off, pattern, npart=128):
        return bass.AP(ab, off, [[ABFW, npart]] + pattern)

    def tA(n=1024):
        return fap(TA, [[1, n]])

    def tB(n=1024):
        return fap(TB, [[1, n]])

    def tC(n=1024):
        return fap(TC, [[1, n]])

    def accq(ti, n=1024, off=0):
        return fap(ACCQ + ti * 1024 + off, [[1, n]])

    trig_ap = fap(TRIG, [[1, 2 * NT * 32]])
    mask_full = fap(MASK, [[1, 4 * QW]])

    def mask_ap(mi):
        return fap(MASK + mi * QW, [[1, QW]])

    n1w_bc = fap(BCN, [[1, D]])
    n2w_bc = fap(BCN + 1024, [[1, D]])
    ema_bc = fap(BCN + 2048, [[1, D]])
    COLV = MISC
    WGT = MISC + 32
    SEQS = MISC + 160
    EMAS = MISC + 176
    INVN = MISC + 192
    SCBC = MISC + 208
    GBBC = MISC + 216
    ROWS = MISC + 232

    def colv(a, b):
        return fap(COLV + a, [[1, b - a]])

    def row(off, n, parts=1):
        return bass.AP(af, ROWS + off, [[AF32W, parts], [1, n]])

    CPOOL = ROWS + 1200
    _cvals = [0.0, float(EPS_LN), float(EPS_RMS)]
    for _i, _v in enumerate(_cvals):
        _ap = fap(CPOOL + _i, [[1, 1]])
        p.emit("vector", lambda e, a=_ap, v=_v: e.memset(a, v))
        nc.const_aps.register(F32, _v, _ap)
    p.drain("vector")
    _tcp = p.mark("vector")
    for _e in ("scalar", "tensor", "gpsimd", "sync"):
        p.wait(_e, _tcp)

    def split_pair(eng, dst_h, dst_l, src_ap, hi_scalar=False):
        if hi_scalar:
            th = p.tick("scalar", lambda e, a=dst_h, b=src_ap:
                        e.activation(out=a, in_=b, func=AF.Copy))
            p.wait(eng, th)
            return p.tick(eng, lambda e, a=dst_l, b=src_ap, c=dst_h:
                          e.tensor_tensor(out=a, in0=b, in1=c, op=OP.subtract))
        p.tick(eng, lambda e, a=dst_h, b=src_ap: e.tensor_copy(out=a, in_=b))
        p.drain(eng)
        return p.tick(eng, lambda e, a=dst_l, b=src_ap, c=dst_h:
                      e.tensor_tensor(out=a, in0=b, in1=c, op=OP.subtract))

    def mm3(psum_ap, lh, ll, rh, rl, start, stop, tile_position=None, inc=False):
        p.emit("tensor", lambda e, o=psum_ap, a=lh, b=rh, tp=tile_position:
               e.matmul(out=o, lhsT=a, rhs=b, start=start, stop=False,
                        tile_position=tp))
        p.emit("tensor", lambda e, o=psum_ap, a=lh, b=rl, tp=tile_position:
               e.matmul(out=o, lhsT=a, rhs=b, start=False, stop=False,
                        tile_position=tp))
        return p.emit("tensor", lambda e, o=psum_ap, a=ll, b=rh, tp=tile_position:
                      e.matmul(out=o, lhsT=a, rhs=b, start=False, stop=stop,
                               tile_position=tp),
                      inc="bar_tensor" if inc else None)

    def ln_tile(src_ap, dst_ap, sq=None, cv=None):
        sq = tB(D) if sq is None else sq
        cb = COLV if cv is None else cv

        def c(a, b):
            return fap(cb + a, [[1, b - a]])

        p.drain("vector")
        p.drain("scalar")
        t1 = p.tick("vector", lambda e: e.reduce_sum(out=c(0, 1), in_=src_ap,
                                                     axis=AX.X))
        t2 = p.tick("scalar", lambda e: e.activation(out=sq, in_=src_ap,
                                                     func=AF.Square,
                                                     accum_out=c(1, 2)))
        p.wait("vector", t2)
        p.drain("vector")
        p.tick("vector", lambda e: e.tensor_scalar(out=c(2, 3), in0=c(0, 1),
                                                   scalar1=1.0 / D, scalar2=None,
                                                   op0=OP.mult))
        p.drain("vector")
        p.tick("vector", lambda e: e.tensor_tensor(out=c(3, 4), in0=c(2, 3),
                                                   in1=c(2, 3), op=OP.mult))
        p.drain("vector")
        t3 = p.tick("vector", lambda e: e.tensor_scalar(out=c(4, 5), in0=c(1, 2),
                                                        scalar1=1.0 / D,
                                                        scalar2=c(3, 4),
                                                        op0=OP.mult, op1=OP.subtract))
        p.wait("scalar", t3)
        t4 = p.tick("scalar", lambda e: e.activation(out=c(5, 6), in_=c(4, 5),
                                                     func=AF.Sqrt, bias=float(EPS_LN)))
        p.wait("vector", t4)
        p.tick("vector", lambda e: e.reciprocal(out=c(6, 7), in_=c(5, 6)))
        p.drain("vector")
        t5 = p.tick("vector", lambda e, s=src_ap, d=dst_ap:
                    e.tensor_scalar(out=d, in0=s, scalar1=c(2, 3),
                                    scalar2=c(6, 7),
                                    op0=OP.subtract, op1=OP.mult))
        p.drain("vector")
        return t5

    def rms_tile(src_ap, dst_ap, w_bc, sq=None, cv=None):
        sq = tB(D) if sq is None else sq
        cb = COLV if cv is None else cv

        def c(a, b):
            return fap(cb + a, [[1, b - a]])

        p.drain("scalar")
        t2 = p.tick("scalar", lambda e, s=src_ap: e.activation(
            out=sq, in_=s, func=AF.Square, accum_out=c(1, 2)))
        p.wait("scalar", t2)
        t4 = p.tick("scalar", lambda e: e.activation(
            out=c(5, 6), in_=c(1, 2), func=AF.Sqrt,
            scale=1.0 / D, bias=float(EPS_RMS)))
        p.wait("vector", t4)
        p.tick("vector", lambda e: e.reciprocal(out=c(6, 7), in_=c(5, 6)))
        p.drain("vector")
        p.tick("vector", lambda e, s=src_ap: e.tensor_scalar(
            out=sq, in0=s, scalar1=c(6, 7), scalar2=None, op0=OP.mult))
        p.drain("vector")
        t = p.tick("vector", lambda e, d=dst_ap, w=w_bc, s2=sq: e.tensor_tensor(
            out=d, in0=s2, in1=w, op=OP.mult))
        p.drain("vector")
        return t

    iden_sb = es.enter_context(nc.sbuf_tensor("iden_sb", [128, 128], F32))

    def transpose_split(src_f32, nblk, dst_fn, banks=(4, 5, 6, 7), guard=None):
        nb = len(banks)
        tvs = {}
        last = None
        if guard is not None:
            p.wait("tensor", guard)
        for c in range(nblk):
            bank = ps[banks[c % nb]]
            if c - nb >= 0:
                p.wait("tensor", tvs[c - nb])
            ttok = p.tick("tensor", lambda e, s=src_f32, c=c, b=bank:
                          e.transpose(out=b[:, 0:128],
                                      in_=bass.AP(s.tensor, s.offset + c * 128,
                                                  [[AF32W, 128], [1, 128]]),
                                      identity=iden_sb[:]))
            p.wait("vector", ttok)
            p.wait("scalar", ttok)
            dh, dl = dst_fn(c)
            tv = split_pair("vector", dh, dl, bank[:, 0:128], hi_scalar=True)
            tvs[c] = tv
            last = tv
        p.wait("tensor", last)
        return last

    stage_guard = [None]

    def load_warena(nm, in_ap, base, kchunks, N, first):
        W = kchunks * N
        if first:
            step = max(1, 2048 // N) if N <= 2048 else 1
            prev = stage_guard[0]
            for c0 in range(0, kchunks, step):
                cn = min(step, kchunks - c0)
                stg_ap = fap(SS, [[1, cn * N]])
                stg3 = fap(SS, [[N, cn], [1, N]])
                if prev is not None:
                    p.wait("sync", prev)
                tok = p.dma("sync", stg3,
                            in_ap.rearrange("(c p) n -> p c n", p=128)[:, c0:c0 + cn, :],
                            stream="wst")
                p.wait("vector", tok)
                prev = split_pair("vector",
                                  bap(base + c0 * N, [[1, cn * N]]),
                                  bap(base + W + c0 * N, [[1, cn * N]]),
                                  stg_ap)
            stage_guard[0] = prev
            p.drain("vector")
            tv = p.mark("vector")
            p.wait("gpsimd", tv)
            t1 = p.dma("gpsimd", wc[nm]["h"][:, :], bap(base, [[1, W]]))
            t2 = p.dma("gpsimd", wc[nm]["l"][:, :], bap(base + W, [[1, W]]))
            return (tv, t2)
        th = p.dma("sync", bap(base, [[1, W]]), wc[nm]["h"][:, :])
        tl = p.dma("sync", bap(base + W, [[1, W]]), wc[nm]["l"][:, :])
        return (tl, tl)

    cache_guard = {}

    def load_slice(nm, in_ap, j, slot_base, first, kind, stream="wsl"):
        if kind == "w2e":
            e_i, jj = j
            rec = wcm["w2"][e_i * 16 + jj, :, :]
            srcw = w2_in[e_i, ts(jj, 128), :]
            colmode = False
        elif kind == "row":
            rec = wcm[nm][j, :, :]
            srcw = in_ap[ts(j, 128), :]
            colmode = False
        else:
            rec = wcm[nm][j, :, :]
            srcw = in_ap.rearrange("(c p) n -> p c n", p=128)[:, :, ts(j, 128)]
            colmode = True
        if first:
            stg_ap = fap(SS, [[1, 1024]])
            if stage_guard[0] is not None:
                p.wait("sync", stage_guard[0])
            tok = p.dma("sync", fap(SS, [[128, 8], [1, 128]]) if colmode else stg_ap,
                        srcw, stream="wst")
            p.wait("vector", tok)
            if cache_guard.get(slot_base) is not None:
                p.wait("vector", cache_guard[slot_base])
            p.drain("vector")
            p.tick("vector", lambda e: e.tensor_copy(
                out=bap(slot_base, [[1, 1024]]), in_=stg_ap))
            p.drain("vector")
            tg = p.tick("vector", lambda e: e.tensor_tensor(
                out=bap(slot_base + 1024, [[1, 1024]]), in0=stg_ap,
                in1=bap(slot_base, [[1, 1024]]), op=OP.subtract))
            p.wait("gpsimd", tg)
            tl = p.dma("gpsimd", rec, bap(slot_base, [[1, 2048]]),
                       stream="wc_" + stream)
            stage_guard[0] = tg
            cache_guard[slot_base] = tl
            return tg
        if cache_guard.get(slot_base) is not None:
            p.wait("sync", cache_guard[slot_base])
            cache_guard.pop(slot_base, None)
        return p.dma("sync", bap(slot_base, [[1, 2048]]), rec, stream=stream)

    p.dma("sync", iden_sb[:], iden_in[:, :])
    p.dma("sync", trig_ap, trig_in[:, :])
    p.dma("sync", mask_full, mask_in[:, :])
    tok_e = p.dma("sync", row(0, D), ema_in[:, :], stream="lde")
    tok_s = p.dma("sync", row(1024, 4), scal_in[:, :], stream="lds")
    tok_g = p.dma("sync", row(1056, 16), gbias_in[:, :], stream="ldg")
    tok_n1 = p.dma("sync", bass.AP(af, SS, [[AF32W, 1], [1, D]]), n1w_in[:, :], stream="ldn1")
    tok_n2 = p.dma("sync", bass.AP(af, SS + 1024, [[AF32W, 1], [1, D]]),
                   n2w_in[:, :], stream="ldn2")

    def srow_in(i):
        return bass.AP(af, ROWS + 1024 + i, [[AF32W, 1], [1, 1]])

    def sout(i, n=1):
        return bass.AP(af, ROWS + 1104 + i, [[AF32W, 1], [1, n]])

    p.wait("scalar", tok_s)
    p.tick("scalar", lambda e: e.activation(out=sout(5), in_=srow_in(0), func=AF.Exp))
    p.drain("scalar")
    p.tick("scalar", lambda e: e.activation(out=sout(1), in_=srow_in(1),
                                            func=AF.Sigmoid))
    p.drain("scalar")
    t_sc = p.tick("scalar", lambda e: e.activation(out=sout(3), in_=srow_in(2),
                                                   func=AF.Sigmoid))
    p.wait("vector", t_sc)
    p.tick("vector", lambda e: e.tensor_scalar(out=sout(0), in0=sout(5),
                                               scalar1=-1.0, scalar2=None,
                                               op0=OP.mult))
    p.tick("vector", lambda e: e.tensor_scalar(out=sout(2), in0=sout(1),
                                               scalar1=-1.0, scalar2=1.0,
                                               op0=OP.mult, op1=OP.add))
    t_sc2 = p.tick("vector", lambda e: e.tensor_scalar(out=sout(4), in0=sout(3),
                                                       scalar1=-1.0, scalar2=1.0,
                                                       op0=OP.mult, op1=OP.add))
    p.wait("scalar", tok_e)
    t_e1 = p.tick("scalar", lambda e: e.activation(out=bass.AP(af, SS + 2048,
                                                               [[AF32W, 1], [1, D]]),
                                                   in_=row(0, D), func=AF.Square,
                                                   accum_out=sout(8)))
    p.wait("scalar", t_e1)
    t_e2 = p.tick("scalar", lambda e: e.activation(out=sout(9), in_=sout(8),
                                                   func=AF.Sqrt))
    p.wait("vector", t_e2)
    p.tick("vector", lambda e: e.tensor_scalar(out=sout(10), in0=sout(9),
                                               scalar1=1e-12, scalar2=None,
                                               op0=OP.max))
    p.drain("vector")
    p.tick("vector", lambda e: e.reciprocal(out=sout(11), in_=sout(10)))
    p.drain("vector")
    t_e3 = p.tick("vector", lambda e: e.tensor_scalar(out=row(0, D), in0=row(0, D),
                                                      scalar1=sout(11),
                                                      scalar2=None, op0=OP.mult))
    p.wait("gpsimd", t_e3)
    tb1 = p.dma("gpsimd", bounce[0:1, 0:D], row(0, D))
    p.wait("gpsimd", tok_n1)
    tb2 = p.dma("gpsimd", bounce[0:1, D:2 * D],
                bass.AP(af, SS, [[AF32W, 1], [1, D]]))
    p.wait("gpsimd", tok_n2)
    tb3 = p.dma("gpsimd", bounce[0:1, 2 * D:3 * D],
                bass.AP(af, SS + 1024, [[AF32W, 1], [1, D]]))
    p.wait("gpsimd", t_sc2)
    p.wait("gpsimd", tok_g)
    tb4 = p.dma("gpsimd", bounce[0:1, 3 * D:3 * D + 5], sout(0, 5))
    tb5 = p.dma("gpsimd", bounce[0:1, 3 * D + 16:3 * D + 32], row(1056, 16))
    p.wait("gpsimd", tb5)
    p.dma("gpsimd", ema_bc, bass.AP(bounce.tensor, 0, [[0, 128], [1, D]]))
    p.dma("gpsimd", n1w_bc, bass.AP(bounce.tensor, D, [[0, 128], [1, D]]))
    p.dma("gpsimd", n2w_bc, bass.AP(bounce.tensor, 2 * D, [[0, 128], [1, D]]))
    p.dma("gpsimd", fap(SCBC, [[1, 5]]),
          bass.AP(bounce.tensor, 3 * D, [[0, 128], [1, 5]]))
    p.dma("gpsimd", fap(GBBC, [[1, 16]]),
          bass.AP(bounce.tensor, 3 * D + 16, [[0, 128], [1, 16]]))
    p.barrier("consts")

    scbc = fap(SCBC, [[1, 5]])

    def scb(i):
        return fap(SCBC + i, [[1, 1]])

    gbbc = fap(GBBC, [[1, 16]])

    def ffn_call(call, src_x, dst_x, gated=False):
        first = (call == 0)

        load_warena("wkc", wkc_in, BSH2, 8, DC, first)
        load_warena("wvc", wvc_in, BSH2 + 4096, 8, DC, first)
        load_warena("wk", wk_in, BSH2 + 8192, 2, D, first)
        load_warena("wv", wv_in, BSH2 + 12288, 2, D, first)
        p.barrier(f"aw_{call}")
        for q in range(NQ):
            firstq = first and q == 0
            prev_ts = [None]
            tsv = {}
            XBS = (TA, TC, SS, SS + 1024)
            SQS = (TB, SS + 2080, SS + 3104, SS + 4224)
            CVS = (COLV, COLV + 8, COLV + 16, COLV + 24)
            GT = MISC + 1768

            def c_(i, a, b):
                return fap(CVS[i] + a, [[1, b - a]])

            def xv(i, n=D):
                return fap(XBS[i], [[1, n]])

            def sv(i, n=D):
                return fap(SQS[i], [[1, n]])

            def gt(a, b):
                return fap(GT + a, [[1, b - a]])

            ltok = {}
            gtok = {}
            for i in range(TPQ):
                tt = q * TPQ + i
                if gated:
                    gtok[i] = p.dma("sync", sv(i), sp_g[ts(tt, 128), :],
                                    stream=f"ldg{i}")
                ltok[i] = p.dma("sync", xv(i), src_x[ts(tt, 128), :],
                                stream=f"ldx{i}")
            if gated:
                p.drain("vector")
                p.tick("vector", lambda e, q=q: e.tensor_scalar(
                    out=gt(0, TPQ), in0=fap(EMAS + q * TPQ, [[1, TPQ]]),
                    scalar1=scb(4), scalar2=None, op0=OP.mult))
                p.drain("vector")
                tg = p.tick("vector", lambda e, q=q: e.scalar_tensor_tensor(
                    out=gt(4, 4 + TPQ), in0=fap(SEQS + q * TPQ, [[1, TPQ]]),
                    scalar=scb(3), in1=gt(0, TPQ), op0=OP.mult, op1=OP.add))
                p.wait("scalar", tg)
                p.drain("scalar")
                tg = p.tick("scalar", lambda e: e.activation(
                    out=gt(8, 8 + TPQ), in_=gt(4, 4 + TPQ), func=AF.Exp,
                    scale=scb(0)))
                p.wait("vector", tg)
                p.tick("vector", lambda e: e.scalar_tensor_tensor(
                    out=gt(12, 12 + TPQ), in0=gt(8, 8 + TPQ), scalar=scb(1),
                    in1=bass.AP(af, SCBC + 2, [[AF32W, 128], [0, TPQ]]),
                    op0=OP.mult, op1=OP.add))
                p.drain("vector")
                for i in range(TPQ):
                    p.wait("vector", gtok[i])
                    p.tick("vector", lambda e, i=i: e.tensor_scalar(
                        out=sv(i), in0=sv(i), scalar1=fap(GT + 12 + i, [[1, 1]]),
                        scalar2=None, op0=OP.mult))
                p.drain("vector")
                tres = None
                sp2 = {}
                for i in range(TPQ):
                    tt = q * TPQ + i
                    p.wait("vector", ltok[i])
                    tres = p.tick("vector", lambda e, i=i: e.tensor_tensor(
                        out=xv(i), in0=xv(i), in1=sv(i), op=OP.add))
                    p.drain("vector")
                    p.wait("gpsimd", tres)
                    sp2[i] = p.dma("gpsimd", sp_x[ts(tt, 128), :], xv(i),
                                   stream=f"spx2{i}")
                for i in range(TPQ):
                    p.wait("vector", sp2[i])
                p.wait("scalar", tres)
            p.drain("vector")
            p.drain("scalar")
            for i in range(TPQ):
                p.wait("vector", ltok[i])
                p.wait("scalar", ltok[i])
                p.tick("vector", lambda e, i=i: e.reduce_sum(
                    out=c_(i, 0, 1), in_=xv(i), axis=AX.X))
                p.tick("scalar", lambda e, i=i: e.activation(
                    out=sv(i), in_=xv(i), func=AF.Square, accum_out=c_(i, 1, 2)))
            tS = p.mark("scalar")
            p.wait("vector", tS)
            p.drain("vector")
            for i in range(TPQ):
                p.tick("vector", lambda e, i=i: e.tensor_scalar(
                    out=c_(i, 2, 3), in0=c_(i, 0, 1), scalar1=1.0 / D,
                    scalar2=None, op0=OP.mult))
            p.drain("vector")
            for i in range(TPQ):
                p.tick("vector", lambda e, i=i: e.tensor_tensor(
                    out=c_(i, 3, 4), in0=c_(i, 2, 3), in1=c_(i, 2, 3),
                    op=OP.mult))
            p.drain("vector")
            tV = None
            for i in range(TPQ):
                tV = p.tick("vector", lambda e, i=i: e.tensor_scalar(
                    out=c_(i, 4, 5), in0=c_(i, 1, 2), scalar1=1.0 / D,
                    scalar2=c_(i, 3, 4), op0=OP.mult, op1=OP.subtract))
            p.wait("scalar", tV)
            p.drain("scalar")
            tS = None
            for i in range(TPQ):
                tS = p.tick("scalar", lambda e, i=i: e.activation(
                    out=c_(i, 5, 6), in_=c_(i, 4, 5), func=AF.Sqrt,
                    bias=float(EPS_LN)))
            p.wait("vector", tS)
            p.drain("vector")
            for i in range(TPQ):
                p.tick("vector", lambda e, i=i: e.reciprocal(
                    out=c_(i, 6, 7), in_=c_(i, 5, 6)))
            p.drain("vector")
            for i in range(TPQ):
                tt = q * TPQ + i
                t5 = p.tick("vector", lambda e, i=i: e.tensor_scalar(
                    out=xv(i), in0=xv(i), scalar1=c_(i, 2, 3), scalar2=c_(i, 6, 7),
                    op0=OP.subtract, op1=OP.mult))
                p.drain("vector")
                p.wait("gpsimd", t5)
                p.dma("gpsimd", sp_y[ts(tt, 128), :], xv(i), stream=f"spy{i % 2}")
                p.wait("tensor", t5)
                prev_ts[0] = transpose_split(
                    fap(XBS[i], [[1, 1024]]), 8,
                    lambda c, ti=i: (
                        bap(BFM + c * 2 * QTOK + ti * 128, [[1, 128]]),
                        bap(BFM + c * 2 * QTOK + QTOK + ti * 128, [[1, 128]])),
                    guard=prev_ts[0])
                tsv[i] = prev_ts[0]
            p.barrier(f"a1_{call}_{q}")
            def fmh(c, w0, wn):
                return bap(BFM + c * 2 * QTOK + w0, [[1, wn]])

            def fml(c, w0, wn):
                return bap(BFM + c * 2 * QTOK + QTOK + w0, [[1, wn]])

            NW = QTOK // 512 if QTOK >= 512 else 1
            W5 = min(512, QTOK)
            for (wb, vb) in ((BSH2, BSH1), (BSH2 + 4096, BSH1 + 4 * QTOK)):
                for dc in range(2):
                    for w in range(NW):
                        bank = ps[dc * NW + w]
                        tokmm = None
                        for kc in range(8):
                            tokmm = mm3(bank[:, 0:W5],
                                        bap(wb + kc * DC + ts(dc, 128).start, [[1, 128]]),
                                        bap(wb + 8 * DC + kc * DC + dc * 128, [[1, 128]]),
                                        fmh(kc, w * W5, W5), fml(kc, w * W5, W5),
                                        start=(kc == 0), stop=(kc == 7),
                                        inc=(kc == 7))
                        tmm = tokmm
                        p.wait("vector", tmm)
                        p.wait("scalar", tmm)
                        tv = split_pair("vector",
                                        bap(vb + dc * QTOK + w * W5, [[1, W5]]),
                                        bap(vb + 2 * QTOK + dc * QTOK + w * W5,
                                            [[1, W5]]),
                                        bank[:, 0:W5], hi_scalar=True)
                        p.wait("tensor", tv)

            def kch(dc, w0, wn):
                return bap(BSH1 + dc * QTOK + w0, [[1, wn]])

            def kcl(dc, w0, wn):
                return bap(BSH1 + 2 * QTOK + dc * QTOK + w0, [[1, wn]])

            def vch(dc, w0, wn):
                return bap(BSH1 + 4 * QTOK + dc * QTOK + w0, [[1, wn]])

            def vcl(dc, w0, wn):
                return bap(BSH1 + 6 * QTOK + dc * QTOK + w0, [[1, wn]])

            WVB = BSH2 + 12288
            for bi in range(2):
                p.tick("vector", lambda e, bi=bi: e.memset(
                    bass.AP(af, SS + bi * 1040 + 64, [[AF32W, 128], [65, 16]]),
                    1.0))
            p.drain("vector")
            tvz = p.mark("vector")
            p.wait("tensor", tvz)
            spv_last = {}
            for ti in range(TPQ):
                tt = q * TPQ + ti
                vboff = SS + (ti % 2) * 1040
                if ti - 2 >= 0:
                    p.wait("vector", spv_last[ti - 2])
                for half in range(2):
                    bank = ps[half]
                    tokmm = None
                    for dc in range(2):
                        tokmm = mm3(bank[:, 0:512],
                                    vch(dc, ti * 128, 128), vcl(dc, ti * 128, 128),
                                    bap(WVB + dc * D + half * 512, [[1, 512]]),
                                    bap(WVB + 2 * D + dc * D + half * 512, [[1, 512]]),
                                    start=(dc == 0), stop=(dc == 1),
                                    inc=(dc == 1))
                    tmm = tokmm
                    p.wait("vector", tmm)
                    tv = p.tick("vector", lambda e, b=bank, o=vboff, hf=half:
                                e.tensor_copy(
                                    out=bass.AP(af, o + hf * 8 * 65,
                                                [[AF32W, 128], [65, 8], [1, 64]]),
                                    in_=b[:, 0:512].rearrange("p (h d) -> p h d", h=8)))
                    p.wait("tensor", tv)
                p.drain("vector")
                tv2 = p.mark("vector")
                p.wait("gpsimd", tv2)
                for hp in range(NHP):
                    spv_last[ti] = p.dma(
                        "gpsimd", sp_v[hp, ts(tt, 128), :],
                        bass.AP(af, vboff + hp * 130, [[AF32W, 128], [1, 130]]),
                        stream=f"spv{ti % 2}")

            WKB = BSH2 + 8192

            def qk_post(sp_dst, tag, tmm, prev_spl=None, bar=True):
                p.wait("vector", tmm)
                raw = SS + 2080
                for ti in range(TPQ):
                    for half in range(2):
                        bank = ps[ti * 2 + half]
                        p.tick("vector", lambda e, b=bank, ti=ti, hf=half:
                               e.tensor_copy(out=bass.AP(af,
                                                         raw + ti * 1024 + hf * 512,
                                                         [[AF32W, 128], [1, 512]]),
                                             in_=b[:, 0:512]))
                p.drain("vector")
                tv = p.mark("vector")
                p.wait("tensor", tv)
                spl_tok = {}
                prev_ts = [None]
                for ti in range(TPQ):
                    tt = q * TPQ + ti
                    par = ti % 2
                    robase = TB if par == 0 else TC
                    stg_b = BSH1 + 8 * QTOK + par * 2048
                    if ti - 2 >= 0 and (ti - 2) in spl_tok:
                        p.wait("vector", spl_tok[ti - 2])
                        p.wait("scalar", spl_tok[ti - 2])
                    elif prev_spl is not None and par in prev_spl:
                        p.wait("vector", prev_spl[par])
                        p.wait("scalar", prev_spl[par])
                    trp = rope_tile(raw + ti * 1024, tt, out_off=robase)
                    p.wait("tensor", trp)
                    prev_ts[0] = transpose_split(
                        fap(robase, [[1, 1024]]), 8,
                        lambda c, sb=stg_b: (bap(sb + c * 128, [[1, 128]]),
                                             bap(sb + 1024 + c * 128, [[1, 128]])),
                        guard=prev_ts[0])
                    p.drain("vector")
                    tvx = p.mark("vector")
                    p.wait("gpsimd", tvx)
                    tsp = None
                    for c in range(8):
                        p.dma("gpsimd", sp_dst["h"][c, :, ts(tt, 128)],
                              bap(stg_b + c * 128, [[1, 128]]), stream=f"spqk{par}")
                        tsp = p.dma("gpsimd", sp_dst["l"][c, :, ts(tt, 128)],
                                    bap(stg_b + 1024 + c * 128, [[1, 128]]),
                                    stream=f"spqk{par}")
                    spl_tok[ti] = tsp
                if bar:
                    p.barrier(f"qk_{tag}_{call}_{q}")
                return {ti % 2: t for ti, t in spl_tok.items()}

            qk_last = [None]
            for kc in range(2):
                for ti in range(TPQ):
                    for half in range(2):
                        bank = ps[ti * 2 + half]
                        last = (kc == 1 and ti == TPQ - 1 and half == 1)
                        qk_last[0] = mm3(bank[:, 0:512],
                            kch(kc, ti * 128, 128), kcl(kc, ti * 128, 128),
                            bap(WKB + kc * D + half * 512, [[1, 512]]),
                            bap(WKB + 2 * D + kc * D + half * 512, [[1, 512]]),
                            start=(kc == 0), stop=(kc == 1), inc=last)
            spk_tok = qk_post(sp_k, "k", qk_last[0], bar=False)

            wq_tok = {}

            def load_wq(kc):
                wq_tok[kc] = load_slice("wq", wq_in, kc, BST + (kc % 4) * 2048,
                                        firstq, "row", stream=f"wq{kc % 4}")

            if firstq:
                p.wait_max("sync", [spv_last[ti] for ti in spv_last])
            load_wq(0)
            load_wq(1)
            load_wq(2)
            load_wq(3)
            for kc in range(8):
                p.wait("tensor", wq_tok[kc])
                for ti in range(TPQ):
                    for half in range(2):
                        bank = ps[ti * 2 + half]
                        last = (ti == TPQ - 1 and half == 1)
                        qk_last[0] = mm3(bank[:, 0:512],
                            fmh(kc, ti * 128, 128), fml(kc, ti * 128, 128),
                            bap(BST + (kc % 4) * 2048 + half * 512, [[1, 512]]),
                            bap(BST + (kc % 4) * 2048 + 1024 + half * 512,
                                [[1, 512]]),
                            start=(kc == 0), stop=(kc == 7), inc=last)
                if kc < 7:
                    tmk = qk_last[0]
                    nxt = kc + 4
                    if nxt < 8:
                        p.wait("sync", tmk)
                        if firstq:
                            p.wait("vector", tmk)
                        load_wq(nxt)
            qk_post(sp_q, "q", qk_last[0], prev_spl=spk_tok)

        attn_phase()
        p.barrier(f"attn_{call}")

        t_wo = load_warena("wo", wo_in, BSH2, 8, D, first)
        t_wg = load_warena("wgate", wgate_in, BST, 8, 16, first)
        p.barrier(f"wo_w_{call}")
        for q in range(NQ):
            prev_ts = [None]
            sp_tok = {}
            tsv = {}
            add_tok = {}
            for ti in range(TPQ):
                tt = q * TPQ + ti
                par = ti % 2
                sab = BSH1 + 8 * QTOK + par * 2048
                x1b = TA if par == 0 else SS
                yb = TC if par == 0 else SS + 1024
                sqb = TB if par == 0 else SS + 2080
                cvb = COLV if par == 0 else COLV + 8
                if ti - 2 >= 0:
                    p.wait("sync", tsv[ti - 2])
                    p.wait("sync", sp_tok[ti - 2])
                tok1 = p.dma("sync", bap(sab, [[128, 8], [1, 128]]),
                             sp_sa["h"][:, :, ts(tt, 128)].rearrange("c p j -> p c j"),
                             stream=f"ldsa{par}")
                tok2 = p.dma("sync", bap(sab + 1024, [[128, 8], [1, 128]]),
                             sp_sa["l"][:, :, ts(tt, 128)].rearrange("c p j -> p c j"),
                             stream=f"ldsa{par}")
                toky = p.dma("sync", fap(yb, [[1, D]]), sp_y[ts(tt, 128), :],
                             stream=f"ldy{par}")
                p.wait_max("tensor", [tok1, tok2])
                for half in range(2):
                    bank = ps[half + 2 * par]
                    if ti - 2 >= 0:
                        p.wait("tensor", add_tok[(ti - 2, half)])
                    for kc in range(8):
                        tokmm = mm3(bank[:, 0:512],
                                    bap(sab + kc * 128, [[1, 128]]),
                                    bap(sab + 1024 + kc * 128, [[1, 128]]),
                                    bap(BSH2 + kc * D + half * 512, [[1, 512]]),
                                    bap(BSH2 + 8 * D + kc * D + half * 512, [[1, 512]]),
                                    start=(kc == 0), stop=(kc == 7),
                                    inc=(kc == 7))
                    tmm = tokmm
                    p.wait("vector", tmm)
                    p.wait("vector", toky)
                    add_tok[(ti, half)] = p.tick(
                        "vector", lambda e, b=bank, hf=half, xb=x1b, yb2=yb:
                        e.tensor_tensor(out=fap(xb + hf * 512, [[1, 512]]),
                                        in0=fap(yb2 + hf * 512, [[1, 512]]),
                                        in1=b[:, 0:512], op=OP.add))
                p.drain("vector")
                tvv = p.mark("vector")
                p.wait("scalar", tvv)
                trms = rms_tile(fap(x1b, [[1, D]]), fap(x1b, [[1, D]]), n1w_bc,
                                sq=fap(sqb, [[1, D]]), cv=cvb)
                p.wait("gpsimd", trms)
                sp_tok[ti] = p.dma("gpsimd", sp_flat[ts(tt, 128), :],
                                   fap(x1b, [[1, D]]), stream=f"spfl{par}")
                p.wait("tensor", trms)
                prev_ts[0] = transpose_split(
                    fap(x1b, [[1, 1024]]), 8,
                    lambda c, ti=ti: (
                        bap(BFM + c * 2 * QTOK + ti * 128, [[1, 128]]),
                        bap(BFM + c * 2 * QTOK + QTOK + ti * 128, [[1, 128]])),
                    guard=prev_ts[0])
                tsv[ti] = prev_ts[0]

            def fmh(c, w0, wn):
                return bap(BFM + c * 2 * QTOK + w0, [[1, wn]])

            def fml(c, w0, wn):
                return bap(BFM + c * 2 * QTOK + QTOK + w0, [[1, wn]])

            p.wait("gpsimd", tsv[TPQ - 1])
            for c in range(8):
                p.dma("gpsimd", sp_fm["h"][c, :, ts(q, QTOK)], fmh(c, 0, QTOK))
                p.dma("gpsimd", sp_fm["l"][c, :, ts(q, QTOK)], fml(c, 0, QTOK))
            gating_quarter(q, fmh, fml)
            p.barrier(f"cg_{call}_{q}")

        fm_war = None
        ps_war = None

        def reload_fm(q):
            toks = []
            for c in range(8):
                toks.append(p.dma("sync", bap(BFM + c * 2 * QTOK, [[1, QTOK]]),
                                  sp_fm["h"][c, :, ts(q, QTOK)], stream="fmld"))
                toks.append(p.dma("sync",
                                  bap(BFM + c * 2 * QTOK + QTOK, [[1, QTOK]]),
                                  sp_fm["l"][c, :, ts(q, QTOK)], stream="fmld"))
            return toks

        fm_pre = {}
        for q in range(NQ):
            firstq = first and q == 0
            if q in fm_pre:
                fm_toks = fm_pre.pop(q)
            else:
                if fm_war is not None:
                    p.wait("sync", fm_war)
                fm_toks = reload_fm(q)

            def fmh(c, w0, wn):
                return bap(BFM + c * 2 * QTOK + w0, [[1, wn]])

            def fml(c, w0, wn):
                return bap(BFM + c * 2 * QTOK + QTOK + w0, [[1, wn]])

            wr = wso_ring(firstq)
            if not firstq:
                for _j in range(4):
                    wr[1](_j)
            sw_mm, sw_spl = produce_hidden("wsi", wsi_in, q, fmh, fml, firstq,
                                           fm_guard=fm_toks, psum_guard=ps_war)
            if firstq:
                for _j in range(4):
                    wr[1](_j)
            for _t in sw_spl.values():
                p.wait("tensor", _t)
            ev_tok = consume_hidden_shared(q, firstq, wr)
            fm_war, h_spl = produce_hidden("w1", w1_in, q, fmh, fml, firstq,
                                           ring_guard=sw_mm,
                                           psum_guard=ev_tok)
            if q + 1 < NQ:
                p.wait("sync", fm_war)
                fm_pre[q + 1] = reload_fm(q + 1)
            ps_war = experts_quarter(q, firstq, pre_splits=h_spl,
                                     ring_guard=fm_war)
            sp_tok = {}
            res_tok = {}
            p.drain("vector")
            for ti in range(TPQ):
                tt = q * TPQ + ti
                par = ti % 2
                fb = TA if par == 0 else SS + 4224
                xb = TC if par == 0 else SS + 5248
                sqb = SS + 3104 if par == 0 else SS + 6272
                cvb = COLV + 16 if par == 0 else COLV + 24
                if ti - 2 >= 0:
                    p.wait("sync", sp_tok[ti - 2])
                tokf = p.dma("sync", fap(fb, [[1, D]]), sp_flat[ts(tt, 128), :],
                             stream=f"ldy{par}")
                tokx = p.dma("sync", fap(xb, [[1, D]]), src_x[ts(tt, 128), :],
                             stream=f"ldx{par}")
                p.wait("vector", tokf)
                p.tick("vector", lambda e, ti=ti, fb=fb: e.tensor_tensor(
                    out=fap(fb, [[1, D]]), in0=fap(fb, [[1, D]]), in1=accq(ti),
                    op=OP.add))
                p.drain("vector")
                tvv = p.mark("vector")
                p.wait("scalar", tvv)
                trms = rms_tile(fap(fb, [[1, D]]), fap(fb, [[1, D]]), n2w_bc,
                                sq=fap(sqb, [[1, D]]), cv=cvb)
                p.wait("vector", tokx)
                tres = p.tick("vector", lambda e, fb=fb, xb=xb:
                              e.scalar_tensor_tensor(
                    out=fap(fb, [[1, D]]), in0=fap(fb, [[1, D]]), scalar=0.5,
                    in1=fap(xb, [[1, D]]), op0=OP.mult, op1=OP.add))
                p.drain("vector")
                res_tok[ti] = tres
                p.wait("gpsimd", tres)
                sp_tok[ti] = p.dma("gpsimd", dst_x[ts(tt, 128), :],
                                   fap(fb, [[1, D]]), stream=f"spx{par}")
        p.barrier(f"ffn_{call}")

    def rope_tile(raw_off, tt, out_off=None):
        out_off = TB if out_off is None else out_off
        sc = SS + 2080 + TPQ * 1024
        cosap = bass.AP(af, TRIG + tt * 64, [[AF32W, 128], [0, 16], [1, 32]])
        sinap = bass.AP(af, TRIG + tt * 64 + 32,
                        [[AF32W, 128], [0, 16], [1, 32]])
        ev = bass.AP(af, raw_off, [[AF32W, 128], [64, 16], [2, 32]])
        od = bass.AP(af, raw_off + 1, [[AF32W, 128], [64, 16], [2, 32]])
        top = bass.AP(af, out_off, [[AF32W, 128], [64, 16], [1, 32]])
        bot = bass.AP(af, out_off + 32, [[AF32W, 128], [64, 16], [1, 32]])
        sc1 = bass.AP(af, sc, [[AF32W, 128], [64, 16], [1, 32]])
        sc2 = bass.AP(af, sc + 32, [[AF32W, 128], [64, 16], [1, 32]])
        p.tick("vector", lambda e: e.tensor_tensor(out=sc1, in0=ev, in1=cosap, op=OP.mult))
        p.tick("vector", lambda e: e.tensor_tensor(out=sc2, in0=od, in1=sinap, op=OP.mult))
        p.drain("vector")
        p.tick("vector", lambda e: e.tensor_tensor(out=top, in0=sc1, in1=sc2,
                                                   op=OP.subtract))
        p.drain("vector")
        p.tick("vector", lambda e: e.tensor_tensor(out=sc1, in0=ev, in1=sinap, op=OP.mult))
        p.tick("vector", lambda e: e.tensor_tensor(out=sc2, in0=od, in1=cosap, op=OP.mult))
        p.drain("vector")
        t = p.tick("vector", lambda e: e.tensor_tensor(out=bot, in0=sc1, in1=sc2,
                                                       op=OP.add))
        p.drain("vector")
        return t

    def attn_phase():
        KB = BSH2
        QB = BSH2 + 4 * T
        SAB = QB + 4 * QW
        VS = NKT * 130
        EXPS = SS + 2 * VS
        BCD = EXPS + 6 * QW
        RR = BCD + QW
        tpv = {}
        tfin = {}
        kv_tok = {}
        q_tok = {}

        def load_kv(hp):
            db = hp % 2
            kst = f"k{db}"
            t1 = p.dma("sync", bap(KB + db * T, [[1, T]]), sp_k["h"][hp, :, :],
                       stream=kst)
            t2 = p.dma("sync", bap(KB + 2 * T + db * T, [[1, T]]),
                       sp_k["l"][hp, :, :], stream=kst)
            t3 = p.dma("sync",
                       bass.AP(af, SS + db * VS,
                               [[AF32W, 128], [130, NKT], [1, 130]]),
                       sp_v[hp, :, :].rearrange("(k p) n -> p k n", p=128),
                       stream=f"v{db}")
            kv_tok[hp] = [t1, t2, t3]

        def load_q(hp, qt, qdb):
            qst = f"q{qdb}"
            t1 = p.dma("sync", bap(QB + qdb * QW, [[1, QW]]),
                       sp_q["h"][hp, :, ts(qt, QW)], stream=qst)
            t2 = p.dma("sync", bap(QB + 2 * QW + qdb * QW, [[1, QW]]),
                       sp_q["l"][hp, :, ts(qt, QW)], stream=qst)
            q_tok[(hp, qt)] = [t1, t2]

        load_kv(0)
        load_q(0, 0, 0)
        last_pv_user = {}
        seq = [(hp, qt) for hp in range(NHP) for qt in range(NQT)]
        for idx, (hp, qt) in enumerate(seq):
            db = hp % 2
            qdb = idx % 2
            pvpar = idx % 2
            pvb = (ps[4], ps[5]) if pvpar == 0 else (ps[6], ps[7])
            if idx + 1 < len(seq):
                nhp, nqt = seq[idx + 1]
                if idx >= 1:
                    p.wait("sync", tpv[seq[idx - 1]])
                if nqt == 0:
                    load_kv(nhp)
                load_q(nhp, nqt, (idx + 1) % 2)
            p.wait_max("tensor", kv_tok[hp] + q_tok[(hp, qt)])
            texp = {}

            def scores(kt):
                pa = ps[kt % 2]
                pb = ps[2 + kt % 2]
                if kt >= 2 and (kt - 2) in texp:
                    for t in texp[kt - 2]:
                        p.wait("tensor", t)
                tmm = None
                for (base, bank, tp) in ((0, pa, (0, 0)), (64, pb, (64, 0))):
                    tmm = mm3(bank[:, 0:QW],
                        bass.AP(ab, base * ABFW + KB + db * T + kt * 128,
                                [[ABFW, 64], [1, 128]]),
                        bass.AP(ab, base * ABFW + KB + 2 * T + db * T
                                + kt * 128, [[ABFW, 64], [1, 128]]),
                        bass.AP(ab, base * ABFW + QB + qdb * QW,
                                [[ABFW, 64], [1, QW]]),
                        bass.AP(ab, base * ABFW + QB + 2 * QW + qdb * QW,
                                [[ABFW, 64], [1, QW]]),
                        start=True, stop=True, tile_position=tp,
                        inc=(base == 64))
                p.wait("scalar", tmm)
                if qt >= 1 or (hp, qt) != (0, 0):
                    prev = seq[idx - 1] if kt == 0 else None
                    if prev is not None and prev in tpv:
                        p.wait("scalar", tpv[prev])
                es = []
                for (i, bank) in ((0, pa), (1, pb)):
                    es.append(p.tick("scalar", lambda e, i=i, b=bank, kt=kt:
                                     e.activation(
                                         out=fap(EXPS + (3 * i + kt % 3) * QW,
                                                 [[1, QW]]),
                                         in_=b[:, 0:QW], func=AF.Exp,
                                         scale=0.125)))
                texp[kt] = es

            def pv(kt):
                for t in texp[kt]:
                    p.wait("tensor", t)
                if kt == 0 and last_pv_user.get(pvpar) is not None:
                    p.wait("tensor", tfin[last_pv_user[pvpar]])
                for (i, bank) in ((0, pvb[0]), (1, pvb[1])):
                    voff = SS + db * VS + kt * 130 + i * 65
                    pv_last[0] = p.emit(
                        "tensor", lambda e, b=bank, i=i, kt=kt, voff=voff:
                        e.matmul(out=b[0:65, 0:QW],
                                 lhsT=bass.AP(af, voff,
                                              [[AF32W, 128], [1, 65]]),
                                 rhs=fap(EXPS + (3 * i + kt % 3) * QW,
                                         [[1, QW]]),
                                 start=(kt == 0), stop=(kt == NKT - 1)),
                        inc="bar_tensor" if (kt == NKT - 1 and i == 1) else None)

            pv_last = [None]
            scores(0)
            if NKT > 1:
                scores(1)
            for kt in range(2, NKT):
                scores(kt)
                pv(kt - 2)
            if NKT > 1:
                pv(NKT - 2)
            pv(NKT - 1)
            tpv[(hp, qt)] = pv_last[0]
            p.wait("vector", tpv[(hp, qt)])
            for (i, bank) in ((0, pvb[0]), (1, pvb[1])):
                p.tick("vector", lambda e, b=bank: e.reciprocal(
                    out=bass.AP(af, 64 * AF32W + RR, [[AF32W, 1], [1, QW]]),
                    in_=b[64:65, 0:QW]))
                p.drain("vector")
                tv = p.mark("vector")
                p.wait("gpsimd", tv)
                t1 = p.dma("gpsimd", bounce[0:1, 0:QW],
                           bass.AP(af, 64 * AF32W + RR,
                                   [[AF32W, 1], [1, QW]]), stream="bnc")
                p.wait("gpsimd", t1)
                t2 = p.dma("gpsimd",
                           bass.AP(af, BCD, [[AF32W, 64], [1, QW]]),
                           bass.AP(bounce.tensor, 0, [[0, 64], [1, QW]]),
                           stream="bnc")
                p.wait("vector", t2)
                p.tick("vector", lambda e, b=bank: e.tensor_tensor(
                    out=bass.AP(af, RR, [[AF32W, 64], [1, QW]]),
                    in0=b[0:64, 0:QW],
                    in1=bass.AP(af, BCD, [[AF32W, 64], [1, QW]]),
                    op=OP.mult))
                p.drain("vector")
                tv2 = split_pair("vector",
                                 bass.AP(ab, SAB, [[ABFW, 64], [1, QW]]),
                                 bass.AP(ab, SAB + QW, [[ABFW, 64], [1, QW]]),
                                 bass.AP(af, RR, [[AF32W, 64], [1, QW]]))
                p.wait("gpsimd", tv2)
                rb = 0 if i == 0 else 64
                td1 = p.dma("gpsimd", sp_sa["h"][hp, rb:rb + 64, ts(qt, QW)],
                            bass.AP(ab, SAB, [[ABFW, 64], [1, QW]]), stream="spsa")
                td2 = p.dma("gpsimd", sp_sa["l"][hp, rb:rb + 64, ts(qt, QW)],
                            bass.AP(ab, SAB + QW, [[ABFW, 64], [1, QW]]),
                            stream="spsa")
                p.wait("vector", td2)
            tfin[(hp, qt)] = p.mark("vector")
            last_pv_user[pvpar] = (hp, qt)

    def gating_quarter(q, fmh, fml):
        NW = QTOK // 512 if QTOK >= 512 else 1
        W5 = min(512, QTOK)
        LOG = ACCQ
        for w in range(NW):
            bank = ps[w % 2]
            tokmm = None
            for kc in range(8):
                tokmm = mm3(bank[0:10, 0:W5],
                            bap(BST + kc * 16, [[1, 10]]),
                            bap(BST + 8 * 16 + kc * 16, [[1, 10]]),
                            fmh(kc, w * W5, W5), fml(kc, w * W5, W5),
                            start=(kc == 0), stop=(kc == 7), inc=(kc == 7))
            tmm = tokmm
            p.wait("vector", tmm)
            tv = p.tick("vector", lambda e, b=bank, w=w: e.tensor_copy(
                out=bass.AP(af, LOG + w * W5, [[AF32W, 10], [1, W5]]),
                in_=b[0:10, 0:W5]))
            p.wait("tensor", tv)
        p.drain("vector")
        tvx = p.mark("vector")
        p.wait("tensor", tvx)
        tcop = None
        for ti in range(TPQ):
            bank = ps[2 + ti % 2]
            if ti - 2 >= 0 and tcop is not None:
                p.wait("tensor", tcop)
            ttok = p.tick("tensor", lambda e, ti=ti, b=bank: e.transpose(
                out=b[:, 0:10], in_=bass.AP(af, LOG + ti * 128,
                                            [[AF32W, 10], [1, 128]]),
                identity=iden_sb[0:10, 0:10]))
            p.wait("vector", ttok)
            tcop = p.tick("vector", lambda e, ti=ti, b=bank: e.tensor_copy(
                out=fap(TB + ti * 24, [[1, 10]]), in_=b[:, 0:10]))
        p.drain("vector")
        gate_batch(q)
        p.drain("vector")

    def gate_batch(q):
        NTI = TPQ

        def L(a, w=1):
            return bass.AP(af, TB + a, [[AF32W, 128], [24, NTI], [1, w]])

        def Lb(a, w):
            return bass.AP(af, TB + a, [[AF32W, 128], [24, NTI], [0, w]])

        def vt(fn):
            p.tick("vector", fn)
            p.drain("vector")

        tS0 = p.mark("scalar")
        p.wait("vector", tS0)
        vt(lambda e: e.tensor_tensor(
            out=L(0, 10), in0=L(0, 10),
            in1=bass.AP(af, GBBC, [[AF32W, 128], [0, NTI], [1, 10]]), op=OP.add))
        vt(lambda e: e.tensor_tensor(out=L(10), in0=L(0), in1=L(1), op=OP.max))
        tsub = p.tick("vector", lambda e: e.tensor_tensor(
            out=L(11, 2), in0=L(0, 2), in1=Lb(10, 2), op=OP.subtract))
        p.drain("vector")
        p.wait("scalar", tsub)
        p.drain("scalar")
        t = p.tick("scalar", lambda e: e.activation(out=L(11, 2), in_=L(11, 2),
                                                    func=AF.Exp))
        p.wait("vector", t)
        vt(lambda e: e.tensor_tensor(out=L(13), in0=L(11), in1=L(12), op=OP.add))
        vt(lambda e: e.reciprocal(out=L(14), in_=L(13)))
        vt(lambda e: e.tensor_tensor(out=L(11, 2), in0=L(11, 2), in1=Lb(14, 2),
                                     op=OP.mult))
        vt(lambda e: e.tensor_tensor(out=L(15), in0=L(11), in1=L(12), op=OP.max))
        vt(lambda e: e.tensor_tensor(out=L(10), in0=L(12), in1=L(11), op=OP.is_gt))
        vt(lambda e: e.tensor_scalar(out=L(16), in0=L(10), scalar1=float(NEGBIG),
                                     scalar2=None, op0=OP.mult))
        vt(lambda e: e.tensor_scalar(out=L(17), in0=L(10), scalar1=-1.0,
                                     scalar2=1.0, op0=OP.mult, op1=OP.add))
        vt(lambda e: e.tensor_scalar(out=L(17), in0=L(17), scalar1=float(NEGBIG),
                                     scalar2=None, op0=OP.mult))
        vt(lambda e: e.tensor_tensor(out=L(2, 4), in0=L(2, 4), in1=Lb(16, 4),
                                     op=OP.add))
        vt(lambda e: e.tensor_tensor(out=L(6, 4), in0=L(6, 4), in1=Lb(17, 4),
                                     op=OP.add))
        vt(lambda e: e.reduce_max(out=L(10), in_=L(2, 8), axis=AX.X))
        tv = p.tick("vector", lambda e: e.tensor_tensor(
            out=L(2, 8), in0=L(2, 8), in1=Lb(10, 8), op=OP.subtract))
        p.wait("scalar", tv)
        p.drain("scalar")
        t = p.tick("scalar", lambda e: e.activation(out=L(2, 8), in_=L(2, 8),
                                                    func=AF.Exp))
        p.wait("vector", t)
        vt(lambda e: e.reduce_sum(out=L(10), in_=L(2, 8), axis=AX.X))
        vt(lambda e: e.reciprocal(out=L(11), in_=L(10)))
        vt(lambda e: e.tensor_tensor(out=L(2, 8), in0=L(2, 8), in1=Lb(11, 8),
                                     op=OP.mult))
        vt(lambda e: e.tensor_tensor(out=L(2, 8), in0=L(2, 8), in1=Lb(15, 8),
                                     op=OP.mult))
        vt(lambda e: e.reduce_max(out=L(10), in_=L(2, 8), axis=AX.X))
        vt(lambda e: e.tensor_tensor(out=L(16, 8), in0=L(2, 8), in1=Lb(10, 8),
                                     op=OP.is_equal))
        vt(lambda e: e.tensor_scalar(out=L(16, 8), in0=L(16, 8), scalar1=-1.0,
                                     scalar2=1.0, op0=OP.mult, op1=OP.add))
        vt(lambda e: e.tensor_tensor(out=L(16, 8), in0=L(16, 8), in1=L(2, 8),
                                     op=OP.mult))
        vt(lambda e: e.reduce_max(out=L(12), in_=L(16, 8), axis=AX.X))
        vt(lambda e: e.tensor_tensor(out=L(16, 8), in0=L(2, 8), in1=Lb(12, 8),
                                     op=OP.is_ge))
        p.tick("vector", lambda e, q=q: e.tensor_tensor(
            out=bass.AP(af, WGT + q * TPQ * 8, [[AF32W, 128], [8, NTI], [1, 8]]),
            in0=L(16, 8), in1=L(2, 8), op=OP.mult))
        p.drain("vector")

    def produce_hidden(nm, in_ap, q, fmh, fml, firstq, fm_guard=None,
                       psum_guard=None, ring_guard=None):
        NW = QTOK // 512 if QTOK >= 512 else 1
        W5 = min(512, QTOK)

        def slot_a(j):
            return BST + (j % 4) * 2048

        def slot_b(j):
            return BSH2 + (j % 4) * 2048

        tok_load = {}

        def load_j(j):
            ta_ = load_slice(nm, in_ap, j, slot_a(j), firstq, "col",
                             stream=f"wsa{j % 4}")
            tb_ = load_slice(nm, in_ap, 16 + j, slot_b(j), firstq, "col",
                             stream=f"wsb{j % 4}")
            tok_load[j] = (ta_, tb_)

        mm_tok = {}
        prev_split = {}
        if ring_guard is not None:
            p.wait("sync", ring_guard)
        load_j(0)
        load_j(1)
        load_j(2)
        load_j(3)
        if psum_guard is not None:
            p.wait("tensor", psum_guard)
        if fm_guard:
            p.wait_max("tensor", fm_guard)
        for j in range(16):
            p.wait_max("tensor", list(tok_load[j]))
            for w in range(NW):
                if (j + w) % 2 in prev_split:
                    p.wait("tensor", prev_split[(j + w) % 2])
                banka = ps[(j + w) % 2]
                bankb = ps[2 + (j + w) % 2]
                for kc in range(8):
                    mm3(banka[:, 0:W5],
                        bap(slot_a(j) + kc * 128, [[1, 128]]),
                        bap(slot_a(j) + 1024 + kc * 128, [[1, 128]]),
                        fmh(kc, w * W5, W5), fml(kc, w * W5, W5),
                        start=(kc == 0), stop=(kc == 7))
                tmm = None
                for kc in range(8):
                    tmm = mm3(bankb[:, 0:W5],
                        bap(slot_b(j) + kc * 128, [[1, 128]]),
                        bap(slot_b(j) + 1024 + kc * 128, [[1, 128]]),
                        fmh(kc, w * W5, W5), fml(kc, w * W5, W5),
                        start=(kc == 0), stop=(kc == 7), inc=(kc == 7))
                if w == NW - 1:
                    mm_tok[j] = tmm
                    nxt = j + 4
                    if nxt < 16:
                        p.wait("sync", tmm)
                        if firstq:
                            p.wait("vector", tmm)
                        load_j(nxt)
                par = (j + w) % 2
                tbo = TB + par * W5 if W5 <= 512 else TB
                if par in prev_split:
                    p.wait("scalar", prev_split[par])
                p.wait("scalar", tmm)
                tsi = p.tick("scalar", lambda e, ba=banka, o=tbo: e.activation(
                    out=fap(o, [[1, W5]]), in_=ba[:, 0:W5], func=AF.Silu))
                p.wait("vector", tsi)
                p.tick("vector", lambda e, bb=bankb, o=tbo: e.tensor_tensor(
                    out=fap(o, [[1, W5]]), in0=fap(o, [[1, W5]]),
                    in1=bb[:, 0:W5], op=OP.mult))
                p.drain("vector")
                tv = split_pair("vector",
                                bap(BSH1 + j * QTOK + w * W5, [[1, W5]]),
                                bap(BSH1 + 8192 + j * QTOK + w * W5, [[1, W5]]),
                                fap(tbo, [[1, W5]]))
                prev_split[par] = tv
        return mm_tok[15], dict(prev_split)
    def wso_ring(firstq):
        tok_load = {}

        def load_j(j):
            tok_load[j] = load_slice("wso", wso_in, j,
                                     BSH2 + 8192 + (j % 4) * 2048,
                                     firstq, "row", stream=f"wso{j % 4}")

        return tok_load, load_j

    def consume_hidden_shared(q, firstq, ring):
        tok_load, load_j = ring
        tmm = None
        for j in range(16):
            p.wait("tensor", tok_load[j])
            for ti in range(TPQ):
                for half in range(2):
                    bank = ps[ti * 2 + half]
                    last = (ti == TPQ - 1 and half == 1)
                    tmm = mm3(bank[:, 0:512],
                        bap(BSH1 + j * QTOK + ti * 128, [[1, 128]]),
                        bap(BSH1 + 8192 + j * QTOK + ti * 128, [[1, 128]]),
                        bap(BSH2 + 8192 + (j % 4) * 2048 + half * 512,
                            [[1, 512]]),
                        bap(BSH2 + 8192 + (j % 4) * 2048 + 1024 + half * 512,
                            [[1, 512]]),
                        start=(j == 0), stop=(j == 15), inc=last)
            nxt = j + 4
            if nxt < 16:
                p.wait("sync", tmm)
                if firstq:
                    p.wait("vector", tmm)
                load_j(nxt)
        p.wait("vector", tmm)
        p.drain("vector")
        tv = None
        for ti in range(TPQ):
            for half in range(2):
                bank = ps[ti * 2 + half]
                tv = p.tick("vector", lambda e, b=bank, ti=ti, hf=half:
                            e.tensor_copy(out=accq(ti, 512, hf * 512),
                                          in_=b[:, 0:512]))
        return tv

    def experts_quarter(q, firstq, pre_splits=None, ring_guard=None):
        tok_load = {}

        NSLOT = 4

        def load_L(L):
            e_i, jj = divmod(L, 16)
            tok_load[L] = load_slice("w2", None, (e_i, jj),
                                     BST + (L % NSLOT) * 2048, firstq, "w2e",
                                     stream=f"w2s{L % NSLOT}")

        total = NEXP * 16
        if ring_guard is not None:
            p.wait("sync", ring_guard)
        for _l in range(NSLOT):
            load_L(_l)
        if pre_splits is not None:
            for _t in pre_splits.values():
                p.wait("tensor", _t)
        prev_evac = {}
        last_evac = [None]
        for e_i in range(NEXP):
            for j in range(16):
                L = e_i * 16 + j
                slot = BST + (L % NSLOT) * 2048
                p.wait("tensor", tok_load[L])
                for ti in range(TPQ):
                    for half in range(2):
                        bank = ps[ti * 2 + half]
                        if j == 0 and (ti * 2 + half) in prev_evac:
                            p.wait("tensor", prev_evac[ti * 2 + half])
                        last = (ti == TPQ - 1 and half == 1)
                        tmm = mm3(bank[:, 0:512],
                            bap(BSH1 + j * QTOK + ti * 128, [[1, 128]]),
                            bap(BSH1 + 8192 + j * QTOK + ti * 128, [[1, 128]]),
                            bap(slot + half * 512, [[1, 512]]),
                            bap(slot + 1024 + half * 512, [[1, 512]]),
                            start=(j == 0), stop=(j == 15), inc=last)
                nxt = L + NSLOT
                if nxt < total:
                    p.wait("sync", tmm)
                    if firstq:
                        p.wait("vector", tmm)
                    load_L(nxt)
            p.wait("vector", tmm)
            for ti in range(TPQ):
                tt = q * TPQ + ti
                for half in range(2):
                    bank = ps[ti * 2 + half]
                    tv = p.tick("vector", lambda e, b=bank, ti=ti, hf=half, ei=e_i,
                                tt=tt: e.scalar_tensor_tensor(
                        out=accq(ti, 512, hf * 512), in0=b[:, 0:512],
                        scalar=fap(WGT + tt * 8 + ei, [[1, 1]]),
                        in1=accq(ti, 512, hf * 512), op0=OP.mult, op1=OP.add))
                    p.drain("vector")
                    prev_evac[ti * 2 + half] = tv
                    last_evac[0] = tv
        return last_evac[0]

    def attn_slot():
        qguard = {}
        for q in range(NQ):
            prev_ts = [None]
            tsv = {}
            XBS = (TA, TC, SS, SS + 1024)
            SQS = (TB, SS + 2080, SS + 3104, SS + 4224)
            CVS = (COLV, COLV + 8, COLV + 16, COLV + 24)

            def c_(i, a, b):
                return fap(CVS[i] + a, [[1, b - a]])

            def xv(i, n=D):
                return fap(XBS[i], [[1, n]])

            def sv(i, n=D):
                return fap(SQS[i], [[1, n]])

            ltok = {}
            for i in range(TPQ):
                tt = q * TPQ + i
                if qguard.get(i) is not None:
                    p.wait("sync", qguard[i])
                ltok[i] = p.dma("sync", xv(i), sp_x[ts(tt, 128), :],
                                stream=f"ldx{i}")
            p.drain("vector")
            p.drain("scalar")
            for i in range(TPQ):
                p.wait("vector", ltok[i])
                p.wait("scalar", ltok[i])
                p.tick("vector", lambda e, i=i: e.reduce_sum(
                    out=c_(i, 0, 1), in_=xv(i), axis=AX.X))
                p.tick("scalar", lambda e, i=i: e.activation(
                    out=sv(i), in_=xv(i), func=AF.Square, accum_out=c_(i, 1, 2)))
            tS = p.mark("scalar")
            p.wait("vector", tS)
            p.drain("vector")
            for i in range(TPQ):
                p.tick("vector", lambda e, i=i: e.tensor_scalar(
                    out=c_(i, 2, 3), in0=c_(i, 0, 1), scalar1=1.0 / D,
                    scalar2=None, op0=OP.mult))
            p.drain("vector")
            for i in range(TPQ):
                p.tick("vector", lambda e, i=i: e.tensor_tensor(
                    out=c_(i, 3, 4), in0=c_(i, 2, 3), in1=c_(i, 2, 3), op=OP.mult))
            p.drain("vector")
            tV = None
            for i in range(TPQ):
                tV = p.tick("vector", lambda e, i=i: e.tensor_scalar(
                    out=c_(i, 4, 5), in0=c_(i, 1, 2), scalar1=1.0 / D,
                    scalar2=c_(i, 3, 4), op0=OP.mult, op1=OP.subtract))
            p.wait("scalar", tV)
            p.drain("scalar")
            tS = None
            for i in range(TPQ):
                tS = p.tick("scalar", lambda e, i=i: e.activation(
                    out=c_(i, 5, 6), in_=c_(i, 4, 5), func=AF.Sqrt,
                    bias=float(EPS_LN)))
            p.wait("vector", tS)
            p.drain("vector")
            for i in range(TPQ):
                p.tick("vector", lambda e, i=i: e.reciprocal(
                    out=c_(i, 6, 7), in_=c_(i, 5, 6)))
            p.drain("vector")
            tVA = None
            for i in range(TPQ):
                tVA = p.tick("vector", lambda e, i=i: e.tensor_scalar(
                    out=xv(i), in0=xv(i), scalar1=c_(i, 2, 3), scalar2=c_(i, 6, 7),
                    op0=OP.subtract, op1=OP.mult))
            p.wait("scalar", tVA)
            p.drain("scalar")
            gtok = {}
            sgp = {}
            for i in range(TPQ):
                gtok[i] = p.tick("scalar", lambda e, i=i: e.activation(
                    out=xv(i), in_=xv(i), func=AF.Gelu_apprx_tanh))
            for i in range(TPQ):
                tt = q * TPQ + i
                p.wait("gpsimd", gtok[i])
                sgp[i] = p.dma("gpsimd", sp_g[ts(tt, 128), :], xv(i),
                               stream=f"spg{i}")
            p.drain("scalar")
            tS = None
            for i in range(TPQ):
                tS = p.tick("scalar", lambda e, i=i: e.activation(
                    out=sv(i), in_=xv(i), func=AF.Square, accum_out=c_(i, 1, 2)))
            p.drain("scalar")
            tS2 = None
            for i in range(TPQ):
                tS2 = p.tick("scalar", lambda e, i=i: e.activation(
                    out=c_(i, 5, 6), in_=c_(i, 1, 2), func=AF.Sqrt))
            p.wait("vector", tS2)
            p.drain("vector")
            for i in range(TPQ):
                p.tick("vector", lambda e, i=i: e.tensor_scalar(
                    out=c_(i, 5, 6), in0=c_(i, 5, 6), scalar1=1e-12, scalar2=None,
                    op0=OP.max))
            p.drain("vector")
            for i in range(TPQ):
                tt = q * TPQ + i
                p.tick("vector", lambda e, i=i, tt=tt: e.reciprocal(
                    out=fap(INVN + tt, [[1, 1]]), in_=c_(i, 5, 6)))
            p.drain("vector")
            for i in range(TPQ):
                p.tick("vector", lambda e, i=i: e.tensor_tensor(
                    out=sv(i), in0=xv(i), in1=ema_bc, op=OP.mult))
            p.drain("vector")
            for i in range(TPQ):
                p.tick("vector", lambda e, i=i: e.reduce_sum(
                    out=c_(i, 7, 8), in_=sv(i), axis=AX.X))
            p.drain("vector")
            for i in range(TPQ):
                tt = q * TPQ + i
                p.tick("vector", lambda e, i=i, tt=tt: e.tensor_tensor(
                    out=fap(EMAS + tt, [[1, 1]]), in0=c_(i, 7, 8),
                    in1=fap(INVN + tt, [[1, 1]]), op=OP.mult))
            p.drain("vector")
            for i in range(TPQ):
                tt = q * TPQ + i
                p.wait("vector", sgp[i])
                t5 = p.tick("vector", lambda e, i=i, tt=tt: e.tensor_scalar(
                    out=xv(i), in0=xv(i), scalar1=fap(INVN + tt, [[1, 1]]),
                    scalar2=None, op0=OP.mult))
                p.drain("vector")
                p.wait("tensor", t5)
                prev_ts[0] = transpose_split(
                    fap(XBS[i], [[1, 1024]]), 8,
                    lambda c, ti=i: (
                        bap(BFM + c * 2 * QTOK + ti * 128, [[1, 128]]),
                        bap(BFM + c * 2 * QTOK + QTOK + ti * 128, [[1, 128]])),
                    guard=prev_ts[0])
                tsv[i] = prev_ts[0]
                qguard[i] = prev_ts[0]
            p.wait("gpsimd", tsv[TPQ - 1])
            for c in range(8):
                p.dma("gpsimd", sp_fm["h"][c, :, ts(q, QTOK)],
                      bap(BFM + c * 2 * QTOK, [[1, QTOK]]))
                p.dma("gpsimd", sp_fm["l"][c, :, ts(q, QTOK)],
                      bap(BFM + c * 2 * QTOK + QTOK, [[1, QTOK]]))
            if q == 0:
                for tt in range(NT):
                    p.tick("vector", lambda e, tt=tt: e.memset(
                        fap(SEQS + tt, [[1, 1]]), -2.0))
                p.drain("vector")
                tvz = p.mark("vector")
                p.wait("tensor", tvz)
            last_blk = [None]
            rhs_tok = {}

            def load_rhs(sh):
                par = sh % 2
                base = BSH1 + par * 8192
                trr = []
                for c in range(8):
                    trr.append(p.dma("sync", bap(base + c * 1024, [[1, QW]]),
                                     sp_fm["h"][c, :, ts(sh, QW)],
                                     stream=f"simr{par}"))
                    trr.append(p.dma("sync", bap(base + c * 1024 + 512, [[1, QW]]),
                                     sp_fm["l"][c, :, ts(sh, QW)],
                                     stream=f"simr{par}"))
                rhs_tok[sh] = trr

            if q > 0:
                load_rhs(0)
            for sh in range(q + 1):
                diag = (sh == q)
                RHS = BSH1 + (sh % 2) * 8192
                if not diag:
                    p.wait_max("tensor", rhs_tok[sh])
                    if sh + 1 < q:
                        if last_blk[0] is not None:
                            p.wait("sync", last_blk[0])
                        load_rhs(sh + 1)
                for ti in range(TPQ):
                    tt = q * TPQ + ti
                    bank = ps[tt % 2]
                    tokmm = None
                    for kc in range(8):
                        if diag:
                            rh = bap(BFM + kc * 2 * QTOK, [[1, QTOK]])
                            rl = bap(BFM + kc * 2 * QTOK + QTOK, [[1, QTOK]])
                        else:
                            rh = bap(RHS + kc * 1024, [[1, QW]])
                            rl = bap(RHS + kc * 1024 + 512, [[1, QW]])
                        tokmm = mm3(bank[:, 0:QW],
                                    bap(BFM + kc * 2 * QTOK + ti * 128, [[1, 128]]),
                                    bap(BFM + kc * 2 * QTOK + QTOK + ti * 128,
                                        [[1, 128]]),
                                    rh, rl,
                                    start=(kc == 0), stop=(kc == 7), inc=(kc == 7))
                    tmm = tokmm
                    last_blk[0] = tmm
                    p.wait("vector", tmm)
                    off = tt * 128 - sh * QW
                    if off >= QW:
                        p.tick("vector", lambda e, b=bank: e.reduce_max(
                            out=colv(8, 9), in_=b[:, 0:QW], axis=AX.X))
                    else:
                        mi = off // 128
                        p.tick("vector", lambda e, b=bank, mi=mi: e.tensor_tensor(
                            out=fap(SS + 5248, [[1, QW]]), in0=b[:, 0:QW],
                            in1=mask_ap(mi), op=OP.add))
                        p.drain("vector")
                        p.tick("vector", lambda e: e.reduce_max(
                            out=colv(8, 9), in_=fap(SS + 5248, [[1, QW]]),
                            axis=AX.X))
                    p.drain("vector")
                    p.tick("vector", lambda e, tt=tt: e.tensor_tensor(
                        out=fap(SEQS + tt, [[1, 1]]), in0=fap(SEQS + tt, [[1, 1]]),
                        in1=colv(8, 9), op=OP.max))
                    p.drain("vector")
                    tvv = p.mark("vector")
                    p.wait("tensor", tvv)
            p.barrier(f"sim_q{q}")
        p.tick("vector", lambda e: e.tensor_copy(
            out=bass.AP(af, SEQS, [[AF32W, 1], [1, 1]]),
            in_=bass.AP(af, EMAS, [[AF32W, 1], [1, 1]])))
        p.drain("vector")
        p.barrier("sim_done")

        p.barrier("as_done")

    if stages >= 3:
        ffn_call(0, x_in, sp_x)
        attn_slot()
        ffn_call(1, sp_x, out_t, gated=True)
    elif stages == 2:
        ffn_call(0, x_in, sp_x)
        attn_slot()
        for tt in range(NT):
            tok = p.dma("sync", tA(D), sp_x[ts(tt, 128), :], stream="ldx")
            p.wait("gpsimd", tok)
            p.dma("gpsimd", out_t[ts(tt, 128), :], tA(D))
            p.barrier(f"st_{tt}")
    else:
        ffn_call(0, x_in, out_t)
    p.barrier("end")

    with nc.Block() as block:
        p.finalize(block)
    es.close()
    return nc


def _consts(T):
    NT = T // 128
    QW = min(512, T)
    iden = np.eye(128, dtype=np.float32)
    inv_freq = (1.0 / 10000.0 ** (np.arange(0, DH, 2, dtype=np.float32) / DH)
                ).astype(np.float32)
    t_idx = np.arange(T, dtype=np.float32)
    ang = (t_idx[:, None] * inv_freq[None, :]).astype(np.float32)
    sin, cos = np.sin(ang).astype(np.float32), np.cos(ang).astype(np.float32)
    trig = np.zeros((128, 2 * NT * 32), dtype=np.float32)
    for tt in range(NT):
        trig[:, tt * 64:tt * 64 + 32] = cos[tt * 128:(tt + 1) * 128]
        trig[:, tt * 64 + 32:tt * 64 + 64] = sin[tt * 128:(tt + 1) * 128]
    masks = np.zeros((128, 4 * QW), dtype=np.float32)
    for mi in range(4):
        off = mi * 128
        pp = np.arange(128)[:, None]
        cc = np.arange(QW)[None, :]
        masks[:, mi * QW:(mi + 1) * QW] = np.where(cc < off + pp, 0.0, -1e9)
    return iden, trig, masks


_NC_CACHE = {}


def _get_nc(T):
    if T not in _NC_CACHE:
        _NC_CACHE[T] = build_program(T)
    return _NC_CACHE[T]


def kernel(L0_attn_x, L0_attn_ema_out, L0_attn_logit_decay, L0_attn_log_tau,
           L0_attn_log_blend, L0_attn_logit_blend_seq, L0_ffn_Wq, L0_ffn_Wk_c,
           L0_ffn_Wv_c, L0_ffn_Wk, L0_ffn_Wv, L0_ffn_Wo, L0_ffn_W_shared_in,
           L0_ffn_W_shared_out, L0_ffn_W1_shared, L0_ffn_W2_experts,
           L0_ffn_Wg_gate, L0_ffn_We_gate, L0_ffn_group_bias,
           L0_ffn_expert_bias, L0_ffn_norm1_w, L0_ffn_norm2_w):
    T = L0_attn_x.shape[1]
    nc = _get_nc(T)
    iden, trig, masks = _consts(T)
    scal = np.zeros((1, 4), dtype=np.float32)
    scal[0, 0] = np.float32(L0_attn_log_tau)
    scal[0, 1] = np.float32(L0_attn_log_blend)
    scal[0, 2] = np.float32(L0_attn_logit_blend_seq)
    wgate = np.zeros((D, 16), dtype=np.float32)
    wgate[:, 0:2] = np.asarray(L0_ffn_Wg_gate, dtype=np.float32)
    wgate[:, 2:10] = np.asarray(L0_ffn_We_gate, dtype=np.float32)
    gbias = np.zeros((1, 16), dtype=np.float32)
    gbias[0, 0:2] = np.asarray(L0_ffn_group_bias, dtype=np.float32)
    gbias[0, 2:10] = np.asarray(L0_ffn_expert_bias, dtype=np.float32)
    base = {
        "ema_in": np.asarray(L0_attn_ema_out, dtype=np.float32).reshape(1, D),
        "scal_in": scal,
        "wq_in": np.ascontiguousarray(L0_ffn_Wq, dtype=np.float32),
        "wkc_in": np.ascontiguousarray(L0_ffn_Wk_c, dtype=np.float32),
        "wvc_in": np.ascontiguousarray(L0_ffn_Wv_c, dtype=np.float32),
        "wk_in": np.ascontiguousarray(L0_ffn_Wk, dtype=np.float32),
        "wv_in": np.ascontiguousarray(L0_ffn_Wv, dtype=np.float32),
        "wo_in": np.ascontiguousarray(L0_ffn_Wo, dtype=np.float32),
        "wsi_in": np.ascontiguousarray(L0_ffn_W_shared_in, dtype=np.float32),
        "wso_in": np.ascontiguousarray(L0_ffn_W_shared_out, dtype=np.float32),
        "w1_in": np.ascontiguousarray(L0_ffn_W1_shared, dtype=np.float32),
        "w2_in": np.ascontiguousarray(L0_ffn_W2_experts, dtype=np.float32),
        "wgate_in": wgate,
        "gbias_in": gbias,
        "n1w_in": np.asarray(L0_ffn_norm1_w, dtype=np.float32).reshape(1, D),
        "n2w_in": np.asarray(L0_ffn_norm2_w, dtype=np.float32).reshape(1, D),
        "iden_in": iden,
        "trig_in": trig,
        "mask_in": masks,
    }
    xs = np.ascontiguousarray(L0_attn_x, dtype=np.float32)
    in_maps = [dict(base, x_in=xs[b]) for b in range(xs.shape[0])]
    res = run_bass_kernel_spmd(nc, in_maps, core_ids=list(range(len(in_maps))))
    out = np.stack([r["out_t"] for r in res.results], axis=0).astype(np.float32)
    return out
